# version 03
# baseline (speedup 1.0000x reference)
import numpy as np
import ml_dtypes
from contextlib import ExitStack

import concourse.bass as bass
import concourse.mybir as mybir
from concourse.bass_utils import run_bass_kernel_spmd

F32 = mybir.dt.float32
BF16 = mybir.dt.bfloat16
AF = mybir.ActivationFunctionType
ALU = mybir.AluOpType
AX = mybir.AxisListType
BF = ml_dtypes.bfloat16

NCORE = 8
NB = 4
T = 2048
D = 1024
NCH = 8
NH = 16
HD = 64
NPAIR = 8
G = 512
NGB = T // G
NE = 32
NEX = 34
EI = 512
LNEPS = 1e-5

ENGINES = ("tensor", "vector", "scalar", "gpsimd", "sync")
DBG = False
P3S = 99
P3S2 = 99
P3IT = 6
P3IP = 99
P3W2 = 0


class Sem:
    __slots__ = ("h", "n")

    def __init__(self, h):
        self.h = h
        self.n = 0


class Prog:

    def __init__(self, nc, ctx):
        self.nc = nc
        self.ctx = ctx
        self.items = {e: [] for e in ENGINES}
        self._sems = {}

    def sem(self, name):
        s = self._sems.get(name)
        if s is None:
            s = Sem(self.ctx.enter_context(self.nc.semaphore(name)))
            self._sems[name] = s
        return s

    def emit(self, eng, fn, waits=(), inc=None, inck=1):
        ws = tuple((s, v) for (s, v) in waits if v > 0)
        self.items[eng].append((fn, ws, inc, inck))
        if inc is not None:
            inc.n += inck
        return inc.n if inc is not None else 0

    def flush(self):
        live = {e: items for e, items in self.items.items() if items}
        if not live:
            return
        with self.nc.Block() as block:
            for e, items in live.items():
                def body(eh, _items=tuple(items)):
                    for fn, waits, inc, inck in _items:
                        for (s, v) in waits:
                            eh.wait_ge(s.h, v)
                        instr = fn(eh)
                        if inc is not None:
                            instr.then_inc(inc.h, inck)
                getattr(block, e)(body)
        self.items = {e: [] for e in ENGINES}

    def mm(self, out, lhsT, rhs, start, stop, tile=None, waits=(), inc=None):
        return self.emit(
            "tensor",
            lambda eh, o=out, l=lhsT, r=rhs, s1=start, s2=stop, tp=tile:
            eh.matmul(o, l, r, start=s1, stop=s2, tile_position=tp),
            waits, inc)

    def transp(self, out, in_, ident, waits=(), inc=None):
        return self.emit(
            "tensor", lambda eh, o=out, i=in_, d=ident: eh.transpose(o, i, d),
            waits, inc)

    def act(self, out, in_, func, bias=0.0, scale=1.0, accum=None, waits=(), inc=None):
        return self.emit(
            "scalar",
            lambda eh, o=out, i=in_, f=func, b=bias, s=scale, a=accum:
            eh.activation(o, i, f, bias=b, scale=s, accum_out=a),
            waits, inc)

    def dma(self, eng, dst, src, waits=(), inc=None):
        return self.emit(eng, lambda eh, d=dst, s=src: eh.dma_start(d, s),
                         waits, inc, inck=16)

    def vop(self, eng, name, *args, waits=(), inc=None, **kw):
        return self.emit(
            eng, lambda eh, n=name, a=args, k=kw: getattr(eh, n)(*a, **k),
            waits, inc)


def ap_full(t):
    return t[tuple(slice(0, s) for s in t.shape)]



def prep_shared_inputs(inputs):
    d = {}
    W = np.asarray(inputs["L0_attn_W_qkv"])
    b = np.asarray(inputs["L0_attn_b_qkv"])
    qrows = np.array([192 * (2 * p + j // 64) + (j % 64)
                      for p in range(NPAIR) for j in range(128)])
    krows = qrows + 64
    vrows = np.array([192 * h + 128 + dd for h in range(NH) for dd in range(HD)])
    d["wq"] = np.ascontiguousarray(W[qrows].T)
    d["wk"] = np.ascontiguousarray(W[krows].T)
    d["wv"] = np.ascontiguousarray(W[vrows].T)
    d["bq"] = np.ascontiguousarray(b[qrows].reshape(NPAIR, 128).T)
    d["bk"] = np.ascontiguousarray(b[krows].reshape(NPAIR, 128).T)
    d["bv"] = np.ascontiguousarray(b[vrows].reshape(1, 1024))
    d["wo"] = np.ascontiguousarray(np.asarray(inputs["L0_attn_W_out"]).T)
    d["bo"] = np.ascontiguousarray(np.asarray(inputs["L0_attn_b_out"]).reshape(NCH, 128).T)

    d["gw"] = np.ascontiguousarray(np.asarray(inputs["L0_ffn_gate_w"]).T)
    d["gb"] = np.ascontiguousarray(np.asarray(inputs["L0_ffn_gate_bias"]).reshape(1, NE))
    wg = np.asarray(inputs["L0_ffn_w_gate"])
    wu = np.asarray(inputs["L0_ffn_w_up"])
    wd = np.asarray(inputs["L0_ffn_w_down"])
    shgT = np.asarray(inputs["L0_ffn_sh_gate"]).T
    shuT = np.asarray(inputs["L0_ffn_sh_up"]).T
    shdT = np.asarray(inputs["L0_ffn_sh_down"]).T
    wg3 = np.empty((NEX, 1024, 512), np.float32)
    wu3 = np.empty((NEX, 1024, 512), np.float32)
    wd3 = np.empty((NEX, 512, 1024), np.float32)
    wg3[:NE] = wg.transpose(0, 2, 1)
    wu3[:NE] = wu.transpose(0, 2, 1)
    wd3[:NE] = wd.transpose(0, 2, 1)
    wg3[NE], wg3[NE + 1] = shgT[:, 0:512], shgT[:, 512:1024]
    wu3[NE], wu3[NE + 1] = shuT[:, 0:512], shuT[:, 512:1024]
    wd3[NE], wd3[NE + 1] = shdT[0:512, :], shdT[512:1024, :]
    d["wg3"] = np.ascontiguousarray(wg3)
    d["wu3"] = np.ascontiguousarray(wu3)
    d["wd3"] = np.ascontiguousarray(wd3)

    d["w1qkv"] = np.ascontiguousarray(np.asarray(inputs["L1_attn_w_qkv"]))
    d["w1p"] = np.ascontiguousarray(np.asarray(inputs["L1_attn_w_proj"]))

    w1inT = np.zeros((1024, 2048), np.float32)
    w1inT[:, :2040] = np.asarray(inputs["L1_ffn_W_in"]).T
    b1p = np.zeros((1, 2048), np.float32)
    b1p[0, :2040] = np.asarray(inputs["L1_ffn_b_in"])
    w1outT = np.zeros((2048, 1024), np.float32)
    w1outT[:2040] = np.asarray(inputs["L1_ffn_W_out"]).T
    d["w1in"] = w1inT
    d["b1in"] = b1p
    d["w1out"] = w1outT

    d["c_or16"] = np.ones((1, 128), BF)
    d["c_or32"] = np.ones((1, 128), np.float32)
    d["c_oc16"] = np.ones((128, 1), BF)
    d["c_oc32"] = np.ones((128, 1), np.float32)
    d["c_id16"] = np.eye(128, dtype=BF)
    d["c_id32"] = np.eye(128, dtype=np.float32)
    hm = np.zeros((128, 2), np.float32)
    hm[0:64, 0] = 1.0
    hm[64:128, 1] = 1.0
    d["c_hm"] = hm
    sel = np.zeros((32, NE * 128), BF)
    for e in range(NE):
        sel[e, e * 128:(e + 1) * 128] = 1.0
    d["c_sel"] = sel
    return d


def prep_core_inputs(inputs, core, shared):
    d = dict(shared)
    x = np.asarray(inputs["L0_attn_x"])
    xs = x[core * NB:(core + 1) * NB]
    d["x"] = np.ascontiguousarray(xs.transpose(0, 2, 1))
    return d


def declare_inputs(nc):
    t = {}

    def di(name, shape, dtype=F32):
        t[name] = nc.dram_tensor(name, list(shape), dtype, kind="ExternalInput")
    di("x", (NB, D, T))
    di("wq", (D, D)); di("wk", (D, D)); di("wv", (D, D))
    di("bq", (128, NPAIR)); di("bk", (128, NPAIR)); di("bv", (1, D))
    di("wo", (D, D)); di("bo", (128, NCH))
    di("gw", (D, NE)); di("gb", (1, NE))
    di("wg3", (NEX, D, EI)); di("wu3", (NEX, D, EI)); di("wd3", (NEX, EI, D))
    di("w1qkv", (D, 3 * D)); di("w1p", (D, D))
    di("w1in", (D, 2048)); di("b1in", (1, 2048)); di("w1out", (2048, D))
    di("c_or16", (1, 128), BF16); di("c_or32", (1, 128))
    di("c_oc16", (128, 1), BF16); di("c_oc32", (128, 1))
    di("c_id16", (128, 128), BF16); di("c_id32", (128, 128))
    di("c_hm", (128, 2))
    di("c_sel", (32, NE * 128), BF16)
    return t



class Bufs:
    def __init__(self, nc, ctx, p, hb):
        sb = lambda n, s, dt: ctx.enter_context(nc.sbuf_tensor(n, s, dt))
        ps = lambda n, s, dt: ctx.enter_context(nc.psum_tensor(n, s, dt))
        self.x = [sb(f"x_{c}", [128, T], BF16) for c in range(NCH)]
        self.or16 = sb("k_or16", [1, 128], BF16)
        self.or32 = sb("k_or32", [1, 128], F32)
        self.oc16 = sb("k_oc16", [128, 1], BF16)
        self.oc32 = sb("k_oc32", [128, 1], F32)
        self.id16 = sb("k_id16", [128, 128], BF16)
        self.hm = sb("k_hm", [128, 2], F32)
        self.PB = [ps(f"psumB{i}", [128, 4 * G], F32) for i in range(2)]
        self.P = [self.PB[i // 4][:, (i % 4) * G:(i % 4 + 1) * G] for i in range(8)]
        self.hb = hb
        s = p.sem("ld_c")
        for nm, dst in (("c_or16", self.or16), ("c_or32", self.or32),
                        ("c_oc16", self.oc16), ("c_oc32", self.oc32),
                        ("c_id16", self.id16),
                        ("c_hm", self.hm)):
            p.dma("sync", ap_full(dst), ap_full(hb[nm]), inc=s)
        self.ld_c = s


def emit_x_load(p, bufs, hb, b, prev=()):
    s = p.sem("ld_x")
    st = p.sem("st_x")
    for c in range(NCH):
        w = list(prev) + [(st, st.n)] if c == 0 else ()
        p.dma("gpsimd", bufs.x[c][:, :], hb["x"][b, c * 128:(c + 1) * 128, :],
              waits=w, inc=s)
    return s


def emit_x_store(p, bufs, ydram, b, waits):
    s = p.sem("st_x")
    for c in range(NCH):
        p.dma("gpsimd", ydram[b, c * 128:(c + 1) * 128, :], bufs.x[c][:, :],
              waits=waits, inc=s)
    return s



class LNState:
    def __init__(self, nc, ctx, tag=""):
        sb = lambda n, s, dt: ctx.enter_context(nc.sbuf_tensor(n + tag, s, dt))
        self.xsq = [sb(f"ln_xsq{i}", [128, G], BF16) for i in range(2)]
        self.rA = sb("ln_rA", [1, G], F32)
        self.rB = sb("ln_rB", [1, G], F32)
        self.rC = sb("ln_rC", [1, G], F32)
        self.mb16 = sb("ln_mb16", [128, G], BF16)
        self.rb16 = sb("ln_rb16", [128, G], BF16)
        self.tsub = sb("ln_tsub", [128, G], BF16)


def emit_ln(p, bufs, ln, xtiles, t0, h16, hcol, pst, pst2, pbc, xw=()):
    s_sq = p.sem("ln_sq")
    s_st = p.sem("ln_st")
    s_row = p.sem("ln_row")
    s_bc = p.sem("ln_bc")
    s_ev = p.sem("ln_ev")
    s_ap = p.sem("ln_ap")

    ap0 = s_ap.n
    sq0 = s_sq.n
    st0 = s_st.n
    row0 = s_row.n
    ev0 = s_ev.n
    for c in range(NCH):
        xa = xtiles[c][:, t0:t0 + G]
        sqt = ln.xsq[c % 2]
        w_act = [(s_st, st0 + c - 1)] if c >= 2 else [(s_st, st0)]
        p.act(sqt[:, :], xa, AF.Square, waits=w_act, inc=s_sq)
        wmm = ([(s_row, row0)] + list(xw)) if c == 0 else ()
        p.mm(pst[0:1, :], bufs.oc16[:, :], xa, start=(c == 0), stop=(c == NCH - 1),
             waits=wmm)
        p.mm(pst2[64:65, :], bufs.oc16[:, :], sqt[:, :], start=(c == 0), stop=(c == NCH - 1),
             tile=(0, 64), waits=[(s_sq, sq0 + c + 1)], inc=s_st)
    st_done = s_st.n

    p.act(ln.rA[:, :], pst[0:1, :], AF.Copy, scale=1.0 / D,
          waits=[(s_st, st_done)], inc=s_row)
    p.act(ln.rB[:, :], pst2[64:65, :], AF.Copy, scale=1.0 / D, inc=s_row)
    p.act(ln.rC[:, :], ln.rA[:, :], AF.Square, waits=[(s_row, row0 + 1)], inc=s_row)
    p.emit("vector", lambda eh: eh.scalar_tensor_tensor(
        ln.rB[:, :], ln.rB[:, :], LNEPS, ln.rC[:, :], op0=ALU.add, op1=ALU.subtract),
        waits=[(s_row, s_row.n)], inc=s_row)
    p.act(ln.rB[:, :], ln.rB[:, :], AF.Sqrt, waits=[(s_row, s_row.n)], inc=s_row)
    p.emit("vector", lambda eh: eh.reciprocal(ln.rC[:, :], ln.rB[:, :]),
           waits=[(s_row, s_row.n)], inc=s_row)
    row_done = s_row.n

    p.mm(pbc[:, :], bufs.or32[:, :], ln.rA[:, :], start=True, stop=True,
         waits=[(s_row, row_done), (s_ev, ev0)], inc=s_bc)
    p.act(ln.mb16[:, :], pbc[:, :], AF.Copy, waits=[(s_bc, s_bc.n), (s_ap, ap0)], inc=s_ev)
    p.mm(pbc[:, :], bufs.or32[:, :], ln.rC[:, :], start=True, stop=True,
         waits=[(s_ev, s_ev.n)], inc=s_bc)
    p.act(ln.rb16[:, :], pbc[:, :], AF.Copy, waits=[(s_bc, s_bc.n)], inc=s_ev)
    ev_done = s_ev.n

    for c in range(NCH):
        xa = xtiles[c][:, t0:t0 + G]
        dst = h16[c][:, hcol:hcol + G]
        wv = [(s_ev, ev_done)] if c == 0 else [(s_ap, s_ap.n)]
        p.emit("vector", lambda eh, a=xa: eh.tensor_sub(ln.tsub[:, :], a, ln.mb16[:, :]),
               waits=wv, inc=s_ap)
        wp = [(s_ap, s_ap.n)]
        if c == 0:
            wp.append((s_ev, ev_done))
        p.emit("gpsimd", lambda eh, o=dst: eh.tensor_mul(o, ln.tsub[:, :], ln.rb16[:, :]),
               waits=wp, inc=s_ap)
    return s_ap.n



class P1State:
    def __init__(self, nc, ctx, tag=""):
        sb = lambda n, s, dt: ctx.enter_context(nc.sbuf_tensor(n + tag, s, dt))
        self.h16 = [sb(f"p1_h{c}", [128, T], BF16) for c in range(NCH)]
        self.logits = [sb(f"p1_lg{c}", [128, T // 2], BF16) for c in range(NCH)]
        self.vaug = [sb(f"p1_va{k}", [128, 2 * (HD + 1)], BF16) for k in range(16)]
        self.wv16 = sb("p1_wv", [128, 1024], BF16)
        self.wq16 = sb("p1_wq", [128, 1024], BF16)
        self.wk16 = sb("p1_wk", [128, 1024], BF16)
        self.wo16 = [sb("p1_wo0", [128, 1024], BF16)]
        self.bq = sb("p1_bq", [128, NPAIR], F32)
        self.bk = sb("p1_bk", [128, NPAIR], F32)
        self.bo = sb("p1_bo", [128, NCH], F32)
        self.bv16 = sb("p1_bv16", [1, 128], BF16)
        self.q = sb("p1_q", [128, T // 2], BF16)
        self.k = sb("p1_k", [128, T], BF16)
        self.nt = [sb(f"p1_nt{i}", [128, G], BF16) for i in range(4)]
        self.mqa = sb("p1_mqa", [128, 1], F32)
        self.mka = sb("p1_mka", [128, 1], F32)
        self.mprod = sb("p1_mprod", [128, 1], F32)
        self.cb = sb("p1_cb", [128, 2], F32)
        self.zinv = sb("p1_zinv", [1, 2 * G], F32)
        self.zb16 = sb("p1_zb16", [128, G], BF16)


def emit_l0_attn_batch(p, bufs, st, ln, hb, xld):
    x = bufs.x
    P = bufs.P
    s_wdma = p.sem("p1_wdma")
    s_bdma = p.sem("p1_bdma")
    s_pe = p.sem("p1_pe")
    s_qev = p.sem("p1_qev")
    s_exp = p.sem("p1_exp")
    s_av = p.sem("p1_av")
    s_dve = p.sem("p1_dve")
    s_lgt = p.sem("p1_lgt")
    s_res = p.sem("p1_res")
    s_ap = p.sem("ln_ap")

    if not hasattr(st, "tr"):
        p.dma("sync", st.bq[:, :], ap_full(hb["bq"]), inc=s_bdma)
        p.dma("sync", st.bk[:, :], ap_full(hb["bk"]), inc=s_bdma)
        p.dma("sync", st.bo[:, :], ap_full(hb["bo"]), inc=s_bdma)
        for kt in range(16):
            od = st.vaug[kt].rearrange("p (h u) -> p h u", u=HD + 1)[:, :, HD:HD + 1]
            p.emit("gpsimd", lambda eh, d2=od: eh.memset(d2, 1.0), inc=s_dve)
    bias_ld = s_bdma.n

    p.emit("tensor", lambda eh: eh.nop(), waits=[(xld, xld.n)])
    p.emit("scalar", lambda eh: eh.nop(), waits=[(xld, xld.n)])
    p.emit("vector", lambda eh: eh.nop(), waits=[(xld, xld.n)])
    for g in range(NGB):
        emit_ln(p, bufs, ln, x, g * G, st.h16, g * G, P[0], P[1], P[2])
    ln_done = s_ap.n

    if not hasattr(st, "tr"):
        st.tr = {"qk_pe": 0, "av_end": 0, "exp_end": 0, "wo_pe": [0, 0]}
    tr = st.tr

    for qh in range(2):
        for pr in range(NPAIR):
            wprev = [(s_pe, tr["qk_pe"])]
            p.dma("gpsimd", st.wq16[:, :].rearrange("p (c j) -> p c j", j=128),
                  hb["wq"][:, pr * 128:(pr + 1) * 128].rearrange("(c i) j -> i c j", i=128),
                  waits=wprev, inc=s_wdma)
            p.dma("gpsimd", st.wk16[:, :].rearrange("p (c j) -> p c j", j=128),
                  hb["wk"][:, pr * 128:(pr + 1) * 128].rearrange("(c i) j -> i c j", i=128),
                  inc=s_wdma)
            p.dma("gpsimd", st.wv16[:, :].rearrange("p (c j) -> p c j", j=128),
                  hb["wv"][:, pr * 128:(pr + 1) * 128].rearrange("(c i) j -> i c j", i=128),
                  inc=s_wdma)
            p.dma("gpsimd", st.bv16[:, :], hb["bv"][:, pr * 128:(pr + 1) * 128],
                  inc=s_wdma)
            w_ld = s_wdma.n

            prev_av_end = tr["av_end"]
            prev_exp_end = tr["exp_end"]
            for kt in range(16):
                pv = P[2 + kt % 2]
                for c in range(NCH):
                    wmm = []
                    if c == 0:
                        wmm.append((s_wdma, w_ld))
                        if kt < 2:
                            wmm.append((s_ap, ln_done))
                            wmm.append((s_exp, prev_exp_end))
                        else:
                            wmm.append((s_dve, s_dve.n - 1))
                    p.mm(pv[:, 0:128], st.h16[c][:, kt * 128:(kt + 1) * 128],
                         st.wv16[:, c * 128:(c + 1) * 128],
                         start=(c == 0), stop=False, waits=wmm)
                p.mm(pv[:, 0:128], bufs.or16[:, :],
                     st.bv16[:, :], start=False, stop=True, inc=s_pe)
                vdst = st.vaug[kt].rearrange("p (h u) -> p h u", u=HD + 1)[:, :, 0:HD]
                vsrc = pv[:, 0:128].rearrange("p (h u) -> p h u", u=HD)
                p.emit("vector", lambda eh, d2=vdst, s2=vsrc: eh.tensor_copy(d2, s2),
                       waits=[(s_pe, s_pe.n), (s_av, prev_av_end)], inc=s_dve)
            v_done = s_dve.n

            def qk_emit(wt, bt, dst, groups):
                for gi, g in enumerate(groups):
                    pq = P[gi % 2]
                    for c in range(NCH):
                        wmm = []
                        if c == 0:
                            wmm.append((s_wdma, w_ld))
                            wmm.append((s_qev, max(s_qev.n - 1, 0)))
                            wmm.append((s_exp, prev_exp_end))
                        p.mm(pq[:, :], wt[:, c * 128:(c + 1) * 128],
                             st.h16[c][:, g * G:(g + 1) * G],
                             start=(c == 0), stop=(c == NCH - 1), waits=wmm,
                             inc=s_pe if c == NCH - 1 else None)
                    p.act(dst[:, gi * G:(gi + 1) * G], pq[:, :], AF.Identity,
                          bias=bt[:, pr:pr + 1],
                          waits=[(s_pe, s_pe.n), (s_bdma, bias_ld)], inc=s_qev)
            qk_emit(st.wk16, st.bk, st.k, list(range(NGB)))
            qk_emit(st.wq16, st.bq, st.q, [2 * qh, 2 * qh + 1])
            tr["qk_pe"] = s_pe.n
            qk_done = s_qev.n

            p.emit("vector", lambda eh: eh.reduce_max(
                st.mqa[:, :], st.q[:, :], axis=AX.X, apply_absolute_value=True),
                waits=[(s_qev, qk_done)], inc=s_dve)
            p.emit("vector", lambda eh: eh.reduce_max(
                st.mka[:, :], st.k[:, :], axis=AX.X, apply_absolute_value=True), inc=s_dve)
            p.emit("vector", lambda eh: eh.tensor_mul(
                st.mprod[:, :], st.mqa[:, :], st.mka[:, :]),
                waits=[(s_dve, s_dve.n)], inc=s_dve)
            mp_done = s_dve.n
            p.mm(P[6][0:1, 0:2], st.mprod[:, :], bufs.hm[:, :], start=True, stop=True,
                 waits=[(s_dve, mp_done), (s_exp, s_exp.n)], inc=s_pe)
            p.act(st.cb[0:1, 0:2], P[6][0:1, 0:2], AF.Copy, scale=-0.125,
                  waits=[(s_pe, s_pe.n)], inc=s_qev)
            for h in range(2):
                p.mm(P[6][:, 4 + h:5 + h], bufs.or32[:, :], st.cb[0:1, h:h + 1],
                     start=True, stop=True, waits=[(s_qev, s_qev.n)],
                     inc=s_pe if h == 1 else None)
            p.emit("vector", lambda eh: eh.tensor_copy(st.cb[:, 0:2], P[6][:, 4:6]),
                   waits=[(s_pe, s_pe.n)], inc=s_dve)
            cb_done = s_dve.n

            exp_hist = []
            av_hist = []
            for gi in range(2):
                qcols = slice(gi * G, (gi + 1) * G)
                for kc in range(16):
                    j = gi * 16 + kc
                    spar = kc % 2
                    psA, psB = P[0 + spar], P[2 + spar]
                    kc1 = slice(kc * 128, kc * 128 + 64)
                    kc2 = slice(kc * 128 + 64, (kc + 1) * 128)
                    wmm = []
                    if j == 0:
                        wmm.append((s_qev, qk_done))
                        wmm.append((s_dve, v_done))
                    if j >= 2:
                        wmm.append((s_exp, exp_hist[j - 2]))
                    p.mm(psA[0:64, :], st.k[0:64, kc1], st.q[0:64, qcols],
                         start=True, stop=True, tile=(0, 0), waits=wmm)
                    p.mm(psA[64:128, :], st.k[0:64, kc2], st.q[0:64, qcols],
                         start=True, stop=True, tile=(0, 64))
                    p.mm(psB[0:64, :], st.k[64:128, kc1], st.q[64:128, qcols],
                         start=True, stop=True, tile=(64, 0))
                    p.mm(psB[64:128, :], st.k[64:128, kc2], st.q[64:128, qcols],
                         start=True, stop=True, tile=(64, 64), inc=s_pe)
                    sc_done = s_pe.n
                    for h, pst_ in ((0, psA), (1, psB)):
                        nt = st.nt[2 * h + spar]
                        wact = [(s_pe, sc_done), (s_dve, cb_done)]
                        if j >= 2:
                            wact.append((s_av, av_hist[j - 2]))
                        p.act(nt[:, :], pst_[:, :], AF.Exp, bias=st.cb[:, h:h + 1],
                              scale=0.125, waits=wact, inc=s_exp)
                    exp_hist.append(s_exp.n)
                    first = (kc == 0)
                    va = st.vaug[kc].rearrange("p (h u) -> p h u", u=HD + 1)
                    wav = [(s_exp, exp_hist[-1])]
                    if first:
                        wav.append((s_lgt, s_lgt.n))
                        wav.append((s_dve, s_dve.n))
                    last = (kc == 15)
                    p.mm(P[4][0:64, :], va[:, 0, 0:HD], st.nt[0 + spar][:, :],
                         start=first, stop=last, tile=(0, 0), waits=wav)
                    p.mm(P[7][64:128, :], va[:, 1, 0:HD], st.nt[2 + spar][:, :],
                         start=first, stop=last, tile=(0, 64))
                    p.mm(P[5][0:1, :], bufs.oc16[:, :], st.nt[0 + spar][:, :],
                         start=first, stop=last, tile=(0, 0))
                    p.mm(P[6][0:1, :], bufs.oc16[:, :], st.nt[2 + spar][:, :],
                         start=first, stop=last, tile=(0, 0), inc=s_av)
                    av_hist.append(s_av.n)
                av_done = s_av.n
                tr["av_end"] = av_done
                p.emit("vector", lambda eh: eh.reciprocal(st.zinv[0:1, 0:G], P[5][0:1, :]),
                       waits=[(s_av, av_done)], inc=s_dve)
                p.emit("vector", lambda eh: eh.reciprocal(st.zinv[0:1, G:2 * G], P[6][0:1, :]),
                       inc=s_dve)
                zi_done = s_dve.n
                p.mm(P[6][0:64, :], bufs.or32[:, 0:64], st.zinv[0:1, 0:G],
                     start=True, stop=True, tile=(0, 0),
                     waits=[(s_dve, zi_done), (s_exp, s_exp.n)])
                p.mm(P[6][64:128, :], bufs.or32[:, 0:64], st.zinv[0:1, G:2 * G],
                     start=True, stop=True, tile=(0, 64), inc=s_pe)
                p.act(st.zb16[:, :], P[6][:, :], AF.Copy, waits=[(s_pe, s_pe.n)], inc=s_exp)
                tr["exp_end"] = s_exp.n
                lt = st.logits[pr]
                lcols = slice(gi * G, (gi + 1) * G)
                p.emit("vector", lambda eh, lt=lt, lc=lcols: eh.tensor_mul(
                    lt[0:64, lc], P[4][0:64, :], st.zb16[0:64, :]),
                    waits=[(s_exp, s_exp.n)], inc=s_lgt)
                p.emit("vector", lambda eh, lt=lt, lc=lcols: eh.tensor_mul(
                    lt[64:128, lc], P[7][64:128, :], st.zb16[64:128, :]),
                    inc=s_lgt)
        lg_done = s_lgt.n

        s_wo = p.sem("p1_wo")
        for oc in range(NCH):
            wbuf = st.wo16[0]
            p.dma("gpsimd", wbuf[:, :].rearrange("p (c j) -> p c j", j=128),
                  hb["wo"][:, oc * 128:(oc + 1) * 128].rearrange("(c i) j -> i c j", i=128),
                  waits=[(s_pe, tr["wo_pe"][0])], inc=s_wo)
            wo_ld = s_wo.n
            for gi in range(2):
                po = P[0 + gi % 2]
                for pc in range(NCH):
                    wmm = []
                    if pc == 0:
                        wmm.append((s_lgt, lg_done))
                        wmm.append((s_wo, wo_ld))
                        wmm.append((s_res, max(s_res.n - 1, 0)))
                        if oc == 0 and gi == 0:
                            wmm.append((s_exp, tr["exp_end"]))
                    p.mm(po[:, :], wbuf[:, pc * 128:(pc + 1) * 128],
                         st.logits[pc][:, gi * G:(gi + 1) * G],
                         start=(pc == 0), stop=(pc == NCH - 1), waits=wmm,
                         inc=s_pe if pc == NCH - 1 else None)
                tr["wo_pe"][0] = s_pe.n
                xa = x[oc][:, (2 * qh + gi) * G:(2 * qh + gi + 1) * G]
                p.emit("vector", lambda eh, po=po, xa=xa, oc=oc: eh.scalar_tensor_tensor(
                    xa, po[:, :], st.bo[:, oc:oc + 1], xa, op0=ALU.add, op1=ALU.add),
                    waits=[(s_pe, s_pe.n), (s_bdma, bias_ld)], inc=s_res)
        tr["qk_pe"] = s_pe.n
    return [(s_res, s_res.n), (s_wdma, s_wdma.n), (p.sem("p1_wo"), p.sem("p1_wo").n),
            (s_bdma, s_bdma.n)]



def emit_fences(p, engines, waits):
    for e in engines:
        p.emit(e, lambda eh: eh.nop(), waits=waits)


class P2State:
    def __init__(self, nc, ctx, tag=""):
        sb = lambda n, s, dt: ctx.enter_context(nc.sbuf_tensor(n + tag, s, dt))
        self.h16 = [sb(f"p2_h{c}", [128, G], BF16) for c in range(NCH)]
        self.gw16 = sb("p2_gw16", [128, NCH * NE], BF16)
        self.gb32 = sb("p2_gb32", [1, NE], F32)
        self.gbb = sb("p2_gbb", [128, NE], F32)
        self.csel = sb("p2_csel", [32, NE * 128], BF16)
        self.scores = [sb(f"p2_sc{t}", [128, NE], F32) for t in range(4)]
        self.scb = [sb(f"p2_scb{t}", [128, NE], F32) for t in range(4)]
        self.gs = [sb(f"p2_gs{t}", [128, 8], F32) for t in range(4)]
        self.gtmp = [sb(f"p2_gt{t}", [128, 8], F32) for t in range(4)]
        self.top8 = [sb(f"p2_t8{t}", [128, 8], F32) for t in range(4)]
        self.masked = [sb(f"p2_mk{t}", [128, NE], F32) for t in range(4)]
        self.sel = [sb(f"p2_sl{t}", [128, NE], F32) for t in range(4)]
        self.w8 = [sb(f"p2_w8{t}", [128, NE], F32) for t in range(4)]
        self.sumw = [sb(f"p2_sw{t}", [128, 1], F32) for t in range(4)]
        self.c2 = [sb(f"p2_c2{t}", [128, 1], F32) for t in range(4)]
        self.comb = [sb(f"p2_cb{t}", [128, NE], BF16) for t in range(4)]
        self.combT = sb("p2_combT", [32, G], BF16)
        self.wg16 = [[sb(f"p2_wg{i}_{c}", [128, EI], BF16) for c in range(NCH)] for i in range(2)]
        self.wu16 = [[sb(f"p2_wu{i}_{c}", [128, EI], BF16) for c in range(NCH)] for i in range(2)]
        self.wd16 = [[sb(f"p2_wd{i}_{c}", [128, D], BF16) for c in range(4)] for i in range(2)]
        self.silu = [sb(f"p2_si{i}", [128, G], F32) for i in range(2)]
        self.gu = [[sb(f"p2_gu{i}_{c}", [128, G], BF16) for c in range(4)] for i in range(2)]
        self.accum = [sb(f"p2_ac{c}", [128, G], F32) for c in range(4)]
        self.c_sb = [sb(f"p2_cs{i}", [128, G], F32) for i in range(2)]


def emit_moe_casts(p, hb, scr):
    s = p.sem("p2_cast")
    for e in range(NEX):
        p.dma("gpsimd", scr["wg3s"][e], hb["wg3"][e], inc=s)
        p.dma("gpsimd", scr["wu3s"][e], hb["wu3"][e], inc=s)
        p.dma("gpsimd", scr["wd3s"][e], hb["wd3"][e], inc=s)
        p.emit("gpsimd", lambda eh: eh.nop(), waits=[(s, s.n)])
    return s


def emit_moe_batch(p, bufs, st, ln, hb, scr, prev_done):
    x = bufs.x
    P = bufs.P
    s_cast = p.sem("p2_cast")
    s_wst = p.sem("p2_wst")
    s_gl = p.sem("p2_gl")
    s_sig = p.sem("p2_sig")
    s_rt = p.sem("p2_rt")
    s_pg = p.sem("p2_pg")
    s_pu = p.sem("p2_pu")
    s_si = p.sem("p2_si")
    s_gu = p.sem("p2_gu")
    s_cm = p.sem("p2_cm")
    s_dn = p.sem("p2_dn")
    s_ev = p.sem("p2_ev")
    s_ap = p.sem("ln_ap")

    emit_fences(p, ("tensor", "scalar", "vector", "gpsimd", "sync"), prev_done)

    first_call = not hasattr(st, "tr2")
    if first_call:
        st.tr2 = {"cast": 0}
        p.dma("gpsimd", st.gw16[:, :].rearrange("p (c e) -> p c e", e=NE),
              hb["gw"].rearrange("(c p) e -> p c e", p=128), inc=p.sem("p2_gwd"))
        p.dma("sync", st.gb32[:, :], ap_full(hb["gb"]), inc=p.sem("p2_gbd"))
        p.dma("sync", ap_full(st.csel), ap_full(hb["c_sel"]), inc=p.sem("p2_gbd"))
    s_gwd = p.sem("p2_gwd")
    s_gbd = p.sem("p2_gbd")

    if first_call:
        p.mm(P[3][:, 0:NE], bufs.or32[:, :], st.gb32[:, :], start=True, stop=True,
             waits=[(s_gbd, s_gbd.n)], inc=s_gl)
        p.act(st.gbb[:, :], P[3][:, 0:NE], AF.Copy, waits=[(s_gl, s_gl.n)], inc=s_sig)

    for grp in range(NGB):
        t0 = grp * G
        emit_ln(p, bufs, ln, x, t0, st.h16, 0, P[0], P[1], P[2], xw=[(s_ev, s_ev.n)])
        ln_done = s_ap.n

        glw = [(s_gwd, s_gwd.n), (s_ap, ln_done), (s_sig, s_sig.n), (s_rt, s_rt.n), (s_ev, s_ev.n)]
        for tt in range(4):
            for c in range(NCH):
                p.mm(P[3][:, tt * NE:(tt + 1) * NE],
                     st.h16[c][:, tt * 128:(tt + 1) * 128],
                     st.gw16[:, c * NE:(c + 1) * NE],
                     start=(c == 0), stop=(c == NCH - 1),
                     waits=glw if (tt == 0 and c == 0) else (),
                     inc=s_gl if c == NCH - 1 else None)
        gl_done = s_gl.n
        p.act(st.scores[0][:, :], P[3][:, 0:NE], AF.Sigmoid,
              waits=[(s_gl, gl_done), (s_rt, s_rt.n)], inc=s_sig)
        for tt in range(1, 4):
            p.act(st.scores[tt][:, :], P[3][:, tt * NE:(tt + 1) * NE], AF.Sigmoid, inc=s_sig)
        sig_done = s_sig.n
        for tt in range(4):
            p.emit("vector", lambda eh, t=tt: eh.tensor_add(
                st.scb[t][:, :], st.scores[t][:, :], st.gbb[:, :]),
                waits=[(s_sig, sig_done), (s_rt, s_rt.n)], inc=s_rt)
        pairs = [(0, 1), (0, 2), (0, 3), (1, 2), (1, 3), (2, 3)]
        for tt in range(4):
            v = st.scb[tt][:, :].rearrange("p (g i) -> p g i", i=4)
            for pi, (i, j) in enumerate(pairs):
                dst = st.gs[tt] if pi == 0 else st.gtmp[tt]
                p.emit("vector", lambda eh, d=dst, a=v[:, :, i], b2=v[:, :, j]:
                       eh.tensor_add(d[:, :], a, b2),
                       waits=[(s_rt, s_rt.n)], inc=s_rt)
                if pi > 0:
                    p.emit("vector", lambda eh, t=tt: eh.tensor_max(
                        st.gs[t][:, :], st.gs[t][:, :], st.gtmp[t][:, :]),
                        waits=[(s_rt, s_rt.n)], inc=s_rt)
        for tt in range(4):
            p.emit("vector", lambda eh, t=tt: eh.max(out=st.top8[t][:, :], in_=st.gs[t][:, :]),
                   waits=[(s_rt, s_rt.n)], inc=s_rt)
        for tt in range(4):
            p.emit("vector", lambda eh, t=tt: eh.tensor_scalar(
                st.gtmp[t][:, :], st.gs[t][:, :], st.top8[t][:, 3:4], None, op0=ALU.is_ge),
                waits=[(s_rt, s_rt.n)], inc=s_rt)
        for tt in range(4):
            gb = st.gtmp[tt][:, :, None].broadcast_to([128, 8, 4])
            p.emit("vector", lambda eh, t=tt, g=gb: eh.tensor_mul(
                st.masked[t][:, :].rearrange("p (g i) -> p g i", i=4),
                st.scb[t][:, :].rearrange("p (g i) -> p g i", i=4), g),
                waits=[(s_rt, s_rt.n)], inc=s_rt)
        for tt in range(4):
            p.emit("vector", lambda eh, t=tt: eh.max(out=st.top8[t][:, :], in_=st.masked[t][:, :]),
                   waits=[(s_rt, s_rt.n)], inc=s_rt)
        for tt in range(4):
            p.emit("vector", lambda eh, t=tt: eh.tensor_scalar(
                st.sel[t][:, :], st.masked[t][:, :], st.top8[t][:, 7:8], None, op0=ALU.is_ge),
                waits=[(s_rt, s_rt.n)], inc=s_rt)
        for tt in range(4):
            p.emit("vector", lambda eh, t=tt: eh.scalar_tensor_tensor(
                st.w8[t][:, :], st.scores[t][:, :], 1.0, st.sel[t][:, :],
                op0=ALU.mult, op1=ALU.mult, accum_out=st.sumw[t][:, :]),
                waits=[(s_rt, s_rt.n)], inc=s_rt)
        for tt in range(4):
            p.emit("vector", lambda eh, t=tt: eh.reciprocal(st.c2[t][:, :], st.sumw[t][:, :]),
                   waits=[(s_rt, s_rt.n)], inc=s_rt)
        for tt in range(4):
            p.emit("vector", lambda eh, t=tt: eh.tensor_scalar_mul(
                st.c2[t][:, :], st.c2[t][:, :], 2.5), waits=[(s_rt, s_rt.n)], inc=s_rt)
        for tt in range(4):
            p.emit("vector", lambda eh, t=tt: eh.tensor_scalar_mul(
                st.comb[t][:, :], st.w8[t][:, :], st.c2[t][:, :]),
                waits=[(s_rt, s_rt.n)], inc=s_rt)
        rt_done = s_rt.n
        if DBG and grp == 0 and first_call:
            sdb = p.sem("p2_dbg")
            for c in range(NCH):
                t = p.nc.dram_tensor(f"dbg_h{c}", [128, G], BF16, kind="ExternalOutput")
                p.dma("sync", ap_full(t), st.h16[c][:, :], waits=[(s_ap, ln_done)], inc=sdb)
            for t4 in range(4):
                t = p.nc.dram_tensor(f"dbg_cb{t4}", [128, NE], BF16, kind="ExternalOutput")
                p.dma("sync", ap_full(t), st.comb[t4][:, :], waits=[(s_rt, rt_done)], inc=sdb)
                t = p.nc.dram_tensor(f"dbg_sc{t4}", [128, NE], F32, kind="ExternalOutput")
                p.dma("sync", ap_full(t), st.scores[t4][:, :], waits=[(s_rt, rt_done)], inc=sdb)
        p3b = P[3][:, :].bitcast(BF16)
        for tt in range(4):
            p.transp(p3b[0:32, tt * 128:(tt + 1) * 128], st.comb[tt][:, :], bufs.id16[:, :],
                     waits=[(s_rt, rt_done), (s_sig, s_sig.n)] if tt == 0 else (),
                     inc=s_gl if tt == 3 else None)
        p.emit("vector", lambda eh: eh.tensor_copy(st.combT[:, :], p3b[0:32, 0:G]),
               waits=[(s_gl, s_gl.n)], inc=s_rt)
        combT_done = s_rt.n

        lo_evac_prev = s_ev.n
        n_i = 0
        for e in range(NEX):
            par = e % 2
            routed = e < NE
            if e < 2:
                ecast = 48 * (e + 1)
                for c in range(NCH):
                    p.dma("sync", st.wg16[e % 2][c][:, :], scr["wg3s"][e, c * 128:(c + 1) * 128, :],
                          waits=[(s_cast, ecast), (s_dn, st.tr2.get("dn_prev", 0))] if c == 0 else [], inc=s_wst)
                for c in range(NCH):
                    p.dma("sync", st.wu16[e % 2][c][:, :], scr["wu3s"][e, c * 128:(c + 1) * 128, :], inc=s_wst)
                for c in range(4):
                    p.dma("sync", st.wd16[e % 2][c][:, :], scr["wd3s"][e, c * 128:(c + 1) * 128, :], inc=s_wst)
                p.emit("sync", lambda eh: eh.nop(), waits=[(s_wst, s_wst.n)])
                st.tr2[f"wldgu{e}"] = s_wst.n
                st.tr2[f"wldd{e}"] = s_wst.n
            wld_e = max(st.tr2[f"wldgu{e}"], st.tr2[f"wldd{e}"])

            if routed:
                pc_ = P[6 + (e % 2)]
                cw = [(s_rt, combT_done), (s_ev, st.tr2.get(f"hiev{e % 2}", 0)),
                      (s_gbd, s_gbd.n)]
                p.mm(pc_[:, :], st.csel[:, e * 128:(e + 1) * 128], st.combT[:, :],
                     start=True, stop=True, waits=cw, inc=s_gl)
                p.act(st.c_sb[par][:, :], pc_[:, :], AF.Copy,
                      waits=[(s_gl, s_gl.n), (s_cm, st.tr2.get(f"cmu{par}", 0))], inc=s_sig)
                st.tr2[f"csb{par}"] = s_sig.n

            for i in range(4):
                n_i += 1
                for c in range(NCH):
                    wmm = []
                    if c == 0:
                        wmm.append((s_wst, wld_e))
                        if i == 0 and e == 0:
                            wmm.append((s_ap, ln_done))
                        wmm.append((s_si, s_si.n))
                    p.mm(P[4][:, :], st.wg16[par][c][:, i * 128:(i + 1) * 128],
                         st.h16[c][:, :], start=(c == 0), stop=(c == NCH - 1),
                         waits=wmm, inc=s_pg if c == NCH - 1 else None)
                for c in range(NCH):
                    wmm = []
                    if c == 0:
                        wmm.append((s_gu, s_gu.n))
                    p.mm(P[5][:, :], st.wu16[par][c][:, i * 128:(i + 1) * 128],
                         st.h16[c][:, :], start=(c == 0), stop=(c == NCH - 1),
                         waits=wmm, inc=s_pu if c == NCH - 1 else None)
                sq = st.silu[n_i % 2]
                p.act(sq[:, :], P[4][:, :], AF.Silu,
                      waits=[(s_pg, s_pg.n), (s_gu, max(s_gu.n - 1, 0))], inc=s_si)
                gub = st.gu[par][i]
                p.emit("vector", lambda eh, g=gub, sq=sq: eh.tensor_mul(
                    g[:, :], sq[:, :], P[5][:, :]),
                    waits=[(s_pu, s_pu.n), (s_si, s_si.n),
                           (s_dn, st.tr2.get("dn_prev2", 0))], inc=s_gu)
                if routed:
                    p.emit("gpsimd", lambda eh, g=gub, cc=st.c_sb[par]: eh.tensor_mul(
                        g[:, :], g[:, :], cc[:, :]),
                        waits=[(s_gu, s_gu.n), (s_sig, st.tr2[f"csb{par}"])], inc=s_cm)
            st.tr2[f"cmu{par}"] = s_cm.n
            gu_ready = (s_cm, s_cm.n) if routed else (s_gu, s_gu.n)

            for hc in range(4):
                for i in range(4):
                    wmm = []
                    if hc == 0 and i == 0:
                        wmm.append(gu_ready)
                        if e == 0:
                            wmm.append((s_ev, lo_evac_prev))
                    p.mm(P[hc][:, :], st.wd16[par][i][:, hc * 128:(hc + 1) * 128],
                         st.gu[par][i][:, :], start=(e == 0 and i == 0),
                         stop=(e == NEX - 1 and i == 3), waits=wmm,
                         inc=s_dn if (e == NEX - 1 and hc == 3 and i == 3) else None)
            if e % 2 == 1:
                for hc in range(4, 8):
                    bank = P[6 + (hc % 2)]
                    wmm = []
                    if hc in (4, 5):
                        wmm.append((s_sig, s_sig.n))
                    else:
                        wmm.append((s_ev, s_ev.n))
                    for ee in (e - 1, e):
                        pp = ee % 2
                        gur = (s_cm, st.tr2[f"cmu{pp}"]) if ee < NE else (s_gu, s_gu.n)
                        for i in range(4):
                            w2 = list(wmm) if (ee == e - 1 and i == 0) else []
                            if ee == e - 1 and i == 0:
                                w2.append(gur)
                            p.mm(bank[:, :], st.wd16[pp][i][:, hc * 128:(hc + 1) * 128],
                                 st.gu[pp][i][:, :], start=(ee == e - 1 and i == 0),
                                 stop=(ee == e and i == 3), waits=w2,
                                 inc=s_dn if (ee == e and i == 3) else None)
                    dn_now = s_dn.n
                    acix = hc - 4
                    if e == 1:
                        p.emit("vector", lambda eh, a=st.accum[acix], xc=x[4 + acix], t=t0,
                               b2=bank: eh.tensor_add(a[:, :], xc[:, t:t + G], b2[:, :]),
                               waits=[(s_dn, dn_now)], inc=s_ev)
                    else:
                        p.emit("vector", lambda eh, a=st.accum[acix], b2=bank: eh.tensor_add(
                            a[:, :], a[:, :], b2[:, :]),
                            waits=[(s_dn, dn_now)], inc=s_ev)
                    st.tr2[f"hiev{hc % 2}"] = s_ev.n
                st.tr2["dn_prev2"] = s_dn.n
            en = e + 2
            if en < NEX:
                ecast = 48 * (en + 1)
                p.dma("sync", st.wg16[par][0][:, :], scr["wg3s"][en, 0:128, :],
                      waits=[(s_cast, ecast), (s_pu, s_pu.n)], inc=s_wst)
                for c in range(1, NCH):
                    p.dma("sync", st.wg16[par][c][:, :], scr["wg3s"][en, c * 128:(c + 1) * 128, :], inc=s_wst)
                for c in range(NCH):
                    p.dma("sync", st.wu16[par][c][:, :], scr["wu3s"][en, c * 128:(c + 1) * 128, :], inc=s_wst)
                p.emit("sync", lambda eh: eh.nop(), waits=[(s_wst, s_wst.n)])
                st.tr2[f"wldgu{en}"] = s_wst.n
            if e % 2 == 1:
                for en2 in (e + 1, e + 2):
                    if en2 >= NEX:
                        continue
                    p.dma("sync", st.wd16[en2 % 2][0][:, :], scr["wd3s"][en2, 0:128, :],
                          waits=[(s_cast, 48 * (en2 + 1)), (s_dn, s_dn.n)], inc=s_wst)
                    for c in range(1, 4):
                        p.dma("sync", st.wd16[en2 % 2][c][:, :],
                              scr["wd3s"][en2, c * 128:(c + 1) * 128, :], inc=s_wst)
                p.emit("sync", lambda eh: eh.nop(), waits=[(s_wst, s_wst.n)])
                if e + 1 < NEX:
                    st.tr2[f"wldd{e + 1}"] = s_wst.n
                if e + 2 < NEX:
                    st.tr2[f"wldd{e + 2}"] = s_wst.n
        st.tr2["dn_prev"] = s_dn.n

        lo_done = s_dn.n
        for c in range(4):
            xc = x[c]
            p.emit("vector", lambda eh, xc=xc, t=t0, b2=P[c]: eh.tensor_add(
                xc[:, t:t + G], xc[:, t:t + G], b2[:, :]),
                waits=[(s_dn, lo_done)] if c == 0 else (), inc=s_ev)
        for c in range(4):
            xc = x[4 + c]
            p.emit("vector", lambda eh, xc=xc, t=t0, a=st.accum[c]: eh.tensor_copy(
                xc[:, t:t + G], a[:, :]),
                waits=[(s_ev, s_ev.n)] if c == 0 else (), inc=s_ev)
        st.tr2["si0"] = s_si.n
    return [(s_ev, s_ev.n), (s_wst, s_wst.n), (s_gwd, s_gwd.n), (s_gbd, s_gbd.n)]



class P3State:
    def __init__(self, nc, ctx, tag=""):
        sb = lambda n, s, dt: ctx.enter_context(nc.sbuf_tensor(n + tag, s, dt))
        self.h16 = [sb(f"p3_h{c}", [128, T], BF16) for c in range(NCH)]
        self.logits = [sb(f"p3_lg{c}", [128, T], BF16) for c in range(NCH)]
        self.vaug = [sb(f"p3_va{k}", [128, 2 * (HD + 1)], BF16) for k in range(16)]
        self.wq16 = sb("p3_wq", [128, 1024], BF16)
        self.wk16 = sb("p3_wk", [128, 1024], BF16)
        self.wv16 = sb("p3_wv", [128, 1024], BF16)
        self.wp16 = sb("p3_wp", [128, 1024], BF16)
        self.q = sb("p3_q", [128, T], BF16)
        self.k = sb("p3_k", [128, T], BF16)
        self.qlm = sb("p3_qlm", [128, HD], BF16)
        self.klm = sb("p3_klm", [128, HD], BF16)
        self.lmf = sb("p3_lmf", [128, HD], F32)
        self.mqa = sb("p3_mqa", [128, 1], F32)
        self.mka = sb("p3_mka", [128, 1], F32)
        self.mprod = sb("p3_mprod", [128, 1], F32)
        self.cb1 = sb("p3_cb1", [128, 2], F32)
        self.cb3 = sb("p3_cb3", [128, 2], F32)
        self.nt3 = [sb(f"p3_nt3{h}", [128, 1024], BF16) for h in range(2)]
        self.nt1 = [sb(f"p3_nt1{h}", [64, G], BF16) for h in range(2)]
        self.k3vn = [sb(f"p3_k3vn{h}", [64, HD], BF16) for h in range(2)]
        self.z3i = [sb(f"p3_z3i{h}", [64, 1], F32) for h in range(2)]
        self.K2 = [sb(f"p3_K2{h}", [64, 64], BF16) for h in range(2)]
        self.K2T = [sb(f"p3_K2T{h}", [64, 64], BF16) for h in range(2)]
        self.dcol = sb("p3_dcol", [64, 2], F32)
        self.V = [sb(f"p3_V{h}", [64, 64], BF16) for h in range(2)]
        self.VT = [sb(f"p3_VT{h}", [64, 64], BF16) for h in range(2)]
        self.Aa = sb("p3_Aa", [64, 64], BF16)
        self.Cc = sb("p3_Cc", [64, 64], BF16)
        self.Ee = sb("p3_Ee", [64, 64], BF16)
        self.KVT = sb("p3_KVT", [64, 64], BF16)
        self.i325 = sb("p3_i325", [64, 64], F32)
        self.id32 = sb("p3_id32", [128, 128], F32)
        self.w2 = sb("p3_w2", [64, 128], BF16)
        self.zrow = sb("p3_zrow", [1, 2 * G], F32)
        self.zb16 = sb("p3_zb16", [128, G], BF16)


def emit_l1_attn_batch(p, bufs, st, ln, hb, prev_done):
    x = bufs.x
    P = bufs.P
    s_wdma = p.sem("p3_wdma")
    s_pe = p.sem("p3_pe")
    s_qev = p.sem("p3_qev")
    s_exp = p.sem("p3_exp")
    s_dve = p.sem("p3_dve")
    s_lgt = p.sem("p3_lgt")
    s_res = p.sem("p3_res")
    s_ap = p.sem("ln_ap")

    emit_fences(p, ("tensor", "scalar", "vector", "gpsimd", "sync"), prev_done)
    p.dma("sync", ap_full(st.id32), ap_full(hb["c_id32"]), inc=p.sem("p3_idd"))
    s_idd = p.sem("p3_idd")
    p.emit("vector", lambda eh: eh.tensor_scalar_mul(
        st.i325[:, :], st.id32[0:64, 0:64], 3.25),
        waits=[(s_idd, s_idd.n)], inc=s_dve)
    for kt in range(16):
        od = st.vaug[kt].rearrange("p (h u) -> p h u", u=HD + 1)[:, :, HD:HD + 1]
        p.emit("gpsimd", lambda eh, d2=od: eh.memset(d2, 1.0), inc=s_dve)

    for g in range(NGB):
        emit_ln(p, bufs, ln, x, g * G, st.h16, g * G, P[0], P[1], P[2])
    ln_done = s_ap.n

    tr = {"qk_pe": 0, "exp_end": 0, "k3_pe": 0, "lgt": 0, "k3vn_dve": 0}
    for pr in range(NPAIR):
        p.dma("gpsimd", st.wq16[:, :].rearrange("p (c j) -> p c j", j=128),
              hb["w1qkv"][:, pr * 128:(pr + 1) * 128].rearrange("(c i) j -> i c j", i=128),
              waits=[(s_pe, tr["qk_pe"])], inc=s_wdma)
        p.dma("gpsimd", st.wk16[:, :].rearrange("p (c j) -> p c j", j=128),
              hb["w1qkv"][:, 1024 + pr * 128:1024 + (pr + 1) * 128].rearrange("(c i) j -> i c j", i=128),
              inc=s_wdma)
        p.dma("gpsimd", st.wv16[:, :].rearrange("p (c j) -> p c j", j=128),
              hb["w1qkv"][:, 2048 + pr * 128:2048 + (pr + 1) * 128].rearrange("(c i) j -> i c j", i=128),
              inc=s_wdma)
        w_ld = s_wdma.n

        prev_exp_end = tr["exp_end"]
        for kt in range(16):
            pv = P[2 + kt % 2]
            for c in range(NCH):
                wmm = []
                if c == 0:
                    wmm.append((s_wdma, w_ld))
                    if kt < 2:
                        wmm.append((s_ap, ln_done))
                        wmm.append((s_exp, prev_exp_end))
                        wmm.append((s_dve, tr["k3vn_dve"]))
                    else:
                        wmm.append((s_dve, s_dve.n - 1))
                p.mm(pv[:, 0:128], st.h16[c][:, kt * 128:(kt + 1) * 128],
                     st.wv16[:, c * 128:(c + 1) * 128],
                     start=(c == 0), stop=(c == NCH - 1), waits=wmm,
                     inc=s_pe if c == NCH - 1 else None)
            vdst = st.vaug[kt].rearrange("p (h u) -> p h u", u=HD + 1)[:, :, 0:HD]
            vsrc = pv[:, 0:128].rearrange("p (h u) -> p h u", u=HD)
            p.emit("vector", lambda eh, d2=vdst, s2=vsrc: eh.tensor_copy(d2, s2),
                   waits=[(s_pe, s_pe.n), (s_pe, tr["k3_pe"])], inc=s_dve)
        v_done = s_dve.n

        for which, wt, dst, scale in (("k", st.wk16, st.k, 1.0), ("q", st.wq16, st.q, 0.125)):
            for g in range(NGB):
                pq = P[g % 2]
                for c in range(NCH):
                    wmm = []
                    if c == 0:
                        wmm.append((s_wdma, w_ld))
                        wmm.append((s_qev, max(s_qev.n - 1, 0)))
                        wmm.append((s_exp, prev_exp_end))
                    p.mm(pq[:, :], wt[:, c * 128:(c + 1) * 128],
                         st.h16[c][:, g * G:(g + 1) * G],
                         start=(c == 0), stop=(c == NCH - 1), waits=wmm,
                         inc=s_pe if c == NCH - 1 else None)
                p.act(dst[:, g * G:(g + 1) * G], pq[:, :], AF.Copy, scale=scale,
                      waits=[(s_pe, s_pe.n)], inc=s_qev)
        tr["qk_pe"] = s_pe.n
        qk_done = s_qev.n

        p.emit("vector", lambda eh: eh.reduce_sum(
            st.lmf[:, :], st.q[:, :].rearrange("p (l w) -> p l w", w=32), axis=AX.X),
            waits=[(s_qev, qk_done), (s_dve, s_dve.n)], inc=s_dve)
        p.act(st.qlm[:, :], st.lmf[:, :], AF.Copy, scale=1.0 / 32,
              waits=[(s_dve, s_dve.n)], inc=s_qev)
        p.emit("vector", lambda eh: eh.reduce_sum(
            st.lmf[:, :], st.k[:, :].rearrange("p (l w) -> p l w", w=32), axis=AX.X),
            waits=[(s_qev, s_qev.n), (s_dve, s_dve.n)], inc=s_dve)
        p.act(st.klm[:, :], st.lmf[:, :], AF.Copy, scale=1.0 / 32,
              waits=[(s_dve, s_dve.n)], inc=s_qev)
        lm_done = s_qev.n

        def bound(src_a, src_b, dstcb):
            p.emit("vector", lambda eh: eh.reduce_max(
                st.mqa[:, :], src_a, axis=AX.X, apply_absolute_value=True),
                waits=[(s_qev, s_qev.n), (s_dve, s_dve.n)], inc=s_dve)
            p.emit("vector", lambda eh: eh.reduce_max(
                st.mka[:, :], src_b, axis=AX.X, apply_absolute_value=True),
                waits=[(s_dve, s_dve.n)], inc=s_dve)
            p.emit("vector", lambda eh: eh.tensor_mul(
                st.mprod[:, :], st.mqa[:, :], st.mka[:, :]),
                waits=[(s_dve, s_dve.n)], inc=s_dve)
            p.mm(P[4][0:1, 0:2], st.mprod[:, :], bufs.hm[:, :], start=True, stop=True,
                 waits=[(s_dve, s_dve.n), (s_qev, s_qev.n), (s_exp, s_exp.n)], inc=s_pe)
            p.act(dstcb[0:1, 0:2], P[4][0:1, 0:2], AF.Copy, scale=-1.0,
                  waits=[(s_pe, s_pe.n)], inc=s_qev)
            for h in range(2):
                p.mm(P[4][:, 4 + h:5 + h], bufs.or32[:, :], dstcb[0:1, h:h + 1],
                     start=True, stop=True, waits=[(s_qev, s_qev.n)],
                     inc=s_pe if h == 1 else None)
            p.emit("vector", lambda eh, d=dstcb: eh.tensor_copy(d[:, 0:2], P[4][:, 4:6]),
                   waits=[(s_pe, s_pe.n)], inc=s_dve)
        bound(st.q[:, :], st.klm[:, :], st.cb1)
        bound(st.qlm[:, :], st.k[:, :], st.cb3)
        cb_done = s_dve.n

        if P3S < 1:
            tr["exp_end"] = s_exp.n
            tr["k3vn_dve"] = s_dve.n
            continue
        for h in range(2):
            hr = slice(64 * h, 64 * h + 64)
            tp = (64 * h, 0)
            for half in range(2):
                pb = P[half]
                for j in range(8):
                    kc = half * 8 + j
                    wmm = []
                    if h == 0 and half == 0 and j == 0:
                        wmm.append((s_qev, lm_done))
                        wmm.append((s_dve, cb_done))
                    if h == 1 and half == 0 and j == 0:
                        wmm.append((s_exp, s_exp.n))
                    p.mm(pb[:, j * 64:(j + 1) * 64], st.k[hr, kc * 128:(kc + 1) * 128],
                         st.qlm[hr, :], start=True, stop=True, tile=tp, waits=wmm,
                         inc=s_pe if j == 7 else None)
                p.act(st.nt3[h][:, half * 512:(half + 1) * 512], pb[:, :], AF.Exp,
                      bias=st.cb3[:, h:h + 1], waits=[(s_pe, s_pe.n), (s_dve, cb_done)],
                      inc=s_exp)
        tr["exp_end"] = s_exp.n
        nt3_done = s_exp.n

        for h in range(2):
            pk = P[2 + h]
            for kc in range(16):
                wmm = []
                if kc == 0:
                    wmm.append((s_exp, nt3_done))
                    wmm.append((s_dve, v_done))
                p.mm(pk[0:64, 0:65],
                     st.nt3[h][:, kc * 64:(kc + 1) * 64],
                     st.vaug[kc][:, h * 65:(h + 1) * 65],
                     start=(kc == 0), stop=(kc == 15), waits=wmm,
                     inc=s_pe if kc == 15 else None)
        k3_done = s_pe.n
        p.emit("vector", lambda eh: eh.reciprocal(st.z3i[0][:, :], P[2][0:64, 64:65]),
               waits=[(s_pe, k3_done)], inc=s_dve)
        p.emit("vector", lambda eh: eh.reciprocal(st.z3i[1][:, :], P[3][0:64, 64:65]),
               inc=s_dve)
        p.emit("vector", lambda eh: eh.tensor_scalar_mul(
            st.k3vn[0][:, :], P[2][0:64, 0:64], st.z3i[0][:, 0:1]),
            waits=[(s_dve, s_dve.n)], inc=s_dve)
        p.emit("vector", lambda eh: eh.tensor_scalar_mul(
            st.k3vn[1][:, :], P[3][0:64, 0:64], st.z3i[1][:, 0:1]),
            inc=s_dve)
        k3vn_done = s_dve.n
        tr["k3_pe"] = k3_done
        tr["k3vn_dve"] = k3vn_done

        if P3S < 2:
            continue
        for h in range(2):
            hr = slice(64 * h, 64 * h + 64)
            tp = (64 * h, 0)
            p.mm(P[4][0:64, 0:64], st.qlm[hr, :], st.klm[hr, :], start=True, stop=True,
                 tile=tp, waits=[(s_dve, s_dve.n), (s_qev, s_qev.n)], inc=s_pe)
            sc2 = s_pe.n
            p.emit("vector", lambda eh: eh.reduce_max(
                st.mqa[0:64, :], P[4][0:64, 0:64], axis=AX.X, negate=True),
                waits=[(s_pe, sc2)], inc=s_dve)
            p.act(st.K2[h][:, :], P[4][0:64, 0:64], AF.Exp, bias=st.mqa[0:64, 0:1],
                  accum=st.mka[0:64, 0:1], waits=[(s_dve, s_dve.n)], inc=s_qev)
            p.emit("vector", lambda eh, h=h: eh.reciprocal(st.mprod[0:64, :], st.mka[0:64, :]),
                   waits=[(s_qev, s_qev.n)], inc=s_dve)
            p.emit("vector", lambda eh: eh.tensor_copy(st.lmf[0:64, 0:1], st.mka[0:64, 0:1]),
                   inc=s_dve)
            p.emit("vector", lambda eh, h=h: eh.tensor_scalar_mul(
                st.K2[h][:, :], st.K2[h][:, :], st.mprod[0:64, 0:1]),
                waits=[(s_dve, s_dve.n)], inc=s_dve)
            k2_done = s_dve.n
            if P3S2 < 1:
                continue
            p.mm(P[4][0:64, 64:128], st.K2[h][:, :], bufs.id16[0:64, 0:64],
                 start=True, stop=True, waits=[(s_dve, k2_done)], inc=s_pe)
            p.emit("vector", lambda eh, h=h: eh.tensor_copy(st.K2T[h][:, :], P[4][0:64, 64:128]),
                   waits=[(s_pe, s_pe.n)], inc=s_dve)
            if P3S2 < 2:
                continue
            p.mm(P[4][0:1, 128:192], bufs.oc16[0:64, :], st.K2[h][:, :], start=True, stop=True,
                 waits=[(s_dve, s_dve.n)], inc=s_pe)
            p.emit("vector", lambda eh: eh.reduce_max(
                st.mqa[0:1, :], P[4][0:1, 128:192], axis=AX.X),
                waits=[(s_pe, s_pe.n)], inc=s_dve)
            p.emit("vector", lambda eh: eh.reciprocal(st.mka[0:1, :], st.mqa[0:1, :]),
                   waits=[(s_dve, s_dve.n)], inc=s_dve)
            p.mm(P[4][0:64, 192:193], bufs.or32[:, 0:64], st.mka[0:1, 0:1],
                 start=True, stop=True, waits=[(s_dve, s_dve.n)], inc=s_pe)
            p.emit("vector", lambda eh, h=h: eh.tensor_copy(st.dcol[:, h:h + 1], P[4][0:64, 192:193]),
                   waits=[(s_pe, s_pe.n)], inc=s_dve)
            p.emit("vector", lambda eh: eh.tensor_copy(st.lmf[0:64, 1:2], st.mka[0:64, 0:1]),
                   inc=s_dve)
            p.emit("vector", lambda eh, h=h: eh.tensor_scalar_mul(
                st.V[h][:, :], st.K2T[h][:, :], st.dcol[:, h:h + 1]),
                waits=[(s_dve, s_dve.n)], inc=s_dve)
            p.emit("vector", lambda eh, h=h: eh.tensor_scalar_mul(
                st.VT[h][:, :], st.K2[h][:, :], st.dcol[:, h:h + 1]),
                inc=s_dve)
            v0_done = s_dve.n
            if P3S2 < 3:
                continue
            KV = P[4][0:64, 256:320]
            KVT = P[4][0:64, 320:384]
            BB = P[4][0:64, 384:448]
            DD = P[4][0:64, 448:512]
            for it in range(P3IT):
                p.mm(KV, st.K2T[h][:, :], st.V[h][:, :], start=True, stop=True,
                     waits=[(s_dve, s_dve.n)], inc=s_pe)
                p.mm(KVT, st.V[h][:, :], st.K2T[h][:, :], start=True, stop=True, inc=s_pe)
                pe_kv = s_pe.n
                p.emit("vector", lambda eh: eh.tensor_copy(st.KVT[:, :], KVT),
                       waits=[(s_pe, pe_kv)], inc=s_dve)
                p.emit("vector", lambda eh: eh.scalar_tensor_tensor(
                    st.Aa[:, :], st.id32[0:64, 0:64], 7.0, KV, op0=ALU.mult, op1=ALU.subtract),
                    waits=[(s_dve, s_dve.n)], inc=s_dve)
                a_done = s_dve.n
                if P3IP < 1:
                    continue
                p.mm(BB, st.KVT[:, :], st.Aa[:, :], start=True, stop=True,
                     waits=[(s_dve, a_done)], inc=s_pe)
                p.emit("vector", lambda eh: eh.scalar_tensor_tensor(
                    st.Cc[:, :], st.id32[0:64, 0:64], 15.0, BB, op0=ALU.mult, op1=ALU.subtract),
                    waits=[(s_pe, s_pe.n)], inc=s_dve)
                c_done = s_dve.n
                if P3IP < 2:
                    continue
                p.mm(DD, st.KVT[:, :], st.Cc[:, :], start=True, stop=True,
                     waits=[(s_dve, c_done)], inc=s_pe)
                p.emit("vector", lambda eh: eh.scalar_tensor_tensor(
                    st.Ee[:, :], DD, -0.25, st.i325[:, :], op0=ALU.mult, op1=ALU.add),
                    waits=[(s_pe, s_pe.n)], inc=s_dve)
                e_done = s_dve.n
                if P3IP < 3:
                    continue
                p.mm(KV, st.VT[h][:, :], st.Ee[:, :], start=True, stop=True,
                     waits=[(s_dve, e_done)], inc=s_pe)
                p.mm(KVT, st.Ee[:, :], st.VT[h][:, :], start=True, stop=True, inc=s_pe)
                pe_v = s_pe.n
                p.emit("vector", lambda eh, h=h: eh.tensor_copy(st.V[h][:, :], KV),
                       waits=[(s_pe, pe_v)], inc=s_dve)
                p.emit("vector", lambda eh, h=h: eh.tensor_copy(st.VT[h][:, :], KVT),
                       inc=s_dve)
            w2rhs = st.k3vn[h][:, :] if P3W2 != 1 else bufs.id16[0:64, 0:64]
            w2lhs = st.VT[h][:, :] if P3W2 != 2 else st.K2[h][:, :]
            p.mm(P[4][0:64, 256:320], w2lhs, w2rhs,
                 start=True, stop=True, waits=[(s_dve, s_dve.n)], inc=s_pe)
            if P3W2 != 3:
                p.emit("vector", lambda eh, h=h: eh.tensor_copy(
                    st.w2[:, 64 * h:64 * h + 64], P[4][0:64, 256:320]),
                    waits=[(s_pe, s_pe.n)], inc=s_dve)
        w2_done = s_dve.n

        if P3S < 3:
            continue
        for g in range(NGB):
            gcols = slice(g * G, (g + 1) * G)
            for h in range(2):
                hr = slice(64 * h, 64 * h + 64)
                tp = (64 * h, 0)
                wmm = [(s_dve, w2_done)]
                if g > 0 or h > 0:
                    wmm.append((s_exp, s_exp.n))
                if g > 0 and h == 0:
                    wmm.append((s_lgt, s_lgt.n))
                    wmm.append((s_dve, s_dve.n))
                p.mm(P[5][0:64, :], st.klm[hr, :], st.q[hr, gcols], start=True, stop=True,
                     tile=tp, waits=wmm, inc=s_pe)
                p.act(st.nt1[h][:, :], P[5][0:64, :], AF.Exp, bias=st.cb1[0:64, h:h + 1],
                      waits=[(s_pe, s_pe.n)], inc=s_exp)
                zdst = P[6][0:1, :] if h == 0 else P[5][0:1, :]
                p.mm(zdst, bufs.oc16[0:64, :], st.nt1[h][:, :],
                     start=True, stop=True, tile=(0, 0),
                     waits=[(s_exp, s_exp.n), (s_qev, s_qev.n)], inc=None)
                p.mm(P[7][64 * h:64 * h + 64, :], st.w2[:, 64 * h:64 * h + 64],
                     st.nt1[h][:, :], start=True, stop=True, tile=(0, 64 * h),
                     inc=s_pe)
            sv_done = s_pe.n
            p.emit("vector", lambda eh: eh.reciprocal(st.zrow[0:1, 0:G], P[6][0:1, 0:G]),
                   waits=[(s_pe, sv_done)], inc=s_dve)
            p.emit("vector", lambda eh: eh.reciprocal(st.zrow[0:1, G:2 * G], P[5][0:1, 0:G]),
                   inc=s_dve)
            zi_done = s_dve.n
            p.mm(P[6][0:64, :], bufs.or32[:, 0:64], st.zrow[0:1, 0:G],
                 start=True, stop=True, tile=(0, 0), waits=[(s_dve, zi_done)])
            p.mm(P[6][64:128, :], bufs.or32[:, 0:64], st.zrow[0:1, G:2 * G],
                 start=True, stop=True, tile=(0, 64), inc=s_pe)
            p.act(st.zb16[:, :], P[6][:, :], AF.Copy, waits=[(s_pe, s_pe.n)], inc=s_qev)
            lt = st.logits[pr]
            p.emit("vector", lambda eh, lt=lt, lc=gcols: eh.tensor_mul(
                lt[0:64, lc], P[7][0:64, :], st.zb16[0:64, :]),
                waits=[(s_qev, s_qev.n)], inc=s_lgt)
            p.emit("vector", lambda eh, lt=lt, lc=gcols: eh.tensor_mul(
                lt[64:128, lc], P[7][64:128, :], st.zb16[64:128, :]),
                inc=s_lgt)
    lg_done = s_lgt.n

    s_wp = p.sem("p3_wp")
    if P3S < 4:
        return [(s_pe, s_pe.n), (s_dve, s_dve.n), (s_wdma, s_wdma.n), (s_idd, s_idd.n)]
    wp_pe = 0
    for oc in range(NCH):
        p.dma("gpsimd", st.wp16[:, :].rearrange("p (c j) -> p c j", j=128),
              hb["w1p"][:, oc * 128:(oc + 1) * 128].rearrange("(c i) j -> i c j", i=128),
              waits=[(s_pe, wp_pe)], inc=s_wp)
        wp_ld = s_wp.n
        for g in range(NGB):
            po = P[0 + g % 2]
            for pc in range(NCH):
                wmm = []
                if pc == 0:
                    wmm.append((s_lgt, lg_done))
                    wmm.append((s_wp, wp_ld))
                    wmm.append((s_res, max(s_res.n - 1, 0)))
                    if oc == 0 and g == 0:
                        wmm.append((s_exp, s_exp.n))
                p.mm(po[:, :], st.wp16[:, pc * 128:(pc + 1) * 128],
                     st.logits[pc][:, g * G:(g + 1) * G],
                     start=(pc == 0), stop=(pc == NCH - 1), waits=wmm,
                     inc=s_pe if pc == NCH - 1 else None)
            wp_pe = s_pe.n
            xa = x[oc][:, g * G:(g + 1) * G]
            p.emit("vector", lambda eh, po=po, xa=xa: eh.tensor_add(xa, xa, po[:, :]),
                   waits=[(s_pe, s_pe.n)], inc=s_res)
    return [(s_res, s_res.n), (s_wdma, s_wdma.n), (s_wp, s_wp.n), (s_idd, s_idd.n)]



class P4State:
    def __init__(self, nc, ctx, tag=""):
        sb = lambda n, s, dt: ctx.enter_context(nc.sbuf_tensor(n + tag, s, dt))
        self.h16 = [sb(f"p4_h{c}", [128, G], BF16) for c in range(NCH)]
        self.w1in16 = [sb(f"p4_wi{c}", [128, 2048], BF16) for c in range(NCH)]
        self.b1in16 = sb("p4_b1", [1, 2048], BF16)
        self.w1o16 = sb("p4_wo", [128, 2048], BF16)
        self.dec = sb("p4_dec", [128, 2048], BF16)
        self.z16 = sb("p4_z16", [128, 2048], BF16)
        self.vmask = sb("p4_vm", [128, 2048], BF16)
        self.mk16 = sb("p4_mk", [128, 2048], BF16)
        self.tscr = sb("p4_ts", [128, 512], BF16)
        self.mkT = sb("p4_mkT", [128, 16 * G], BF16)
        self.outst = [sb(f"p4_os{i}", [128, G], F32) for i in range(2)]


def emit_l1_ffn_batch(p, bufs, st, ln, hb, ydram, b, prev_done):
    x = bufs.x
    P = bufs.P
    PB0 = bufs.PB[0]
    PB1b = bufs.PB[1][:, :].bitcast(BF16)
    s_wdma = p.sem("p4_wdma")
    s_pe = p.sem("p4_pe")
    s_po = p.sem("p4_po")
    s_dve = p.sem("p4_dve")
    s_pool = p.sem("p4_pool")
    s_act = p.sem("p4_act")
    s_ev = p.sem("p4_ev")
    s_yst = p.sem("p4_yst")
    s_ap = p.sem("ln_ap")

    emit_fences(p, ("tensor", "scalar", "vector", "gpsimd", "sync"), prev_done)

    for c in range(NCH):
        p.dma("gpsimd", st.w1in16[c][:, :], hb["w1in"][c * 128:(c + 1) * 128, :], inc=s_wdma)
    p.dma("gpsimd", st.b1in16[:, :], ap_full(hb["b1in"]), inc=s_wdma)
    wi_ld = s_wdma.n

    vm3 = st.vmask[:, 0:2040].rearrange("p (a n) -> p a n", n=255)
    dc3 = st.dec[:, 0:2040].rearrange("p (a n) -> p a n", n=255)

    for tg in range(NGB):
        t0 = tg * G
        emit_ln(p, bufs, ln, x, t0, st.h16, 0, P[0], P[1], P[2],
                xw=[(s_dve, s_dve.n), (s_ev, s_ev.n)])
        ln_done = s_ap.n
        for tt in range(4):
            for sl in range(4):
                for c in range(NCH):
                    wmm = []
                    if sl == 0 and c == 0:
                        wmm.append((s_ap, ln_done))
                        wmm.append((s_wdma, wi_ld))
                        wmm.append((s_dve, s_dve.n))
                    p.mm(P[sl][:, :], st.h16[c][:, tt * 128:(tt + 1) * 128],
                         st.w1in16[c][:, sl * G:(sl + 1) * G],
                         start=(c == 0), stop=False, waits=wmm)
                p.mm(P[sl][:, :], bufs.or16[:, :], st.b1in16[:, sl * G:(sl + 1) * G],
                     start=False, stop=True, inc=s_pe if sl == 3 else None)
            mm_done = s_pe.n
            p.emit("vector", lambda eh: eh.tensor_scalar(
                st.dec[:, 0:2040], PB0[:, 0:2040], 0.0, None, op0=ALU.is_gt),
                waits=[(s_pe, mm_done), (s_pool, s_pool.n)], inc=s_dve)
            p.emit("vector", lambda eh: eh.tensor_copy(st.z16[:, :], PB0[:, :]),
                   waits=[(s_dve, s_dve.n), (s_pool, s_pool.n)], inc=s_dve)
            z_done = s_dve.n
            p.emit("gpsimd", lambda eh: eh.memset(st.vmask[:, :], 0.0),
                   waits=[(s_act, s_act.n), (s_pool, s_pool.n)], inc=s_pool)
            p.emit("gpsimd", lambda eh: eh.memset(vm3[:, :, 0:1], 1.0),
                   waits=[(s_pool, s_pool.n)], inc=s_pool)
            for d in range(7):
                a, bnd = 2 ** d - 1, 2 ** (d + 1) - 1
                cnd = 2 ** (d + 2) - 1
                lvl = vm3[:, :, a:bnd]
                dec_d = dc3[:, :, a:bnd]
                nxt = vm3[:, :, bnd:cnd].rearrange("p a (i two) -> p a i two", two=2)
                t0s = st.tscr[:, :].rearrange("p (a n) -> p a n", n=64)[:, :, 0:2 ** d]
                p.emit("gpsimd", lambda eh, t=t0s, l=lvl, dd=dec_d: eh.tensor_mul(t, l, dd),
                       waits=[(s_pool, s_pool.n), (s_dve, z_done)], inc=s_pool)
                p.emit("gpsimd", lambda eh, n=nxt, l=lvl, t=t0s: eh.tensor_sub(
                    n[:, :, :, 0], l, t), waits=[(s_pool, s_pool.n)], inc=s_pool)
                p.emit("gpsimd", lambda eh, n=nxt, t=t0s: eh.tensor_copy(n[:, :, :, 1], t),
                       waits=[(s_pool, s_pool.n)], inc=s_pool)
            p.emit("gpsimd", lambda eh: eh.tensor_mul(st.mk16[:, :], st.z16[:, :], st.vmask[:, :]),
                   waits=[(s_pool, s_pool.n), (s_act, s_act.n), (s_pe, s_pe.n)], inc=s_pool)
            p.act(st.mk16[:, :], st.mk16[:, :], AF.Silu,
                  waits=[(s_pool, s_pool.n)], inc=s_act)
            silu_done = s_act.n
            for wc in range(16):
                wmm = [(s_act, silu_done)]
                if wc == 0:
                    wmm.append((s_dve, s_dve.n))
                    wmm.append((s_po, s_po.n))
                p.transp(PB1b[:, wc * 128:(wc + 1) * 128], st.mk16[:, wc * 128:(wc + 1) * 128],
                         bufs.id16[:, :], waits=wmm, inc=s_pe if wc == 15 else None)
            tr_done = s_pe.n
            dstT = st.mkT[:, :].rearrange("p (w q) -> p w q", q=G)[:, :, tt * 128:(tt + 1) * 128]
            srcT = PB1b[:, 0:2048].rearrange("p (w j) -> p w j", j=128)
            p.emit("vector", lambda eh, d2=dstT, s2=srcT: eh.tensor_copy(d2, s2),
                   waits=[(s_pe, tr_done)], inc=s_dve)
        mkT_done = s_dve.n

        s_wo2 = p.sem("p4_wo2")
        for og in range(NCH):
            p.dma("gpsimd", st.w1o16[:, :].rearrange("p (c j) -> p c j", j=128),
                  hb["w1out"][:, og * 128:(og + 1) * 128].rearrange("(c i) j -> i c j", i=128),
                  waits=[(s_po, s_po.n)], inc=s_wo2)
            wo_ld = s_wo2.n
            po = P[6 + og % 2]
            for wc in range(16):
                wmm = []
                if wc == 0:
                    wmm.append((s_dve, mkT_done))
                    wmm.append((s_wo2, wo_ld))
                    wmm.append((s_ev, max(s_ev.n - 1, 0)))
                p.mm(po[:, :], st.w1o16[:, wc * 128:(wc + 1) * 128],
                     st.mkT[:, wc * G:(wc + 1) * G],
                     start=(wc == 0), stop=(wc == 15), waits=wmm,
                     inc=s_po if wc == 15 else None)
            ost = st.outst[og % 2]
            p.emit("vector", lambda eh, po=po, o=ost, xc=x[og], t=t0: eh.tensor_add(
                o[:, :], xc[:, t:t + G], po[:, :]),
                waits=[(s_po, s_po.n), (s_yst, max(s_yst.n - 16, 0))], inc=s_ev)
            p.dma("sync", ydram[b, og * 128:(og + 1) * 128, t0:t0 + G], ost[:, :],
                  waits=[(s_ev, s_ev.n)], inc=s_yst)
            p.emit("sync", lambda eh: eh.nop(), waits=[(s_yst, s_yst.n)])
    return [(s_yst, s_yst.n), (s_ev, s_ev.n), (s_wdma, s_wdma.n),
            (p.sem("p4_wo2"), p.sem("p4_wo2").n)]



def build_program(phases=("p1", "p2", "p3", "p4")):
    nc = bass.Bass("TRN2", target_bir_lowering=False)
    hb = declare_inputs(nc)
    y = nc.dram_tensor("y", [NB, D, T], F32, kind="ExternalOutput")
    scr = {
        "wg3s": nc.dram_tensor("wg3s", [NEX, D, EI], BF16),
        "wu3s": nc.dram_tensor("wu3s", [NEX, D, EI], BF16),
        "wd3s": nc.dram_tensor("wd3s", [NEX, EI, D], BF16),
    }
    ctx = ExitStack()
    p = Prog(nc, ctx)
    bufs = Bufs(nc, ctx, p, hb)
    p.emit("tensor", lambda eh: eh.nop(), waits=((bufs.ld_c, bufs.ld_c.n),))
    if "p2" in phases:
        emit_moe_casts(p, hb, scr)

    with ExitStack() as pctx:
        ln = LNState(nc, pctx)
        batch_done = ()
        for b in range(NB):
            sx = emit_x_load(p, bufs, hb, b, prev=batch_done)
            last = [(sx, sx.n)]
            if "p1" in phases:
                with ExitStack() as c1:
                    st1 = P1State(nc, c1, tag=f"_b{b}")
                    last = emit_l0_attn_batch(p, bufs, st1, ln, hb, sx)
            if "p2" in phases:
                with ExitStack() as c2:
                    st2 = P2State(nc, c2, tag=f"_b{b}")
                    last = emit_moe_batch(p, bufs, st2, ln, hb, scr, last)
            if "p3" in phases:
                with ExitStack() as c3:
                    st3 = P3State(nc, c3, tag=f"_b{b}")
                    last = emit_l1_attn_batch(p, bufs, st3, ln, hb, last)
            if "p4" in phases:
                with ExitStack() as c4:
                    st4 = P4State(nc, c4, tag=f"_b{b}")
                    last = emit_l1_ffn_batch(p, bufs, st4, ln, hb, y, b, last)
            else:
                emit_x_store(p, bufs, y, b, last)
            batch_done = last
            p.flush()
    s = p.sem("st_x")
    p.emit("gpsimd", lambda eh: eh.nop(), waits=tuple(batch_done) + ((s, s.n),))
    p.flush()
    ctx.close()
    return nc


def kernel(**inputs):
    shared = prep_shared_inputs(inputs)
    per_core = [prep_core_inputs(inputs, c, shared) for c in range(NCORE)]
    nc = build_program(phases=("p1", "p2", "p3", "p4"))
    res = run_bass_kernel_spmd(nc, per_core, core_ids=list(range(NCORE)))
    out = np.empty((32, T, D), np.float32)
    for c in range(NCORE):
        yc = np.asarray(res.results[c]["y"], np.float32)
        out[c * NB:(c + 1) * NB] = yc.transpose(0, 2, 1)
    return out


# version 04
# speedup vs baseline: 1.0444x; 1.0444x over previous
import numpy as np
import ml_dtypes
from contextlib import ExitStack

import concourse.bass as bass
import concourse.mybir as mybir
from concourse.bass_utils import run_bass_kernel_spmd

F32 = mybir.dt.float32
BF16 = mybir.dt.bfloat16
AF = mybir.ActivationFunctionType
ALU = mybir.AluOpType
AX = mybir.AxisListType
BF = ml_dtypes.bfloat16

NCORE = 8
NB = 4
T = 2048
D = 1024
NCH = 8
NH = 16
HD = 64
NPAIR = 8
G = 512
NGB = T // G
NE = 32
NEX = 34
EI = 512
LNEPS = 1e-5

ENGINES = ("tensor", "vector", "scalar", "gpsimd", "sync")
DBG = False
P3S = 99
P3S2 = 99
P3IT = 6
P3IP = 99
P3W2 = 0


class Sem:
    __slots__ = ("h", "n")

    def __init__(self, h):
        self.h = h
        self.n = 0


class Prog:

    def __init__(self, nc, ctx):
        self.nc = nc
        self.ctx = ctx
        self.items = {e: [] for e in ENGINES}
        self._sems = {}

    def sem(self, name):
        s = self._sems.get(name)
        if s is None:
            s = Sem(self.ctx.enter_context(self.nc.semaphore(name)))
            self._sems[name] = s
        return s

    def emit(self, eng, fn, waits=(), inc=None, inck=1):
        ws = tuple((s, v) for (s, v) in waits if v > 0)
        self.items[eng].append((fn, ws, inc, inck))
        if inc is not None:
            inc.n += inck
        return inc.n if inc is not None else 0

    def flush(self):
        live = {e: items for e, items in self.items.items() if items}
        if not live:
            return
        with self.nc.Block() as block:
            for e, items in live.items():
                def body(eh, _items=tuple(items)):
                    for fn, waits, inc, inck in _items:
                        for (s, v) in waits:
                            eh.wait_ge(s.h, v)
                        instr = fn(eh)
                        if inc is not None:
                            instr.then_inc(inc.h, inck)
                getattr(block, e)(body)
        self.items = {e: [] for e in ENGINES}

    def mm(self, out, lhsT, rhs, start, stop, tile=None, waits=(), inc=None):
        return self.emit(
            "tensor",
            lambda eh, o=out, l=lhsT, r=rhs, s1=start, s2=stop, tp=tile:
            eh.matmul(o, l, r, start=s1, stop=s2, tile_position=tp),
            waits, inc)

    def transp(self, out, in_, ident, waits=(), inc=None):
        return self.emit(
            "tensor", lambda eh, o=out, i=in_, d=ident: eh.transpose(o, i, d),
            waits, inc)

    def act(self, out, in_, func, bias=0.0, scale=1.0, accum=None, waits=(), inc=None):
        return self.emit(
            "scalar",
            lambda eh, o=out, i=in_, f=func, b=bias, s=scale, a=accum:
            eh.activation(o, i, f, bias=b, scale=s, accum_out=a),
            waits, inc)

    def dma(self, eng, dst, src, waits=(), inc=None):
        return self.emit(eng, lambda eh, d=dst, s=src: eh.dma_start(d, s),
                         waits, inc, inck=16)

    def vop(self, eng, name, *args, waits=(), inc=None, **kw):
        return self.emit(
            eng, lambda eh, n=name, a=args, k=kw: getattr(eh, n)(*a, **k),
            waits, inc)


def ap_full(t):
    return t[tuple(slice(0, s) for s in t.shape)]



def prep_shared_inputs(inputs):
    d = {}
    W = np.asarray(inputs["L0_attn_W_qkv"])
    b = np.asarray(inputs["L0_attn_b_qkv"])
    qrows = np.array([192 * (2 * p + j // 64) + (j % 64)
                      for p in range(NPAIR) for j in range(128)])
    krows = qrows + 64
    vrows = np.array([192 * h + 128 + dd for h in range(NH) for dd in range(HD)])
    def tile_pairs(*mats):
        out = np.empty((NPAIR, 128, len(mats) * 1024), np.float32)
        for pi in range(NPAIR):
            for mi, M in enumerate(mats):
                blk = M[:, pi * 128:(pi + 1) * 128].reshape(NCH, 128, 128)
                out[pi, :, mi * 1024:(mi + 1) * 1024] = blk.transpose(1, 0, 2).reshape(128, 1024)
        return np.ascontiguousarray(out)
    d["wqkv"] = tile_pairs(W[qrows].T, W[krows].T, W[vrows].T)
    d["bq"] = np.ascontiguousarray(b[qrows].reshape(NPAIR, 128).T)
    d["bk"] = np.ascontiguousarray(b[krows].reshape(NPAIR, 128).T)
    d["bv"] = np.ascontiguousarray(b[vrows].reshape(1, 1024))
    d["wo"] = tile_pairs(np.asarray(inputs["L0_attn_W_out"]).T)
    d["bo"] = np.ascontiguousarray(np.asarray(inputs["L0_attn_b_out"]).reshape(NCH, 128).T)

    d["gw"] = np.ascontiguousarray(np.asarray(inputs["L0_ffn_gate_w"]).T)
    d["gb"] = np.ascontiguousarray(np.asarray(inputs["L0_ffn_gate_bias"]).reshape(1, NE))
    wg = np.asarray(inputs["L0_ffn_w_gate"])
    wu = np.asarray(inputs["L0_ffn_w_up"])
    wd = np.asarray(inputs["L0_ffn_w_down"])
    shgT = np.asarray(inputs["L0_ffn_sh_gate"]).T
    shuT = np.asarray(inputs["L0_ffn_sh_up"]).T
    shdT = np.asarray(inputs["L0_ffn_sh_down"]).T
    wg3 = np.empty((NEX, 1024, 512), np.float32)
    wu3 = np.empty((NEX, 1024, 512), np.float32)
    wd3 = np.empty((NEX, 512, 1024), np.float32)
    wg3[:NE] = wg.transpose(0, 2, 1)
    wu3[:NE] = wu.transpose(0, 2, 1)
    wd3[:NE] = wd.transpose(0, 2, 1)
    wg3[NE], wg3[NE + 1] = shgT[:, 0:512], shgT[:, 512:1024]
    wu3[NE], wu3[NE + 1] = shuT[:, 0:512], shuT[:, 512:1024]
    wd3[NE], wd3[NE + 1] = shdT[0:512, :], shdT[512:1024, :]
    d["wg3"] = np.ascontiguousarray(wg3.reshape(NEX, 8, 128, 512).transpose(0, 2, 1, 3)
                                    .reshape(NEX, 128, 4096))
    d["wu3"] = np.ascontiguousarray(wu3.reshape(NEX, 8, 128, 512).transpose(0, 2, 1, 3)
                                    .reshape(NEX, 128, 4096))
    d["wd3"] = np.ascontiguousarray(wd3.reshape(NEX, 4, 128, 1024).transpose(0, 2, 1, 3)
                                    .reshape(NEX, 128, 4096))

    w1 = np.asarray(inputs["L1_attn_w_qkv"])
    d["w1qkv"] = tile_pairs(w1[:, 0:1024], w1[:, 1024:2048], w1[:, 2048:3072])
    d["w1p"] = tile_pairs(np.asarray(inputs["L1_attn_w_proj"]))

    w1inT = np.zeros((1024, 2048), np.float32)
    w1inT[:, :2040] = np.asarray(inputs["L1_ffn_W_in"]).T
    b1p = np.zeros((1, 2048), np.float32)
    b1p[0, :2040] = np.asarray(inputs["L1_ffn_b_in"])
    w1outT = np.zeros((2048, 1024), np.float32)
    w1outT[:2040] = np.asarray(inputs["L1_ffn_W_out"]).T
    d["w1in"] = w1inT
    d["b1in"] = b1p
    d["w1out"] = np.ascontiguousarray(
        w1outT.reshape(16, 128, 8, 128).transpose(2, 1, 0, 3).reshape(8, 128, 2048))

    d["c_or16"] = np.ones((1, 128), BF)
    d["c_or32"] = np.ones((1, 128), np.float32)
    d["c_oc16"] = np.ones((128, 1), BF)
    d["c_oc32"] = np.ones((128, 1), np.float32)
    d["c_id16"] = np.eye(128, dtype=BF)
    d["c_id32"] = np.eye(128, dtype=np.float32)
    hm = np.zeros((128, 2), np.float32)
    hm[0:64, 0] = 1.0
    hm[64:128, 1] = 1.0
    d["c_hm"] = hm
    sel = np.zeros((32, NE * 128), BF)
    for e in range(NE):
        sel[e, e * 128:(e + 1) * 128] = 1.0
    d["c_sel"] = sel
    return d


def prep_core_inputs(inputs, core, shared):
    d = dict(shared)
    x = np.asarray(inputs["L0_attn_x"])
    xs = x[core * NB:(core + 1) * NB]
    d["x"] = np.ascontiguousarray(xs.transpose(0, 2, 1))
    return d


def declare_inputs(nc):
    t = {}

    def di(name, shape, dtype=F32):
        t[name] = nc.dram_tensor(name, list(shape), dtype, kind="ExternalInput")
    di("x", (NB, D, T))
    di("wqkv", (NPAIR, 128, 3072))
    di("bq", (128, NPAIR)); di("bk", (128, NPAIR)); di("bv", (1, D))
    di("wo", (NPAIR, 128, 1024)); di("bo", (128, NCH))
    di("gw", (D, NE)); di("gb", (1, NE))
    di("wg3", (NEX, 128, 4096)); di("wu3", (NEX, 128, 4096)); di("wd3", (NEX, 128, 4096))
    di("w1qkv", (NPAIR, 128, 3072)); di("w1p", (NPAIR, 128, 1024))
    di("w1in", (D, 2048)); di("b1in", (1, 2048)); di("w1out", (NCH, 128, 2048))
    di("c_or16", (1, 128), BF16); di("c_or32", (1, 128))
    di("c_oc16", (128, 1), BF16); di("c_oc32", (128, 1))
    di("c_id16", (128, 128), BF16); di("c_id32", (128, 128))
    di("c_hm", (128, 2))
    di("c_sel", (32, NE * 128), BF16)
    return t



class Bufs:
    def __init__(self, nc, ctx, p, hb):
        sb = lambda n, s, dt: ctx.enter_context(nc.sbuf_tensor(n, s, dt))
        ps = lambda n, s, dt: ctx.enter_context(nc.psum_tensor(n, s, dt))
        self.x = [sb(f"x_{c}", [128, T], BF16) for c in range(NCH)]
        self.or16 = sb("k_or16", [1, 128], BF16)
        self.or32 = sb("k_or32", [1, 128], F32)
        self.oc16 = sb("k_oc16", [128, 1], BF16)
        self.oc32 = sb("k_oc32", [128, 1], F32)
        self.id16 = sb("k_id16", [128, 128], BF16)
        self.hm = sb("k_hm", [128, 2], F32)
        self.PB = [ps(f"psumB{i}", [128, 4 * G], F32) for i in range(2)]
        self.P = [self.PB[i // 4][:, (i % 4) * G:(i % 4 + 1) * G] for i in range(8)]
        self.hb = hb
        s = p.sem("ld_c")
        for nm, dst in (("c_or16", self.or16), ("c_or32", self.or32),
                        ("c_oc16", self.oc16), ("c_oc32", self.oc32),
                        ("c_id16", self.id16),
                        ("c_hm", self.hm)):
            p.dma("sync", ap_full(dst), ap_full(hb[nm]), inc=s)
        self.ld_c = s


def emit_x_load(p, bufs, hb, b, prev=()):
    s = p.sem("ld_x")
    st = p.sem("st_x")
    for c in range(NCH):
        w = list(prev) + [(st, st.n)] if c == 0 else ()
        p.dma("gpsimd", bufs.x[c][:, :], hb["x"][b, c * 128:(c + 1) * 128, :],
              waits=w, inc=s)
    return s


def emit_x_store(p, bufs, ydram, b, waits):
    s = p.sem("st_x")
    for c in range(NCH):
        p.dma("gpsimd", ydram[b, c * 128:(c + 1) * 128, :], bufs.x[c][:, :],
              waits=waits, inc=s)
    return s



class LNState:
    def __init__(self, nc, ctx, tag=""):
        sb = lambda n, s, dt: ctx.enter_context(nc.sbuf_tensor(n + tag, s, dt))
        self.xsq = [sb(f"ln_xsq{i}", [128, G], BF16) for i in range(2)]
        self.rA = sb("ln_rA", [1, G], F32)
        self.rB = sb("ln_rB", [1, G], F32)
        self.rC = sb("ln_rC", [1, G], F32)
        self.mb16 = sb("ln_mb16", [128, G], BF16)
        self.rb16 = sb("ln_rb16", [128, G], BF16)
        self.tsub = sb("ln_tsub", [128, G], BF16)


def emit_ln(p, bufs, ln, xtiles, t0, h16, hcol, pst, pst2, pbc, xw=()):
    s_sq = p.sem("ln_sq")
    s_st = p.sem("ln_st")
    s_row = p.sem("ln_row")
    s_bc = p.sem("ln_bc")
    s_ev = p.sem("ln_ev")
    s_ap = p.sem("ln_ap")

    ap0 = s_ap.n
    sq0 = s_sq.n
    st0 = s_st.n
    row0 = s_row.n
    ev0 = s_ev.n
    for c in range(NCH):
        xa = xtiles[c][:, t0:t0 + G]
        sqt = ln.xsq[c % 2]
        w_act = [(s_st, st0 + c - 1)] if c >= 2 else [(s_st, st0)]
        p.act(sqt[:, :], xa, AF.Square, waits=w_act, inc=s_sq)
        wmm = ([(s_row, row0)] + list(xw)) if c == 0 else ()
        p.mm(pst[0:1, :], bufs.oc16[:, :], xa, start=(c == 0), stop=(c == NCH - 1),
             waits=wmm)
        p.mm(pst2[64:65, :], bufs.oc16[:, :], sqt[:, :], start=(c == 0), stop=(c == NCH - 1),
             tile=(0, 64), waits=[(s_sq, sq0 + c + 1)], inc=s_st)
    st_done = s_st.n

    p.act(ln.rA[:, :], pst[0:1, :], AF.Copy, scale=1.0 / D,
          waits=[(s_st, st_done)], inc=s_row)
    p.act(ln.rB[:, :], pst2[64:65, :], AF.Copy, scale=1.0 / D, inc=s_row)
    p.act(ln.rC[:, :], ln.rA[:, :], AF.Square, waits=[(s_row, row0 + 1)], inc=s_row)
    p.emit("vector", lambda eh: eh.scalar_tensor_tensor(
        ln.rB[:, :], ln.rB[:, :], LNEPS, ln.rC[:, :], op0=ALU.add, op1=ALU.subtract),
        waits=[(s_row, s_row.n)], inc=s_row)
    p.act(ln.rB[:, :], ln.rB[:, :], AF.Sqrt, waits=[(s_row, s_row.n)], inc=s_row)
    p.emit("vector", lambda eh: eh.reciprocal(ln.rC[:, :], ln.rB[:, :]),
           waits=[(s_row, s_row.n)], inc=s_row)
    row_done = s_row.n

    p.mm(pbc[:, :], bufs.or32[:, :], ln.rA[:, :], start=True, stop=True,
         waits=[(s_row, row_done), (s_ev, ev0)], inc=s_bc)
    p.act(ln.mb16[:, :], pbc[:, :], AF.Copy, waits=[(s_bc, s_bc.n), (s_ap, ap0)], inc=s_ev)
    p.mm(pbc[:, :], bufs.or32[:, :], ln.rC[:, :], start=True, stop=True,
         waits=[(s_ev, s_ev.n)], inc=s_bc)
    p.act(ln.rb16[:, :], pbc[:, :], AF.Copy, waits=[(s_bc, s_bc.n)], inc=s_ev)
    ev_done = s_ev.n

    for c in range(NCH):
        xa = xtiles[c][:, t0:t0 + G]
        dst = h16[c][:, hcol:hcol + G]
        wv = [(s_ev, ev_done)] if c == 0 else [(s_ap, s_ap.n)]
        p.emit("vector", lambda eh, a=xa: eh.tensor_sub(ln.tsub[:, :], a, ln.mb16[:, :]),
               waits=wv, inc=s_ap)
        wp = [(s_ap, s_ap.n)]
        if c == 0:
            wp.append((s_ev, ev_done))
        p.emit("gpsimd", lambda eh, o=dst: eh.tensor_mul(o, ln.tsub[:, :], ln.rb16[:, :]),
               waits=wp, inc=s_ap)
    return s_ap.n



class P1State:
    def __init__(self, nc, ctx, tag=""):
        sb = lambda n, s, dt: ctx.enter_context(nc.sbuf_tensor(n + tag, s, dt))
        self.h16 = [sb(f"p1_h{c}", [128, T], BF16) for c in range(NCH)]
        self.logits = [sb(f"p1_lg{c}", [128, T // 2], BF16) for c in range(NCH)]
        self.vaug = [sb(f"p1_va{k}", [128, 2 * (HD + 1)], BF16) for k in range(16)]
        self.wqkv16 = sb("p1_wqkv", [128, 3072], BF16)
        self.wo16 = [sb("p1_wo0", [128, 1024], BF16)]
        self.bq = sb("p1_bq", [128, NPAIR], F32)
        self.bk = sb("p1_bk", [128, NPAIR], F32)
        self.bo = sb("p1_bo", [128, NCH], F32)
        self.bv16 = sb("p1_bv16", [1, 128], BF16)
        self.q = sb("p1_q", [128, T // 2], BF16)
        self.k = sb("p1_k", [128, T], BF16)
        self.nt = [sb(f"p1_nt{i}", [128, G], BF16) for i in range(4)]
        self.mqa = sb("p1_mqa", [128, 1], F32)
        self.mka = sb("p1_mka", [128, 1], F32)
        self.mprod = sb("p1_mprod", [128, 1], F32)
        self.cb = sb("p1_cb", [128, 2], F32)
        self.zinv = sb("p1_zinv", [1, 2 * G], F32)
        self.zb16 = sb("p1_zb16", [128, G], BF16)


def emit_l0_attn_batch(p, bufs, st, ln, hb, xld):
    x = bufs.x
    P = bufs.P
    s_wdma = p.sem("p1_wdma")
    s_bdma = p.sem("p1_bdma")
    s_pe = p.sem("p1_pe")
    s_qev = p.sem("p1_qev")
    s_exp = p.sem("p1_exp")
    s_av = p.sem("p1_av")
    s_dve = p.sem("p1_dve")
    s_lgt = p.sem("p1_lgt")
    s_res = p.sem("p1_res")
    s_ap = p.sem("ln_ap")

    if not hasattr(st, "tr"):
        p.dma("sync", st.bq[:, :], ap_full(hb["bq"]), inc=s_bdma)
        p.dma("sync", st.bk[:, :], ap_full(hb["bk"]), inc=s_bdma)
        p.dma("sync", st.bo[:, :], ap_full(hb["bo"]), inc=s_bdma)
        for kt in range(16):
            od = st.vaug[kt].rearrange("p (h u) -> p h u", u=HD + 1)[:, :, HD:HD + 1]
            p.emit("gpsimd", lambda eh, d2=od: eh.memset(d2, 1.0), inc=s_dve)
    bias_ld = s_bdma.n

    p.emit("tensor", lambda eh: eh.nop(), waits=[(xld, xld.n)])
    p.emit("scalar", lambda eh: eh.nop(), waits=[(xld, xld.n)])
    p.emit("vector", lambda eh: eh.nop(), waits=[(xld, xld.n)])
    for g in range(NGB):
        emit_ln(p, bufs, ln, x, g * G, st.h16, g * G, P[0], P[1], P[2])
    ln_done = s_ap.n

    if not hasattr(st, "tr"):
        st.tr = {"qk_pe": 0, "av_end": 0, "exp_end": 0, "wo_pe": [0, 0]}
    tr = st.tr

    for qh in range(2):
        for pr in range(NPAIR):
            wprev = [(s_pe, tr["qk_pe"])]
            p.dma("gpsimd", st.wqkv16[:, :], hb["wqkv"][pr], waits=wprev, inc=s_wdma)
            p.dma("gpsimd", st.bv16[:, :], hb["bv"][:, pr * 128:(pr + 1) * 128],
                  inc=s_wdma)
            w_ld = s_wdma.n

            prev_av_end = tr["av_end"]
            prev_exp_end = tr["exp_end"]
            for kt in range(16):
                pv = P[2 + kt % 2]
                for c in range(NCH):
                    wmm = []
                    if c == 0:
                        wmm.append((s_wdma, w_ld))
                        if kt < 2:
                            wmm.append((s_ap, ln_done))
                            wmm.append((s_exp, prev_exp_end))
                        else:
                            wmm.append((s_dve, s_dve.n - 1))
                    p.mm(pv[:, 0:128], st.h16[c][:, kt * 128:(kt + 1) * 128],
                         st.wqkv16[:, 2048 + c * 128:2048 + (c + 1) * 128],
                         start=(c == 0), stop=False, waits=wmm)
                p.mm(pv[:, 0:128], bufs.or16[:, :],
                     st.bv16[:, :], start=False, stop=True, inc=s_pe)
                vdst = st.vaug[kt].rearrange("p (h u) -> p h u", u=HD + 1)[:, :, 0:HD]
                vsrc = pv[:, 0:128].rearrange("p (h u) -> p h u", u=HD)
                p.emit("vector", lambda eh, d2=vdst, s2=vsrc: eh.tensor_copy(d2, s2),
                       waits=[(s_pe, s_pe.n), (s_av, prev_av_end)], inc=s_dve)
            v_done = s_dve.n

            def qk_emit(wt, bt, dst, groups):
                for gi, g in enumerate(groups):
                    pq = P[gi % 2]
                    for c in range(NCH):
                        wmm = []
                        if c == 0:
                            wmm.append((s_wdma, w_ld))
                            wmm.append((s_qev, max(s_qev.n - 1, 0)))
                            wmm.append((s_exp, prev_exp_end))
                        p.mm(pq[:, :], wt[:, c * 128:(c + 1) * 128],
                             st.h16[c][:, g * G:(g + 1) * G],
                             start=(c == 0), stop=(c == NCH - 1), waits=wmm,
                             inc=s_pe if c == NCH - 1 else None)
                    p.act(dst[:, gi * G:(gi + 1) * G], pq[:, :], AF.Identity,
                          bias=bt[:, pr:pr + 1],
                          waits=[(s_pe, s_pe.n), (s_bdma, bias_ld)], inc=s_qev)
            qk_emit(st.wqkv16[:, 1024:2048], st.bk, st.k, list(range(NGB)))
            qk_emit(st.wqkv16[:, 0:1024], st.bq, st.q, [2 * qh, 2 * qh + 1])
            tr["qk_pe"] = s_pe.n
            qk_done = s_qev.n

            p.emit("vector", lambda eh: eh.reduce_max(
                st.mqa[:, :], st.q[:, :], axis=AX.X, apply_absolute_value=True),
                waits=[(s_qev, qk_done)], inc=s_dve)
            p.emit("vector", lambda eh: eh.reduce_max(
                st.mka[:, :], st.k[:, :], axis=AX.X, apply_absolute_value=True), inc=s_dve)
            p.emit("vector", lambda eh: eh.tensor_mul(
                st.mprod[:, :], st.mqa[:, :], st.mka[:, :]),
                waits=[(s_dve, s_dve.n)], inc=s_dve)
            mp_done = s_dve.n
            p.mm(P[6][0:1, 0:2], st.mprod[:, :], bufs.hm[:, :], start=True, stop=True,
                 waits=[(s_dve, mp_done), (s_exp, s_exp.n)], inc=s_pe)
            p.act(st.cb[0:1, 0:2], P[6][0:1, 0:2], AF.Copy, scale=-0.125,
                  waits=[(s_pe, s_pe.n)], inc=s_qev)
            for h in range(2):
                p.mm(P[6][:, 4 + h:5 + h], bufs.or32[:, :], st.cb[0:1, h:h + 1],
                     start=True, stop=True, waits=[(s_qev, s_qev.n)],
                     inc=s_pe if h == 1 else None)
            p.emit("vector", lambda eh: eh.tensor_copy(st.cb[:, 0:2], P[6][:, 4:6]),
                   waits=[(s_pe, s_pe.n)], inc=s_dve)
            cb_done = s_dve.n

            exp_hist = []
            av_hist = []
            for gi in range(2):
                qcols = slice(gi * G, (gi + 1) * G)
                for kc in range(16):
                    j = gi * 16 + kc
                    spar = kc % 2
                    psA, psB = P[0 + spar], P[2 + spar]
                    kc1 = slice(kc * 128, kc * 128 + 64)
                    kc2 = slice(kc * 128 + 64, (kc + 1) * 128)
                    wmm = []
                    if j == 0:
                        wmm.append((s_qev, qk_done))
                        wmm.append((s_dve, v_done))
                    if j >= 2:
                        wmm.append((s_exp, exp_hist[j - 2]))
                    p.mm(psA[0:64, :], st.k[0:64, kc1], st.q[0:64, qcols],
                         start=True, stop=True, tile=(0, 0), waits=wmm)
                    p.mm(psA[64:128, :], st.k[0:64, kc2], st.q[0:64, qcols],
                         start=True, stop=True, tile=(0, 64))
                    p.mm(psB[0:64, :], st.k[64:128, kc1], st.q[64:128, qcols],
                         start=True, stop=True, tile=(64, 0))
                    p.mm(psB[64:128, :], st.k[64:128, kc2], st.q[64:128, qcols],
                         start=True, stop=True, tile=(64, 64), inc=s_pe)
                    sc_done = s_pe.n
                    for h, pst_ in ((0, psA), (1, psB)):
                        nt = st.nt[2 * h + spar]
                        wact = [(s_pe, sc_done), (s_dve, cb_done)]
                        if j >= 2:
                            wact.append((s_av, av_hist[j - 2]))
                        p.act(nt[:, :], pst_[:, :], AF.Exp, bias=st.cb[:, h:h + 1],
                              scale=0.125, waits=wact, inc=s_exp)
                    exp_hist.append(s_exp.n)
                    first = (kc == 0)
                    va = st.vaug[kc].rearrange("p (h u) -> p h u", u=HD + 1)
                    wav = [(s_exp, exp_hist[-1])]
                    if first:
                        wav.append((s_lgt, s_lgt.n))
                        wav.append((s_dve, s_dve.n))
                    last = (kc == 15)
                    p.mm(P[4][0:64, :], va[:, 0, 0:HD], st.nt[0 + spar][:, :],
                         start=first, stop=last, tile=(0, 0), waits=wav)
                    p.mm(P[7][64:128, :], va[:, 1, 0:HD], st.nt[2 + spar][:, :],
                         start=first, stop=last, tile=(0, 64))
                    p.mm(P[5][0:1, :], bufs.oc16[:, :], st.nt[0 + spar][:, :],
                         start=first, stop=last, tile=(0, 0))
                    p.mm(P[6][0:1, :], bufs.oc16[:, :], st.nt[2 + spar][:, :],
                         start=first, stop=last, tile=(0, 0), inc=s_av)
                    av_hist.append(s_av.n)
                av_done = s_av.n
                tr["av_end"] = av_done
                p.emit("vector", lambda eh: eh.reciprocal(st.zinv[0:1, 0:G], P[5][0:1, :]),
                       waits=[(s_av, av_done)], inc=s_dve)
                p.emit("vector", lambda eh: eh.reciprocal(st.zinv[0:1, G:2 * G], P[6][0:1, :]),
                       inc=s_dve)
                zi_done = s_dve.n
                p.mm(P[6][0:64, :], bufs.or32[:, 0:64], st.zinv[0:1, 0:G],
                     start=True, stop=True, tile=(0, 0),
                     waits=[(s_dve, zi_done), (s_exp, s_exp.n)])
                p.mm(P[6][64:128, :], bufs.or32[:, 0:64], st.zinv[0:1, G:2 * G],
                     start=True, stop=True, tile=(0, 64), inc=s_pe)
                p.act(st.zb16[:, :], P[6][:, :], AF.Copy, waits=[(s_pe, s_pe.n)], inc=s_exp)
                tr["exp_end"] = s_exp.n
                lt = st.logits[pr]
                lcols = slice(gi * G, (gi + 1) * G)
                p.emit("vector", lambda eh, lt=lt, lc=lcols: eh.tensor_mul(
                    lt[0:64, lc], P[4][0:64, :], st.zb16[0:64, :]),
                    waits=[(s_exp, s_exp.n)], inc=s_lgt)
                p.emit("vector", lambda eh, lt=lt, lc=lcols: eh.tensor_mul(
                    lt[64:128, lc], P[7][64:128, :], st.zb16[64:128, :]),
                    inc=s_lgt)
        lg_done = s_lgt.n

        s_wo = p.sem("p1_wo")
        for oc in range(NCH):
            wbuf = st.wo16[0]
            p.dma("gpsimd", wbuf[:, :], hb["wo"][oc],
                  waits=[(s_pe, tr["wo_pe"][0])], inc=s_wo)
            wo_ld = s_wo.n
            for gi in range(2):
                po = P[0 + gi % 2]
                for pc in range(NCH):
                    wmm = []
                    if pc == 0:
                        wmm.append((s_lgt, lg_done))
                        wmm.append((s_wo, wo_ld))
                        wmm.append((s_res, max(s_res.n - 1, 0)))
                        if oc == 0 and gi == 0:
                            wmm.append((s_exp, tr["exp_end"]))
                    p.mm(po[:, :], wbuf[:, pc * 128:(pc + 1) * 128],
                         st.logits[pc][:, gi * G:(gi + 1) * G],
                         start=(pc == 0), stop=(pc == NCH - 1), waits=wmm,
                         inc=s_pe if pc == NCH - 1 else None)
                tr["wo_pe"][0] = s_pe.n
                xa = x[oc][:, (2 * qh + gi) * G:(2 * qh + gi + 1) * G]
                p.emit("vector", lambda eh, po=po, xa=xa, oc=oc: eh.scalar_tensor_tensor(
                    xa, po[:, :], st.bo[:, oc:oc + 1], xa, op0=ALU.add, op1=ALU.add),
                    waits=[(s_pe, s_pe.n), (s_bdma, bias_ld)], inc=s_res)
        tr["qk_pe"] = s_pe.n
    return [(s_res, s_res.n), (s_wdma, s_wdma.n), (p.sem("p1_wo"), p.sem("p1_wo").n),
            (s_bdma, s_bdma.n)]



def emit_fences(p, engines, waits):
    for e in engines:
        p.emit(e, lambda eh: eh.nop(), waits=waits)


class P2State:
    def __init__(self, nc, ctx, tag=""):
        sb = lambda n, s, dt: ctx.enter_context(nc.sbuf_tensor(n + tag, s, dt))
        self.h16 = [sb(f"p2_h{c}", [128, G], BF16) for c in range(NCH)]
        self.gw16 = sb("p2_gw16", [128, NCH * NE], BF16)
        self.gb32 = sb("p2_gb32", [1, NE], F32)
        self.gbb = sb("p2_gbb", [128, NE], F32)
        self.csel = sb("p2_csel", [32, NE * 128], BF16)
        self.scores = [sb(f"p2_sc{t}", [128, NE], F32) for t in range(4)]
        self.scb = [sb(f"p2_scb{t}", [128, NE], F32) for t in range(4)]
        self.gs = [sb(f"p2_gs{t}", [128, 8], F32) for t in range(4)]
        self.gtmp = [sb(f"p2_gt{t}", [128, 8], F32) for t in range(4)]
        self.top8 = [sb(f"p2_t8{t}", [128, 8], F32) for t in range(4)]
        self.masked = [sb(f"p2_mk{t}", [128, NE], F32) for t in range(4)]
        self.sel = [sb(f"p2_sl{t}", [128, NE], F32) for t in range(4)]
        self.w8 = [sb(f"p2_w8{t}", [128, NE], F32) for t in range(4)]
        self.sumw = [sb(f"p2_sw{t}", [128, 1], F32) for t in range(4)]
        self.c2 = [sb(f"p2_c2{t}", [128, 1], F32) for t in range(4)]
        self.comb = [sb(f"p2_cb{t}", [128, NE], BF16) for t in range(4)]
        self.combT = sb("p2_combT", [32, G], BF16)
        self.wg16 = [sb(f"p2_wg{i}", [128, 4096], BF16) for i in range(2)]
        self.wu16 = [sb(f"p2_wu{i}", [128, 4096], BF16) for i in range(2)]
        self.wd16 = [sb(f"p2_wd{i}", [128, 4096], BF16) for i in range(2)]
        self.silu = [sb(f"p2_si{i}", [128, G], F32) for i in range(2)]
        self.gu = [[sb(f"p2_gu{i}_{c}", [128, G], BF16) for c in range(4)] for i in range(2)]
        self.accum = [sb(f"p2_ac{c}", [128, G], F32) for c in range(4)]
        self.c_sb = [sb(f"p2_cs{i}", [128, G], F32) for i in range(2)]


def emit_moe_casts(p, hb, scr):
    s = p.sem("p2_cast")
    for e in range(NEX):
        p.dma("gpsimd", scr["wg3s"][e], hb["wg3"][e], inc=s)
        p.dma("gpsimd", scr["wu3s"][e], hb["wu3"][e], inc=s)
        p.dma("gpsimd", scr["wd3s"][e], hb["wd3"][e], inc=s)
        p.emit("gpsimd", lambda eh: eh.nop(), waits=[(s, s.n)])
    return s


def emit_moe_batch(p, bufs, st, ln, hb, scr, prev_done):
    x = bufs.x
    P = bufs.P
    s_cast = p.sem("p2_cast")
    s_wst = p.sem("p2_wst")
    s_gl = p.sem("p2_gl")
    s_sig = p.sem("p2_sig")
    s_rt = p.sem("p2_rt")
    s_pg = p.sem("p2_pg")
    s_pu = p.sem("p2_pu")
    s_si = p.sem("p2_si")
    s_gu = p.sem("p2_gu")
    s_cm = p.sem("p2_cm")
    s_dn = p.sem("p2_dn")
    s_ev = p.sem("p2_ev")
    s_ap = p.sem("ln_ap")

    emit_fences(p, ("tensor", "scalar", "vector", "gpsimd", "sync"), prev_done)

    first_call = not hasattr(st, "tr2")
    if first_call:
        st.tr2 = {"cast": 0}
        p.dma("gpsimd", st.gw16[:, :].rearrange("p (c e) -> p c e", e=NE),
              hb["gw"].rearrange("(c p) e -> p c e", p=128), inc=p.sem("p2_gwd"))
        p.dma("sync", st.gb32[:, :], ap_full(hb["gb"]), inc=p.sem("p2_gbd"))
        p.dma("sync", ap_full(st.csel), ap_full(hb["c_sel"]), inc=p.sem("p2_gbd"))
    s_gwd = p.sem("p2_gwd")
    s_gbd = p.sem("p2_gbd")

    if first_call:
        p.mm(P[3][:, 0:NE], bufs.or32[:, :], st.gb32[:, :], start=True, stop=True,
             waits=[(s_gbd, s_gbd.n)], inc=s_gl)
        p.act(st.gbb[:, :], P[3][:, 0:NE], AF.Copy, waits=[(s_gl, s_gl.n)], inc=s_sig)

    for grp in range(NGB):
        t0 = grp * G
        emit_ln(p, bufs, ln, x, t0, st.h16, 0, P[0], P[1], P[2], xw=[(s_ev, s_ev.n)])
        ln_done = s_ap.n

        glw = [(s_gwd, s_gwd.n), (s_ap, ln_done), (s_sig, s_sig.n), (s_rt, s_rt.n), (s_ev, s_ev.n)]
        for tt in range(4):
            for c in range(NCH):
                p.mm(P[3][:, tt * NE:(tt + 1) * NE],
                     st.h16[c][:, tt * 128:(tt + 1) * 128],
                     st.gw16[:, c * NE:(c + 1) * NE],
                     start=(c == 0), stop=(c == NCH - 1),
                     waits=glw if (tt == 0 and c == 0) else (),
                     inc=s_gl if c == NCH - 1 else None)
        gl_done = s_gl.n
        p.act(st.scores[0][:, :], P[3][:, 0:NE], AF.Sigmoid,
              waits=[(s_gl, gl_done), (s_rt, s_rt.n)], inc=s_sig)
        for tt in range(1, 4):
            p.act(st.scores[tt][:, :], P[3][:, tt * NE:(tt + 1) * NE], AF.Sigmoid, inc=s_sig)
        sig_done = s_sig.n
        for tt in range(4):
            p.emit("vector", lambda eh, t=tt: eh.tensor_add(
                st.scb[t][:, :], st.scores[t][:, :], st.gbb[:, :]),
                waits=[(s_sig, sig_done), (s_rt, s_rt.n)], inc=s_rt)
        pairs = [(0, 1), (0, 2), (0, 3), (1, 2), (1, 3), (2, 3)]
        for tt in range(4):
            v = st.scb[tt][:, :].rearrange("p (g i) -> p g i", i=4)
            for pi, (i, j) in enumerate(pairs):
                dst = st.gs[tt] if pi == 0 else st.gtmp[tt]
                p.emit("vector", lambda eh, d=dst, a=v[:, :, i], b2=v[:, :, j]:
                       eh.tensor_add(d[:, :], a, b2),
                       waits=[(s_rt, s_rt.n)], inc=s_rt)
                if pi > 0:
                    p.emit("vector", lambda eh, t=tt: eh.tensor_max(
                        st.gs[t][:, :], st.gs[t][:, :], st.gtmp[t][:, :]),
                        waits=[(s_rt, s_rt.n)], inc=s_rt)
        for tt in range(4):
            p.emit("vector", lambda eh, t=tt: eh.max(out=st.top8[t][:, :], in_=st.gs[t][:, :]),
                   waits=[(s_rt, s_rt.n)], inc=s_rt)
        for tt in range(4):
            p.emit("vector", lambda eh, t=tt: eh.tensor_scalar(
                st.gtmp[t][:, :], st.gs[t][:, :], st.top8[t][:, 3:4], None, op0=ALU.is_ge),
                waits=[(s_rt, s_rt.n)], inc=s_rt)
        for tt in range(4):
            gb = st.gtmp[tt][:, :, None].broadcast_to([128, 8, 4])
            p.emit("vector", lambda eh, t=tt, g=gb: eh.tensor_mul(
                st.masked[t][:, :].rearrange("p (g i) -> p g i", i=4),
                st.scb[t][:, :].rearrange("p (g i) -> p g i", i=4), g),
                waits=[(s_rt, s_rt.n)], inc=s_rt)
        for tt in range(4):
            p.emit("vector", lambda eh, t=tt: eh.max(out=st.top8[t][:, :], in_=st.masked[t][:, :]),
                   waits=[(s_rt, s_rt.n)], inc=s_rt)
        for tt in range(4):
            p.emit("vector", lambda eh, t=tt: eh.tensor_scalar(
                st.sel[t][:, :], st.masked[t][:, :], st.top8[t][:, 7:8], None, op0=ALU.is_ge),
                waits=[(s_rt, s_rt.n)], inc=s_rt)
        for tt in range(4):
            p.emit("vector", lambda eh, t=tt: eh.scalar_tensor_tensor(
                st.w8[t][:, :], st.scores[t][:, :], 1.0, st.sel[t][:, :],
                op0=ALU.mult, op1=ALU.mult, accum_out=st.sumw[t][:, :]),
                waits=[(s_rt, s_rt.n)], inc=s_rt)
        for tt in range(4):
            p.emit("vector", lambda eh, t=tt: eh.reciprocal(st.c2[t][:, :], st.sumw[t][:, :]),
                   waits=[(s_rt, s_rt.n)], inc=s_rt)
        for tt in range(4):
            p.emit("vector", lambda eh, t=tt: eh.tensor_scalar_mul(
                st.c2[t][:, :], st.c2[t][:, :], 2.5), waits=[(s_rt, s_rt.n)], inc=s_rt)
        for tt in range(4):
            p.emit("vector", lambda eh, t=tt: eh.tensor_scalar_mul(
                st.comb[t][:, :], st.w8[t][:, :], st.c2[t][:, :]),
                waits=[(s_rt, s_rt.n)], inc=s_rt)
        rt_done = s_rt.n
        if DBG and grp == 0 and first_call:
            sdb = p.sem("p2_dbg")
            for c in range(NCH):
                t = p.nc.dram_tensor(f"dbg_h{c}", [128, G], BF16, kind="ExternalOutput")
                p.dma("sync", ap_full(t), st.h16[c][:, :], waits=[(s_ap, ln_done)], inc=sdb)
            for t4 in range(4):
                t = p.nc.dram_tensor(f"dbg_cb{t4}", [128, NE], BF16, kind="ExternalOutput")
                p.dma("sync", ap_full(t), st.comb[t4][:, :], waits=[(s_rt, rt_done)], inc=sdb)
                t = p.nc.dram_tensor(f"dbg_sc{t4}", [128, NE], F32, kind="ExternalOutput")
                p.dma("sync", ap_full(t), st.scores[t4][:, :], waits=[(s_rt, rt_done)], inc=sdb)
        p3b = P[3][:, :].bitcast(BF16)
        for tt in range(4):
            p.transp(p3b[0:32, tt * 128:(tt + 1) * 128], st.comb[tt][:, :], bufs.id16[:, :],
                     waits=[(s_rt, rt_done), (s_sig, s_sig.n)] if tt == 0 else (),
                     inc=s_gl if tt == 3 else None)
        p.emit("vector", lambda eh: eh.tensor_copy(st.combT[:, :], p3b[0:32, 0:G]),
               waits=[(s_gl, s_gl.n)], inc=s_rt)
        combT_done = s_rt.n

        lo_evac_prev = s_ev.n
        n_i = 0
        for e in range(NEX):
            par = e % 2
            routed = e < NE
            if e < 2:
                ecast = 48 * (e + 1)
                p.dma("sync", st.wg16[e % 2][:, :], scr["wg3s"][e],
                      waits=[(s_cast, ecast), (s_dn, st.tr2.get("dn_prev", 0))], inc=s_wst)
                p.dma("sync", st.wu16[e % 2][:, :], scr["wu3s"][e], inc=s_wst)
                p.dma("sync", st.wd16[e % 2][:, :], scr["wd3s"][e], inc=s_wst)
                p.emit("sync", lambda eh: eh.nop(), waits=[(s_wst, s_wst.n)])
                st.tr2[f"wldgu{e}"] = s_wst.n
                st.tr2[f"wldd{e}"] = s_wst.n
            wld_e = max(st.tr2[f"wldgu{e}"], st.tr2[f"wldd{e}"])

            if routed:
                pc_ = P[6 + (e % 2)]
                cw = [(s_rt, combT_done), (s_ev, st.tr2.get(f"hiev{e % 2}", 0)),
                      (s_gbd, s_gbd.n)]
                p.mm(pc_[:, :], st.csel[:, e * 128:(e + 1) * 128], st.combT[:, :],
                     start=True, stop=True, waits=cw, inc=s_gl)
                p.act(st.c_sb[par][:, :], pc_[:, :], AF.Copy,
                      waits=[(s_gl, s_gl.n), (s_cm, st.tr2.get(f"cmu{par}", 0))], inc=s_sig)
                st.tr2[f"csb{par}"] = s_sig.n

            for i in range(4):
                n_i += 1
                for c in range(NCH):
                    wmm = []
                    if c == 0:
                        wmm.append((s_wst, wld_e))
                        if i == 0 and e == 0:
                            wmm.append((s_ap, ln_done))
                        wmm.append((s_si, s_si.n))
                    p.mm(P[4][:, :], st.wg16[par][:, c * 512 + i * 128:c * 512 + (i + 1) * 128],
                         st.h16[c][:, :], start=(c == 0), stop=(c == NCH - 1),
                         waits=wmm, inc=s_pg if c == NCH - 1 else None)
                for c in range(NCH):
                    wmm = []
                    if c == 0:
                        wmm.append((s_gu, s_gu.n))
                    p.mm(P[5][:, :], st.wu16[par][:, c * 512 + i * 128:c * 512 + (i + 1) * 128],
                         st.h16[c][:, :], start=(c == 0), stop=(c == NCH - 1),
                         waits=wmm, inc=s_pu if c == NCH - 1 else None)
                sq = st.silu[n_i % 2]
                p.act(sq[:, :], P[4][:, :], AF.Silu,
                      waits=[(s_pg, s_pg.n), (s_gu, max(s_gu.n - 1, 0))], inc=s_si)
                gub = st.gu[par][i]
                p.emit("vector", lambda eh, g=gub, sq=sq: eh.tensor_mul(
                    g[:, :], sq[:, :], P[5][:, :]),
                    waits=[(s_pu, s_pu.n), (s_si, s_si.n),
                           (s_dn, st.tr2.get("dn_prev2", 0))], inc=s_gu)
                st.tr2[f"gu{par}{i}"] = s_gu.n
                if routed:
                    p.emit("gpsimd", lambda eh, g=gub, cc=st.c_sb[par]: eh.tensor_mul(
                        g[:, :], g[:, :], cc[:, :]),
                        waits=[(s_gu, s_gu.n), (s_sig, st.tr2[f"csb{par}"])], inc=s_cm)
                st.tr2[f"cm{par}{i}"] = s_cm.n
            st.tr2[f"cmu{par}"] = s_cm.n
            gu_ready = (s_cm, s_cm.n) if routed else (s_gu, s_gu.n)

            for i in range(4):
                gr_i = (s_cm, st.tr2[f"cm{par}{i}"]) if routed else (s_gu, st.tr2[f"gu{par}{i}"])
                for hc in range(4):
                    wmm = []
                    if hc == 0:
                        wmm.append(gr_i)
                        if e == 0 and i == 0:
                            wmm.append((s_ev, lo_evac_prev))
                    p.mm(P[hc][:, :], st.wd16[par][:, i * 1024 + hc * 128:i * 1024 + (hc + 1) * 128],
                         st.gu[par][i][:, :], start=(e == 0 and i == 0),
                         stop=(e == NEX - 1 and i == 3), waits=wmm,
                         inc=s_dn if (e == NEX - 1 and hc == 3 and i == 3) else None)
            if e % 2 == 1:
                for hc in range(4, 8):
                    bank = P[6 + (hc % 2)]
                    wmm = []
                    if hc in (4, 5):
                        wmm.append((s_sig, s_sig.n))
                    else:
                        wmm.append((s_ev, s_ev.n))
                    for ee in (e - 1, e):
                        pp = ee % 2
                        gur = (s_cm, st.tr2[f"cmu{pp}"]) if ee < NE else (s_gu, s_gu.n)
                        for i in range(4):
                            w2 = list(wmm) if (ee == e - 1 and i == 0) else []
                            if ee == e - 1 and i == 0:
                                w2.append(gur)
                            p.mm(bank[:, :], st.wd16[pp][:, i * 1024 + hc * 128:i * 1024 + (hc + 1) * 128],
                                 st.gu[pp][i][:, :], start=(ee == e - 1 and i == 0),
                                 stop=(ee == e and i == 3), waits=w2,
                                 inc=s_dn if (ee == e and i == 3) else None)
                    dn_now = s_dn.n
                    acix = hc - 4
                    if e == 1:
                        p.emit("vector", lambda eh, a=st.accum[acix], xc=x[4 + acix], t=t0,
                               b2=bank: eh.tensor_add(a[:, :], xc[:, t:t + G], b2[:, :]),
                               waits=[(s_dn, dn_now)], inc=s_ev)
                    else:
                        p.emit("vector", lambda eh, a=st.accum[acix], b2=bank: eh.tensor_add(
                            a[:, :], a[:, :], b2[:, :]),
                            waits=[(s_dn, dn_now)], inc=s_ev)
                    st.tr2[f"hiev{hc % 2}"] = s_ev.n
                st.tr2["dn_prev2"] = s_dn.n
            en = e + 2
            if en < NEX:
                ecast = 48 * (en + 1)
                p.dma("sync", st.wg16[par][:, :], scr["wg3s"][en],
                      waits=[(s_cast, ecast), (s_pu, s_pu.n)], inc=s_wst)
                p.dma("sync", st.wu16[par][:, :], scr["wu3s"][en], inc=s_wst)
                p.emit("sync", lambda eh: eh.nop(), waits=[(s_wst, s_wst.n)])
                st.tr2[f"wldgu{en}"] = s_wst.n
            if e % 2 == 1:
                for en2 in (e + 1, e + 2):
                    if en2 >= NEX:
                        continue
                    p.dma("sync", st.wd16[en2 % 2][:, :], scr["wd3s"][en2],
                          waits=[(s_cast, 48 * (en2 + 1)), (s_dn, s_dn.n)], inc=s_wst)
                p.emit("sync", lambda eh: eh.nop(), waits=[(s_wst, s_wst.n)])
                if e + 1 < NEX:
                    st.tr2[f"wldd{e + 1}"] = s_wst.n
                if e + 2 < NEX:
                    st.tr2[f"wldd{e + 2}"] = s_wst.n
        st.tr2["dn_prev"] = s_dn.n

        lo_done = s_dn.n
        for c in range(4):
            xc = x[c]
            p.emit("vector", lambda eh, xc=xc, t=t0, b2=P[c]: eh.tensor_add(
                xc[:, t:t + G], xc[:, t:t + G], b2[:, :]),
                waits=[(s_dn, lo_done)] if c == 0 else (), inc=s_ev)
        for c in range(4):
            xc = x[4 + c]
            p.emit("vector", lambda eh, xc=xc, t=t0, a=st.accum[c]: eh.tensor_copy(
                xc[:, t:t + G], a[:, :]),
                waits=[(s_ev, s_ev.n)] if c == 0 else (), inc=s_ev)
        st.tr2["si0"] = s_si.n
    return [(s_ev, s_ev.n), (s_wst, s_wst.n), (s_gwd, s_gwd.n), (s_gbd, s_gbd.n)]



class P3State:
    def __init__(self, nc, ctx, tag=""):
        sb = lambda n, s, dt: ctx.enter_context(nc.sbuf_tensor(n + tag, s, dt))
        self.h16 = [sb(f"p3_h{c}", [128, T], BF16) for c in range(NCH)]
        self.logits = [sb(f"p3_lg{c}", [128, T], BF16) for c in range(NCH)]
        self.vaug = [sb(f"p3_va{k}", [128, 2 * (HD + 1)], BF16) for k in range(16)]
        self.wqkv16 = sb("p3_wqkv", [128, 3072], BF16)
        self.wp16 = sb("p3_wp", [128, 1024], BF16)
        self.q = sb("p3_q", [128, T], BF16)
        self.k = sb("p3_k", [128, T], BF16)
        self.qlm = sb("p3_qlm", [128, HD], BF16)
        self.klm = sb("p3_klm", [128, HD], BF16)
        self.lmf = sb("p3_lmf", [128, HD], F32)
        self.mqa = sb("p3_mqa", [128, 1], F32)
        self.mka = sb("p3_mka", [128, 1], F32)
        self.mprod = sb("p3_mprod", [128, 1], F32)
        self.cb1 = sb("p3_cb1", [128, 2], F32)
        self.cb3 = sb("p3_cb3", [128, 2], F32)
        self.nt3 = [sb(f"p3_nt3{h}", [128, 1024], BF16) for h in range(2)]
        self.nt1 = [sb(f"p3_nt1{h}", [64, G], BF16) for h in range(2)]
        self.k3vn = [sb(f"p3_k3vn{h}", [64, HD], BF16) for h in range(2)]
        self.z3i = [sb(f"p3_z3i{h}", [64, 1], F32) for h in range(2)]
        self.K2 = [sb(f"p3_K2{h}", [64, 64], BF16) for h in range(2)]
        self.K2T = [sb(f"p3_K2T{h}", [64, 64], BF16) for h in range(2)]
        self.dcol = sb("p3_dcol", [64, 2], F32)
        self.V = [sb(f"p3_V{h}", [64, 64], BF16) for h in range(2)]
        self.VT = [sb(f"p3_VT{h}", [64, 64], BF16) for h in range(2)]
        self.Aa = sb("p3_Aa", [64, 64], BF16)
        self.Cc = sb("p3_Cc", [64, 64], BF16)
        self.Ee = sb("p3_Ee", [64, 64], BF16)
        self.KVT = sb("p3_KVT", [64, 64], BF16)
        self.i325 = sb("p3_i325", [64, 64], F32)
        self.id32 = sb("p3_id32", [128, 128], F32)
        self.w2 = sb("p3_w2", [64, 128], BF16)
        self.zrow = sb("p3_zrow", [1, 2 * G], F32)
        self.zb16 = sb("p3_zb16", [128, G], BF16)


def emit_l1_attn_batch(p, bufs, st, ln, hb, prev_done):
    x = bufs.x
    P = bufs.P
    s_wdma = p.sem("p3_wdma")
    s_pe = p.sem("p3_pe")
    s_qev = p.sem("p3_qev")
    s_exp = p.sem("p3_exp")
    s_dve = p.sem("p3_dve")
    s_lgt = p.sem("p3_lgt")
    s_res = p.sem("p3_res")
    s_ap = p.sem("ln_ap")

    emit_fences(p, ("tensor", "scalar", "vector", "gpsimd", "sync"), prev_done)
    p.dma("sync", ap_full(st.id32), ap_full(hb["c_id32"]), inc=p.sem("p3_idd"))
    s_idd = p.sem("p3_idd")
    p.emit("vector", lambda eh: eh.tensor_scalar_mul(
        st.i325[:, :], st.id32[0:64, 0:64], 3.25),
        waits=[(s_idd, s_idd.n)], inc=s_dve)
    for kt in range(16):
        od = st.vaug[kt].rearrange("p (h u) -> p h u", u=HD + 1)[:, :, HD:HD + 1]
        p.emit("gpsimd", lambda eh, d2=od: eh.memset(d2, 1.0), inc=s_dve)

    for g in range(NGB):
        emit_ln(p, bufs, ln, x, g * G, st.h16, g * G, P[0], P[1], P[2])
    ln_done = s_ap.n

    tr = {"qk_pe": 0, "exp_end": 0, "k3_pe": 0, "lgt": 0, "k3vn_dve": 0}
    for pr in range(NPAIR):
        p.dma("gpsimd", st.wqkv16[:, :], hb["w1qkv"][pr],
              waits=[(s_pe, tr["qk_pe"])], inc=s_wdma)
        w_ld = s_wdma.n

        prev_exp_end = tr["exp_end"]
        for kt in range(16):
            pv = P[2 + kt % 2]
            for c in range(NCH):
                wmm = []
                if c == 0:
                    wmm.append((s_wdma, w_ld))
                    if kt < 2:
                        wmm.append((s_ap, ln_done))
                        wmm.append((s_exp, prev_exp_end))
                        wmm.append((s_dve, tr["k3vn_dve"]))
                    else:
                        wmm.append((s_dve, s_dve.n - 1))
                p.mm(pv[:, 0:128], st.h16[c][:, kt * 128:(kt + 1) * 128],
                     st.wqkv16[:, 2048 + c * 128:2048 + (c + 1) * 128],
                     start=(c == 0), stop=(c == NCH - 1), waits=wmm,
                     inc=s_pe if c == NCH - 1 else None)
            vdst = st.vaug[kt].rearrange("p (h u) -> p h u", u=HD + 1)[:, :, 0:HD]
            vsrc = pv[:, 0:128].rearrange("p (h u) -> p h u", u=HD)
            p.emit("vector", lambda eh, d2=vdst, s2=vsrc: eh.tensor_copy(d2, s2),
                   waits=[(s_pe, s_pe.n), (s_pe, tr["k3_pe"])], inc=s_dve)
        v_done = s_dve.n

        for which, wt, dst, scale in (("k", st.wqkv16[:, 1024:2048], st.k, 1.0),
                                      ("q", st.wqkv16[:, 0:1024], st.q, 0.125)):
            for g in range(NGB):
                pq = P[g % 2]
                for c in range(NCH):
                    wmm = []
                    if c == 0:
                        wmm.append((s_wdma, w_ld))
                        wmm.append((s_qev, max(s_qev.n - 1, 0)))
                        wmm.append((s_exp, prev_exp_end))
                    p.mm(pq[:, :], wt[:, c * 128:(c + 1) * 128],
                         st.h16[c][:, g * G:(g + 1) * G],
                         start=(c == 0), stop=(c == NCH - 1), waits=wmm,
                         inc=s_pe if c == NCH - 1 else None)
                p.act(dst[:, g * G:(g + 1) * G], pq[:, :], AF.Copy, scale=scale,
                      waits=[(s_pe, s_pe.n)], inc=s_qev)
        tr["qk_pe"] = s_pe.n
        qk_done = s_qev.n

        p.emit("vector", lambda eh: eh.reduce_sum(
            st.lmf[:, :], st.q[:, :].rearrange("p (l w) -> p l w", w=32), axis=AX.X),
            waits=[(s_qev, qk_done), (s_dve, s_dve.n)], inc=s_dve)
        p.act(st.qlm[:, :], st.lmf[:, :], AF.Copy, scale=1.0 / 32,
              waits=[(s_dve, s_dve.n)], inc=s_qev)
        p.emit("vector", lambda eh: eh.reduce_sum(
            st.lmf[:, :], st.k[:, :].rearrange("p (l w) -> p l w", w=32), axis=AX.X),
            waits=[(s_qev, s_qev.n), (s_dve, s_dve.n)], inc=s_dve)
        p.act(st.klm[:, :], st.lmf[:, :], AF.Copy, scale=1.0 / 32,
              waits=[(s_dve, s_dve.n)], inc=s_qev)
        lm_done = s_qev.n

        def bound(src_a, src_b, dstcb):
            p.emit("vector", lambda eh: eh.reduce_max(
                st.mqa[:, :], src_a, axis=AX.X, apply_absolute_value=True),
                waits=[(s_qev, s_qev.n), (s_dve, s_dve.n)], inc=s_dve)
            p.emit("vector", lambda eh: eh.reduce_max(
                st.mka[:, :], src_b, axis=AX.X, apply_absolute_value=True),
                waits=[(s_dve, s_dve.n)], inc=s_dve)
            p.emit("vector", lambda eh: eh.tensor_mul(
                st.mprod[:, :], st.mqa[:, :], st.mka[:, :]),
                waits=[(s_dve, s_dve.n)], inc=s_dve)
            p.mm(P[4][0:1, 0:2], st.mprod[:, :], bufs.hm[:, :], start=True, stop=True,
                 waits=[(s_dve, s_dve.n), (s_qev, s_qev.n), (s_exp, s_exp.n)], inc=s_pe)
            p.act(dstcb[0:1, 0:2], P[4][0:1, 0:2], AF.Copy, scale=-1.0,
                  waits=[(s_pe, s_pe.n)], inc=s_qev)
            for h in range(2):
                p.mm(P[4][:, 4 + h:5 + h], bufs.or32[:, :], dstcb[0:1, h:h + 1],
                     start=True, stop=True, waits=[(s_qev, s_qev.n)],
                     inc=s_pe if h == 1 else None)
            p.emit("vector", lambda eh, d=dstcb: eh.tensor_copy(d[:, 0:2], P[4][:, 4:6]),
                   waits=[(s_pe, s_pe.n)], inc=s_dve)
        bound(st.q[:, :], st.klm[:, :], st.cb1)
        bound(st.qlm[:, :], st.k[:, :], st.cb3)
        cb_done = s_dve.n

        if P3S < 1:
            tr["exp_end"] = s_exp.n
            tr["k3vn_dve"] = s_dve.n
            continue
        for h in range(2):
            hr = slice(64 * h, 64 * h + 64)
            tp = (64 * h, 0)
            for half in range(2):
                pb = P[half]
                for j in range(8):
                    kc = half * 8 + j
                    wmm = []
                    if h == 0 and half == 0 and j == 0:
                        wmm.append((s_qev, lm_done))
                        wmm.append((s_dve, cb_done))
                    if h == 1 and half == 0 and j == 0:
                        wmm.append((s_exp, s_exp.n))
                    p.mm(pb[:, j * 64:(j + 1) * 64], st.k[hr, kc * 128:(kc + 1) * 128],
                         st.qlm[hr, :], start=True, stop=True, tile=tp, waits=wmm,
                         inc=s_pe if j == 7 else None)
                p.act(st.nt3[h][:, half * 512:(half + 1) * 512], pb[:, :], AF.Exp,
                      bias=st.cb3[:, h:h + 1], waits=[(s_pe, s_pe.n), (s_dve, cb_done)],
                      inc=s_exp)
        tr["exp_end"] = s_exp.n
        nt3_done = s_exp.n

        for h in range(2):
            pk = P[2 + h]
            for kc in range(16):
                wmm = []
                if kc == 0:
                    wmm.append((s_exp, nt3_done))
                    wmm.append((s_dve, v_done))
                p.mm(pk[0:64, 0:65],
                     st.nt3[h][:, kc * 64:(kc + 1) * 64],
                     st.vaug[kc][:, h * 65:(h + 1) * 65],
                     start=(kc == 0), stop=(kc == 15), waits=wmm,
                     inc=s_pe if kc == 15 else None)
        k3_done = s_pe.n
        p.emit("vector", lambda eh: eh.reciprocal(st.z3i[0][:, :], P[2][0:64, 64:65]),
               waits=[(s_pe, k3_done)], inc=s_dve)
        p.emit("vector", lambda eh: eh.reciprocal(st.z3i[1][:, :], P[3][0:64, 64:65]),
               inc=s_dve)
        p.emit("vector", lambda eh: eh.tensor_scalar_mul(
            st.k3vn[0][:, :], P[2][0:64, 0:64], st.z3i[0][:, 0:1]),
            waits=[(s_dve, s_dve.n)], inc=s_dve)
        p.emit("vector", lambda eh: eh.tensor_scalar_mul(
            st.k3vn[1][:, :], P[3][0:64, 0:64], st.z3i[1][:, 0:1]),
            inc=s_dve)
        k3vn_done = s_dve.n
        tr["k3_pe"] = k3_done
        tr["k3vn_dve"] = k3vn_done

        if P3S < 2:
            continue
        for h in range(2):
            hr = slice(64 * h, 64 * h + 64)
            tp = (64 * h, 0)
            p.mm(P[4][0:64, 0:64], st.qlm[hr, :], st.klm[hr, :], start=True, stop=True,
                 tile=tp, waits=[(s_dve, s_dve.n), (s_qev, s_qev.n)], inc=s_pe)
            sc2 = s_pe.n
            p.emit("vector", lambda eh: eh.reduce_max(
                st.mqa[0:64, :], P[4][0:64, 0:64], axis=AX.X, negate=True),
                waits=[(s_pe, sc2)], inc=s_dve)
            p.act(st.K2[h][:, :], P[4][0:64, 0:64], AF.Exp, bias=st.mqa[0:64, 0:1],
                  accum=st.mka[0:64, 0:1], waits=[(s_dve, s_dve.n)], inc=s_qev)
            p.emit("vector", lambda eh, h=h: eh.reciprocal(st.mprod[0:64, :], st.mka[0:64, :]),
                   waits=[(s_qev, s_qev.n)], inc=s_dve)
            p.emit("vector", lambda eh: eh.tensor_copy(st.lmf[0:64, 0:1], st.mka[0:64, 0:1]),
                   inc=s_dve)
            p.emit("vector", lambda eh, h=h: eh.tensor_scalar_mul(
                st.K2[h][:, :], st.K2[h][:, :], st.mprod[0:64, 0:1]),
                waits=[(s_dve, s_dve.n)], inc=s_dve)
            k2_done = s_dve.n
            if P3S2 < 1:
                continue
            p.mm(P[4][0:64, 64:128], st.K2[h][:, :], bufs.id16[0:64, 0:64],
                 start=True, stop=True, waits=[(s_dve, k2_done)], inc=s_pe)
            p.emit("vector", lambda eh, h=h: eh.tensor_copy(st.K2T[h][:, :], P[4][0:64, 64:128]),
                   waits=[(s_pe, s_pe.n)], inc=s_dve)
            if P3S2 < 2:
                continue
            p.mm(P[4][0:1, 128:192], bufs.oc16[0:64, :], st.K2[h][:, :], start=True, stop=True,
                 waits=[(s_dve, s_dve.n)], inc=s_pe)
            p.emit("vector", lambda eh: eh.reduce_max(
                st.mqa[0:1, :], P[4][0:1, 128:192], axis=AX.X),
                waits=[(s_pe, s_pe.n)], inc=s_dve)
            p.emit("vector", lambda eh: eh.reciprocal(st.mka[0:1, :], st.mqa[0:1, :]),
                   waits=[(s_dve, s_dve.n)], inc=s_dve)
            p.mm(P[4][0:64, 192:193], bufs.or32[:, 0:64], st.mka[0:1, 0:1],
                 start=True, stop=True, waits=[(s_dve, s_dve.n)], inc=s_pe)
            p.emit("vector", lambda eh, h=h: eh.tensor_copy(st.dcol[:, h:h + 1], P[4][0:64, 192:193]),
                   waits=[(s_pe, s_pe.n)], inc=s_dve)
            p.emit("vector", lambda eh: eh.tensor_copy(st.lmf[0:64, 1:2], st.mka[0:64, 0:1]),
                   inc=s_dve)
            p.emit("vector", lambda eh, h=h: eh.tensor_scalar_mul(
                st.V[h][:, :], st.K2T[h][:, :], st.dcol[:, h:h + 1]),
                waits=[(s_dve, s_dve.n)], inc=s_dve)
            p.emit("vector", lambda eh, h=h: eh.tensor_scalar_mul(
                st.VT[h][:, :], st.K2[h][:, :], st.dcol[:, h:h + 1]),
                inc=s_dve)
            v0_done = s_dve.n
            if P3S2 < 3:
                continue
            KV = P[4][0:64, 256:320]
            KVT = P[4][0:64, 320:384]
            BB = P[4][0:64, 384:448]
            DD = P[4][0:64, 448:512]
            for it in range(P3IT):
                p.mm(KV, st.K2T[h][:, :], st.V[h][:, :], start=True, stop=True,
                     waits=[(s_dve, s_dve.n)], inc=s_pe)
                p.mm(KVT, st.V[h][:, :], st.K2T[h][:, :], start=True, stop=True, inc=s_pe)
                pe_kv = s_pe.n
                p.emit("vector", lambda eh: eh.tensor_copy(st.KVT[:, :], KVT),
                       waits=[(s_pe, pe_kv)], inc=s_dve)
                p.emit("vector", lambda eh: eh.scalar_tensor_tensor(
                    st.Aa[:, :], st.id32[0:64, 0:64], 7.0, KV, op0=ALU.mult, op1=ALU.subtract),
                    waits=[(s_dve, s_dve.n)], inc=s_dve)
                a_done = s_dve.n
                if P3IP < 1:
                    continue
                p.mm(BB, st.KVT[:, :], st.Aa[:, :], start=True, stop=True,
                     waits=[(s_dve, a_done)], inc=s_pe)
                p.emit("vector", lambda eh: eh.scalar_tensor_tensor(
                    st.Cc[:, :], st.id32[0:64, 0:64], 15.0, BB, op0=ALU.mult, op1=ALU.subtract),
                    waits=[(s_pe, s_pe.n)], inc=s_dve)
                c_done = s_dve.n
                if P3IP < 2:
                    continue
                p.mm(DD, st.KVT[:, :], st.Cc[:, :], start=True, stop=True,
                     waits=[(s_dve, c_done)], inc=s_pe)
                p.emit("vector", lambda eh: eh.scalar_tensor_tensor(
                    st.Ee[:, :], DD, -0.25, st.i325[:, :], op0=ALU.mult, op1=ALU.add),
                    waits=[(s_pe, s_pe.n)], inc=s_dve)
                e_done = s_dve.n
                if P3IP < 3:
                    continue
                p.mm(KV, st.VT[h][:, :], st.Ee[:, :], start=True, stop=True,
                     waits=[(s_dve, e_done)], inc=s_pe)
                p.mm(KVT, st.Ee[:, :], st.VT[h][:, :], start=True, stop=True, inc=s_pe)
                pe_v = s_pe.n
                p.emit("vector", lambda eh, h=h: eh.tensor_copy(st.V[h][:, :], KV),
                       waits=[(s_pe, pe_v)], inc=s_dve)
                p.emit("vector", lambda eh, h=h: eh.tensor_copy(st.VT[h][:, :], KVT),
                       inc=s_dve)
            w2rhs = st.k3vn[h][:, :] if P3W2 != 1 else bufs.id16[0:64, 0:64]
            w2lhs = st.VT[h][:, :] if P3W2 != 2 else st.K2[h][:, :]
            p.mm(P[4][0:64, 256:320], w2lhs, w2rhs,
                 start=True, stop=True, waits=[(s_dve, s_dve.n)], inc=s_pe)
            if P3W2 != 3:
                p.emit("vector", lambda eh, h=h: eh.tensor_copy(
                    st.w2[:, 64 * h:64 * h + 64], P[4][0:64, 256:320]),
                    waits=[(s_pe, s_pe.n)], inc=s_dve)
        w2_done = s_dve.n

        if P3S < 3:
            continue
        for g in range(NGB):
            gcols = slice(g * G, (g + 1) * G)
            for h in range(2):
                hr = slice(64 * h, 64 * h + 64)
                tp = (64 * h, 0)
                wmm = [(s_dve, w2_done)]
                if g > 0 or h > 0:
                    wmm.append((s_exp, s_exp.n))
                if g > 0 and h == 0:
                    wmm.append((s_lgt, s_lgt.n))
                    wmm.append((s_dve, s_dve.n))
                p.mm(P[5][0:64, :], st.klm[hr, :], st.q[hr, gcols], start=True, stop=True,
                     tile=tp, waits=wmm, inc=s_pe)
                p.act(st.nt1[h][:, :], P[5][0:64, :], AF.Exp, bias=st.cb1[0:64, h:h + 1],
                      waits=[(s_pe, s_pe.n)], inc=s_exp)
                zdst = P[6][0:1, :] if h == 0 else P[5][0:1, :]
                p.mm(zdst, bufs.oc16[0:64, :], st.nt1[h][:, :],
                     start=True, stop=True, tile=(0, 0),
                     waits=[(s_exp, s_exp.n), (s_qev, s_qev.n)], inc=None)
                p.mm(P[7][64 * h:64 * h + 64, :], st.w2[:, 64 * h:64 * h + 64],
                     st.nt1[h][:, :], start=True, stop=True, tile=(0, 64 * h),
                     inc=s_pe)
            sv_done = s_pe.n
            p.emit("vector", lambda eh: eh.reciprocal(st.zrow[0:1, 0:G], P[6][0:1, 0:G]),
                   waits=[(s_pe, sv_done)], inc=s_dve)
            p.emit("vector", lambda eh: eh.reciprocal(st.zrow[0:1, G:2 * G], P[5][0:1, 0:G]),
                   inc=s_dve)
            zi_done = s_dve.n
            p.mm(P[6][0:64, :], bufs.or32[:, 0:64], st.zrow[0:1, 0:G],
                 start=True, stop=True, tile=(0, 0), waits=[(s_dve, zi_done)])
            p.mm(P[6][64:128, :], bufs.or32[:, 0:64], st.zrow[0:1, G:2 * G],
                 start=True, stop=True, tile=(0, 64), inc=s_pe)
            p.act(st.zb16[:, :], P[6][:, :], AF.Copy, waits=[(s_pe, s_pe.n)], inc=s_qev)
            lt = st.logits[pr]
            p.emit("vector", lambda eh, lt=lt, lc=gcols: eh.tensor_mul(
                lt[0:64, lc], P[7][0:64, :], st.zb16[0:64, :]),
                waits=[(s_qev, s_qev.n)], inc=s_lgt)
            p.emit("vector", lambda eh, lt=lt, lc=gcols: eh.tensor_mul(
                lt[64:128, lc], P[7][64:128, :], st.zb16[64:128, :]),
                inc=s_lgt)
    lg_done = s_lgt.n

    s_wp = p.sem("p3_wp")
    if P3S < 4:
        return [(s_pe, s_pe.n), (s_dve, s_dve.n), (s_wdma, s_wdma.n), (s_idd, s_idd.n)]
    wp_pe = 0
    for oc in range(NCH):
        p.dma("gpsimd", st.wp16[:, :], hb["w1p"][oc],
              waits=[(s_pe, wp_pe)], inc=s_wp)
        wp_ld = s_wp.n
        for g in range(NGB):
            po = P[0 + g % 2]
            for pc in range(NCH):
                wmm = []
                if pc == 0:
                    wmm.append((s_lgt, lg_done))
                    wmm.append((s_wp, wp_ld))
                    wmm.append((s_res, max(s_res.n - 1, 0)))
                    if oc == 0 and g == 0:
                        wmm.append((s_exp, s_exp.n))
                p.mm(po[:, :], st.wp16[:, pc * 128:(pc + 1) * 128],
                     st.logits[pc][:, g * G:(g + 1) * G],
                     start=(pc == 0), stop=(pc == NCH - 1), waits=wmm,
                     inc=s_pe if pc == NCH - 1 else None)
            wp_pe = s_pe.n
            xa = x[oc][:, g * G:(g + 1) * G]
            p.emit("vector", lambda eh, po=po, xa=xa: eh.tensor_add(xa, xa, po[:, :]),
                   waits=[(s_pe, s_pe.n)], inc=s_res)
    return [(s_res, s_res.n), (s_wdma, s_wdma.n), (s_wp, s_wp.n), (s_idd, s_idd.n)]



class P4State:
    def __init__(self, nc, ctx, tag=""):
        sb = lambda n, s, dt: ctx.enter_context(nc.sbuf_tensor(n + tag, s, dt))
        self.h16 = [sb(f"p4_h{c}", [128, G], BF16) for c in range(NCH)]
        self.w1in16 = [sb(f"p4_wi{c}", [128, 2048], BF16) for c in range(NCH)]
        self.b1in16 = sb("p4_b1", [1, 2048], BF16)
        self.w1o16 = sb("p4_wo", [128, 2048], BF16)
        self.dec = sb("p4_dec", [128, 2048], BF16)
        self.z16 = sb("p4_z16", [128, 2048], BF16)
        self.vmask = sb("p4_vm", [128, 2048], BF16)
        self.mk16 = sb("p4_mk", [128, 2048], BF16)
        self.tscr = sb("p4_ts", [128, 512], BF16)
        self.mkT = sb("p4_mkT", [128, 16 * G], BF16)
        self.outst = [sb(f"p4_os{i}", [128, G], F32) for i in range(2)]


def emit_l1_ffn_batch(p, bufs, st, ln, hb, ydram, b, prev_done):
    x = bufs.x
    P = bufs.P
    PB0 = bufs.PB[0]
    PB1b = bufs.PB[1][:, :].bitcast(BF16)
    s_wdma = p.sem("p4_wdma")
    s_pe = p.sem("p4_pe")
    s_po = p.sem("p4_po")
    s_dve = p.sem("p4_dve")
    s_pool = p.sem("p4_pool")
    s_act = p.sem("p4_act")
    s_ev = p.sem("p4_ev")
    s_yst = p.sem("p4_yst")
    s_ap = p.sem("ln_ap")

    emit_fences(p, ("tensor", "scalar", "vector", "gpsimd", "sync"), prev_done)

    for c in range(NCH):
        p.dma("gpsimd", st.w1in16[c][:, :], hb["w1in"][c * 128:(c + 1) * 128, :], inc=s_wdma)
    p.dma("gpsimd", st.b1in16[:, :], ap_full(hb["b1in"]), inc=s_wdma)
    wi_ld = s_wdma.n

    vm3 = st.vmask[:, 0:2040].rearrange("p (a n) -> p a n", n=255)
    dc3 = st.dec[:, 0:2040].rearrange("p (a n) -> p a n", n=255)

    for tg in range(NGB):
        t0 = tg * G
        emit_ln(p, bufs, ln, x, t0, st.h16, 0, P[0], P[1], P[2],
                xw=[(s_dve, s_dve.n), (s_ev, s_ev.n)])
        ln_done = s_ap.n
        for tt in range(4):
            for sl in range(4):
                for c in range(NCH):
                    wmm = []
                    if sl == 0 and c == 0:
                        wmm.append((s_ap, ln_done))
                        wmm.append((s_wdma, wi_ld))
                        wmm.append((s_dve, s_dve.n))
                    p.mm(P[sl][:, :], st.h16[c][:, tt * 128:(tt + 1) * 128],
                         st.w1in16[c][:, sl * G:(sl + 1) * G],
                         start=(c == 0), stop=False, waits=wmm)
                p.mm(P[sl][:, :], bufs.or16[:, :], st.b1in16[:, sl * G:(sl + 1) * G],
                     start=False, stop=True, inc=s_pe if sl == 3 else None)
            mm_done = s_pe.n
            p.emit("vector", lambda eh: eh.tensor_scalar(
                st.dec[:, 0:2040], PB0[:, 0:2040], 0.0, None, op0=ALU.is_gt),
                waits=[(s_pe, mm_done), (s_pool, s_pool.n)], inc=s_dve)
            p.emit("vector", lambda eh: eh.tensor_copy(st.z16[:, :], PB0[:, :]),
                   waits=[(s_dve, s_dve.n), (s_pool, s_pool.n)], inc=s_dve)
            z_done = s_dve.n
            p.emit("gpsimd", lambda eh: eh.memset(st.vmask[:, :], 0.0),
                   waits=[(s_act, s_act.n), (s_pool, s_pool.n)], inc=s_pool)
            p.emit("gpsimd", lambda eh: eh.memset(vm3[:, :, 0:1], 1.0),
                   waits=[(s_pool, s_pool.n)], inc=s_pool)
            for d in range(7):
                a, bnd = 2 ** d - 1, 2 ** (d + 1) - 1
                cnd = 2 ** (d + 2) - 1
                lvl = vm3[:, :, a:bnd]
                dec_d = dc3[:, :, a:bnd]
                nxt = vm3[:, :, bnd:cnd].rearrange("p a (i two) -> p a i two", two=2)
                t0s = st.tscr[:, :].rearrange("p (a n) -> p a n", n=64)[:, :, 0:2 ** d]
                p.emit("gpsimd", lambda eh, t=t0s, l=lvl, dd=dec_d: eh.tensor_mul(t, l, dd),
                       waits=[(s_pool, s_pool.n), (s_dve, z_done)], inc=s_pool)
                p.emit("gpsimd", lambda eh, n=nxt, l=lvl, t=t0s: eh.tensor_sub(
                    n[:, :, :, 0], l, t), waits=[(s_pool, s_pool.n)], inc=s_pool)
                p.emit("gpsimd", lambda eh, n=nxt, t=t0s: eh.tensor_copy(n[:, :, :, 1], t),
                       waits=[(s_pool, s_pool.n)], inc=s_pool)
            p.emit("gpsimd", lambda eh: eh.tensor_mul(st.mk16[:, :], st.z16[:, :], st.vmask[:, :]),
                   waits=[(s_pool, s_pool.n), (s_act, s_act.n), (s_pe, s_pe.n)], inc=s_pool)
            p.act(st.mk16[:, :], st.mk16[:, :], AF.Silu,
                  waits=[(s_pool, s_pool.n)], inc=s_act)
            silu_done = s_act.n
            for wc in range(16):
                wmm = [(s_act, silu_done)]
                if wc == 0:
                    wmm.append((s_dve, s_dve.n))
                    wmm.append((s_po, s_po.n))
                p.transp(PB1b[:, wc * 128:(wc + 1) * 128], st.mk16[:, wc * 128:(wc + 1) * 128],
                         bufs.id16[:, :], waits=wmm, inc=s_pe if wc == 15 else None)
            tr_done = s_pe.n
            dstT = st.mkT[:, :].rearrange("p (w q) -> p w q", q=G)[:, :, tt * 128:(tt + 1) * 128]
            srcT = PB1b[:, 0:2048].rearrange("p (w j) -> p w j", j=128)
            p.emit("vector", lambda eh, d2=dstT, s2=srcT: eh.tensor_copy(d2, s2),
                   waits=[(s_pe, tr_done)], inc=s_dve)
        mkT_done = s_dve.n

        s_wo2 = p.sem("p4_wo2")
        for og in range(NCH):
            p.dma("gpsimd", st.w1o16[:, :], hb["w1out"][og],
                  waits=[(s_po, s_po.n)], inc=s_wo2)
            wo_ld = s_wo2.n
            po = P[6 + og % 2]
            for wc in range(16):
                wmm = []
                if wc == 0:
                    wmm.append((s_dve, mkT_done))
                    wmm.append((s_wo2, wo_ld))
                    wmm.append((s_ev, max(s_ev.n - 1, 0)))
                p.mm(po[:, :], st.w1o16[:, wc * 128:(wc + 1) * 128],
                     st.mkT[:, wc * G:(wc + 1) * G],
                     start=(wc == 0), stop=(wc == 15), waits=wmm,
                     inc=s_po if wc == 15 else None)
            ost = st.outst[og % 2]
            p.emit("vector", lambda eh, po=po, o=ost, xc=x[og], t=t0: eh.tensor_add(
                o[:, :], xc[:, t:t + G], po[:, :]),
                waits=[(s_po, s_po.n), (s_yst, max(s_yst.n - 16, 0))], inc=s_ev)
            p.dma("sync", ydram[b, og * 128:(og + 1) * 128, t0:t0 + G], ost[:, :],
                  waits=[(s_ev, s_ev.n)], inc=s_yst)
            p.emit("sync", lambda eh: eh.nop(), waits=[(s_yst, s_yst.n)])
    return [(s_yst, s_yst.n), (s_ev, s_ev.n), (s_wdma, s_wdma.n),
            (p.sem("p4_wo2"), p.sem("p4_wo2").n)]



def build_program(phases=("p1", "p2", "p3", "p4")):
    nc = bass.Bass("TRN2", target_bir_lowering=False)
    hb = declare_inputs(nc)
    y = nc.dram_tensor("y", [NB, D, T], F32, kind="ExternalOutput")
    scr = {
        "wg3s": nc.dram_tensor("wg3s", [NEX, 128, 4096], BF16),
        "wu3s": nc.dram_tensor("wu3s", [NEX, 128, 4096], BF16),
        "wd3s": nc.dram_tensor("wd3s", [NEX, 128, 4096], BF16),
    }
    ctx = ExitStack()
    p = Prog(nc, ctx)
    bufs = Bufs(nc, ctx, p, hb)
    p.emit("tensor", lambda eh: eh.nop(), waits=((bufs.ld_c, bufs.ld_c.n),))
    if "p2" in phases:
        emit_moe_casts(p, hb, scr)

    with ExitStack() as pctx:
        ln = LNState(nc, pctx)
        batch_done = ()
        for b in range(NB):
            sx = emit_x_load(p, bufs, hb, b, prev=batch_done)
            last = [(sx, sx.n)]
            if "p1" in phases:
                with ExitStack() as c1:
                    st1 = P1State(nc, c1, tag=f"_b{b}")
                    last = emit_l0_attn_batch(p, bufs, st1, ln, hb, sx)
            if "p2" in phases:
                with ExitStack() as c2:
                    st2 = P2State(nc, c2, tag=f"_b{b}")
                    last = emit_moe_batch(p, bufs, st2, ln, hb, scr, last)
            if "p3" in phases:
                with ExitStack() as c3:
                    st3 = P3State(nc, c3, tag=f"_b{b}")
                    last = emit_l1_attn_batch(p, bufs, st3, ln, hb, last)
            if "p4" in phases:
                with ExitStack() as c4:
                    st4 = P4State(nc, c4, tag=f"_b{b}")
                    last = emit_l1_ffn_batch(p, bufs, st4, ln, hb, y, b, last)
            else:
                emit_x_store(p, bufs, y, b, last)
            batch_done = last
            p.flush()
    s = p.sem("st_x")
    p.emit("gpsimd", lambda eh: eh.nop(), waits=tuple(batch_done) + ((s, s.n),))
    p.flush()
    ctx.close()
    return nc


def kernel(**inputs):
    shared = prep_shared_inputs(inputs)
    per_core = [prep_core_inputs(inputs, c, shared) for c in range(NCORE)]
    nc = build_program(phases=("p1", "p2", "p3", "p4"))
    res = run_bass_kernel_spmd(nc, per_core, core_ids=list(range(NCORE)))
    out = np.empty((32, T, D), np.float32)
    for c in range(NCORE):
        yc = np.asarray(res.results[c]["y"], np.float32)
        out[c * NB:(c + 1) * NB] = yc.transpose(0, 2, 1)
    return out


# version 05 (final)
# speedup vs baseline: 1.1460x; 1.0973x over previous
import numpy as np
import ml_dtypes
from contextlib import ExitStack

import concourse.bass as bass
import concourse.mybir as mybir
from concourse.bass_utils import run_bass_kernel_spmd

F32 = mybir.dt.float32
BF16 = mybir.dt.bfloat16
AF = mybir.ActivationFunctionType
ALU = mybir.AluOpType
AX = mybir.AxisListType
BF = ml_dtypes.bfloat16

NCORE = 8
NB = 4
T = 2048
D = 1024
NCH = 8
NH = 16
HD = 64
NPAIR = 8
G = 512
NGB = T // G
NE = 32
NEX = 34
EI = 512
LNEPS = 1e-5

ENGINES = ("tensor", "vector", "scalar", "gpsimd", "sync")
DBG = False
P3S = 99
P3S2 = 99
P3IT = 6
P3IP = 99
P3W2 = 0


class Sem:
    __slots__ = ("h", "n")

    def __init__(self, h):
        self.h = h
        self.n = 0


class Prog:

    def __init__(self, nc, ctx):
        self.nc = nc
        self.ctx = ctx
        self.items = {e: [] for e in ENGINES}
        self._sems = {}

    def sem(self, name):
        s = self._sems.get(name)
        if s is None:
            s = Sem(self.ctx.enter_context(self.nc.semaphore(name)))
            self._sems[name] = s
        return s

    def emit(self, eng, fn, waits=(), inc=None, inck=1):
        ws = tuple((s, v) for (s, v) in waits if v > 0)
        self.items[eng].append((fn, ws, inc, inck))
        if inc is not None:
            inc.n += inck
        return inc.n if inc is not None else 0

    def flush(self):
        live = {e: items for e, items in self.items.items() if items}
        if not live:
            return
        with self.nc.Block() as block:
            for e, items in live.items():
                def body(eh, _items=tuple(items)):
                    for fn, waits, inc, inck in _items:
                        for (s, v) in waits:
                            eh.wait_ge(s.h, v)
                        instr = fn(eh)
                        if inc is not None:
                            instr.then_inc(inc.h, inck)
                getattr(block, e)(body)
        self.items = {e: [] for e in ENGINES}

    def mm(self, out, lhsT, rhs, start, stop, tile=None, waits=(), inc=None):
        return self.emit(
            "tensor",
            lambda eh, o=out, l=lhsT, r=rhs, s1=start, s2=stop, tp=tile:
            eh.matmul(o, l, r, start=s1, stop=s2, tile_position=tp),
            waits, inc)

    def transp(self, out, in_, ident, waits=(), inc=None):
        return self.emit(
            "tensor", lambda eh, o=out, i=in_, d=ident: eh.transpose(o, i, d),
            waits, inc)

    def act(self, out, in_, func, bias=0.0, scale=1.0, accum=None, waits=(), inc=None):
        return self.emit(
            "scalar",
            lambda eh, o=out, i=in_, f=func, b=bias, s=scale, a=accum:
            eh.activation(o, i, f, bias=b, scale=s, accum_out=a),
            waits, inc)

    def dma(self, eng, dst, src, waits=(), inc=None):
        return self.emit(eng, lambda eh, d=dst, s=src: eh.dma_start(d, s),
                         waits, inc, inck=16)

    def vop(self, eng, name, *args, waits=(), inc=None, **kw):
        return self.emit(
            eng, lambda eh, n=name, a=args, k=kw: getattr(eh, n)(*a, **k),
            waits, inc)


def ap_full(t):
    return t[tuple(slice(0, s) for s in t.shape)]



def prep_shared_inputs(inputs):
    d = {}
    W = np.asarray(inputs["L0_attn_W_qkv"])
    b = np.asarray(inputs["L0_attn_b_qkv"])
    qrows = np.array([192 * (2 * p + j // 64) + (j % 64)
                      for p in range(NPAIR) for j in range(128)])
    krows = qrows + 64
    vrows = np.array([192 * h + 128 + dd for h in range(NH) for dd in range(HD)])
    def tile_pairs(*mats):
        out = np.empty((NPAIR, 128, len(mats) * 1024), np.float32)
        for pi in range(NPAIR):
            for mi, M in enumerate(mats):
                blk = M[:, pi * 128:(pi + 1) * 128].reshape(NCH, 128, 128)
                out[pi, :, mi * 1024:(mi + 1) * 1024] = blk.transpose(1, 0, 2).reshape(128, 1024)
        return np.ascontiguousarray(out)
    d["wqkv"] = tile_pairs(W[qrows].T, W[krows].T, W[vrows].T)
    d["bq"] = np.ascontiguousarray(b[qrows].reshape(NPAIR, 128).T)
    d["bk"] = np.ascontiguousarray(b[krows].reshape(NPAIR, 128).T)
    d["bv"] = np.ascontiguousarray(b[vrows].reshape(1, 1024))
    d["wo"] = tile_pairs(np.asarray(inputs["L0_attn_W_out"]).T)
    d["bo"] = np.ascontiguousarray(np.asarray(inputs["L0_attn_b_out"]).reshape(NCH, 128).T)

    d["gw"] = np.ascontiguousarray(np.asarray(inputs["L0_ffn_gate_w"]).T)
    d["gb"] = np.ascontiguousarray(np.asarray(inputs["L0_ffn_gate_bias"]).reshape(1, NE))
    wg = np.asarray(inputs["L0_ffn_w_gate"])
    wu = np.asarray(inputs["L0_ffn_w_up"])
    wd = np.asarray(inputs["L0_ffn_w_down"])
    shgT = np.asarray(inputs["L0_ffn_sh_gate"]).T
    shuT = np.asarray(inputs["L0_ffn_sh_up"]).T
    shdT = np.asarray(inputs["L0_ffn_sh_down"]).T
    wg3 = np.empty((NEX, 1024, 512), np.float32)
    wu3 = np.empty((NEX, 1024, 512), np.float32)
    wd3 = np.empty((NEX, 512, 1024), np.float32)
    wg3[:NE] = wg.transpose(0, 2, 1)
    wu3[:NE] = wu.transpose(0, 2, 1)
    wd3[:NE] = wd.transpose(0, 2, 1)
    wg3[NE], wg3[NE + 1] = shgT[:, 0:512], shgT[:, 512:1024]
    wu3[NE], wu3[NE + 1] = shuT[:, 0:512], shuT[:, 512:1024]
    wd3[NE], wd3[NE + 1] = shdT[0:512, :], shdT[512:1024, :]
    d["wg3"] = np.ascontiguousarray(wg3.reshape(NEX, 8, 128, 512).transpose(0, 2, 1, 3)
                                    .reshape(NEX, 128, 4096))
    d["wu3"] = np.ascontiguousarray(wu3.reshape(NEX, 8, 128, 512).transpose(0, 2, 1, 3)
                                    .reshape(NEX, 128, 4096))
    d["wd3"] = np.ascontiguousarray(wd3.reshape(NEX, 4, 128, 1024).transpose(0, 2, 1, 3)
                                    .reshape(NEX, 128, 4096))

    w1 = np.asarray(inputs["L1_attn_w_qkv"])
    d["w1qkv"] = tile_pairs(w1[:, 0:1024], w1[:, 1024:2048], w1[:, 2048:3072])
    d["w1p"] = tile_pairs(np.asarray(inputs["L1_attn_w_proj"]))

    w1inT = np.zeros((1024, 2048), np.float32)
    w1inT[:, :2040] = np.asarray(inputs["L1_ffn_W_in"]).T
    b1p = np.zeros((1, 2048), np.float32)
    b1p[0, :2040] = np.asarray(inputs["L1_ffn_b_in"])
    w1outT = np.zeros((2048, 1024), np.float32)
    w1outT[:2040] = np.asarray(inputs["L1_ffn_W_out"]).T
    d["w1in"] = w1inT
    d["b1in"] = b1p
    d["w1out"] = np.ascontiguousarray(
        w1outT.reshape(16, 128, 8, 128).transpose(2, 1, 0, 3).reshape(8, 128, 2048))

    d["c_or16"] = np.ones((1, 128), BF)
    d["c_or32"] = np.ones((1, 128), np.float32)
    d["c_oc16"] = np.ones((128, 1), BF)
    d["c_oc32"] = np.ones((128, 1), np.float32)
    d["c_id16"] = np.eye(128, dtype=BF)
    d["c_id32"] = np.eye(128, dtype=np.float32)
    hm = np.zeros((128, 2), np.float32)
    hm[0:64, 0] = 1.0
    hm[64:128, 1] = 1.0
    d["c_hm"] = hm
    sel = np.zeros((32, NE * 128), BF)
    for e in range(NE):
        sel[e, e * 128:(e + 1) * 128] = 1.0
    d["c_sel"] = sel
    return d


def prep_core_inputs(inputs, core, shared):
    d = dict(shared)
    x = np.asarray(inputs["L0_attn_x"])
    xs = x[core * NB:(core + 1) * NB]
    d["x"] = np.ascontiguousarray(xs.transpose(0, 2, 1))
    return d


def declare_inputs(nc):
    t = {}

    def di(name, shape, dtype=F32):
        t[name] = nc.dram_tensor(name, list(shape), dtype, kind="ExternalInput")
    di("x", (NB, D, T))
    di("wqkv", (NPAIR, 128, 3072))
    di("bq", (128, NPAIR)); di("bk", (128, NPAIR)); di("bv", (1, D))
    di("wo", (NPAIR, 128, 1024)); di("bo", (128, NCH))
    di("gw", (D, NE)); di("gb", (1, NE))
    di("wg3", (NEX, 128, 4096)); di("wu3", (NEX, 128, 4096)); di("wd3", (NEX, 128, 4096))
    di("w1qkv", (NPAIR, 128, 3072)); di("w1p", (NPAIR, 128, 1024))
    di("w1in", (D, 2048)); di("b1in", (1, 2048)); di("w1out", (NCH, 128, 2048))
    di("c_or16", (1, 128), BF16); di("c_or32", (1, 128))
    di("c_oc16", (128, 1), BF16); di("c_oc32", (128, 1))
    di("c_id16", (128, 128), BF16); di("c_id32", (128, 128))
    di("c_hm", (128, 2))
    di("c_sel", (32, NE * 128), BF16)
    return t



class Bufs:
    def __init__(self, nc, ctx, p, hb):
        sb = lambda n, s, dt: ctx.enter_context(nc.sbuf_tensor(n, s, dt))
        ps = lambda n, s, dt: ctx.enter_context(nc.psum_tensor(n, s, dt))
        self.x = [sb(f"x_{c}", [128, T], BF16) for c in range(NCH)]
        self.or16 = sb("k_or16", [1, 128], BF16)
        self.or32 = sb("k_or32", [1, 128], F32)
        self.oc16 = sb("k_oc16", [128, 1], BF16)
        self.oc32 = sb("k_oc32", [128, 1], F32)
        self.id16 = sb("k_id16", [128, 128], BF16)
        self.hm = sb("k_hm", [128, 2], F32)
        self.PB = [ps(f"psumB{i}", [128, 4 * G], F32) for i in range(2)]
        self.P = [self.PB[i // 4][:, (i % 4) * G:(i % 4 + 1) * G] for i in range(8)]
        self.hb = hb
        s = p.sem("ld_c")
        for nm, dst in (("c_or16", self.or16), ("c_or32", self.or32),
                        ("c_oc16", self.oc16), ("c_oc32", self.oc32),
                        ("c_id16", self.id16),
                        ("c_hm", self.hm)):
            p.dma("sync", ap_full(dst), ap_full(hb[nm]), inc=s)
        self.ld_c = s


def emit_x_load(p, bufs, hb, b, prev=()):
    s = p.sem("ld_x")
    st = p.sem("st_x")
    for c in range(NCH):
        w = list(prev) + [(st, st.n)] if c == 0 else ()
        p.dma("gpsimd", bufs.x[c][:, :], hb["x"][b, c * 128:(c + 1) * 128, :],
              waits=w, inc=s)
    return s


def emit_x_store(p, bufs, ydram, b, waits):
    s = p.sem("st_x")
    for c in range(NCH):
        p.dma("gpsimd", ydram[b, c * 128:(c + 1) * 128, :], bufs.x[c][:, :],
              waits=waits, inc=s)
    return s



class LNState:
    def __init__(self, nc, ctx, tag=""):
        sb = lambda n, s, dt: ctx.enter_context(nc.sbuf_tensor(n + tag, s, dt))
        self.xsq = [sb(f"ln_xsq{i}", [128, G], BF16) for i in range(2)]
        self.rA = sb("ln_rA", [1, G], F32)
        self.rB = sb("ln_rB", [1, G], F32)
        self.rC = sb("ln_rC", [1, G], F32)
        self.mb16 = sb("ln_mb16", [128, G], BF16)
        self.rb16 = sb("ln_rb16", [128, G], BF16)
        self.tsub = [sb(f"ln_tsub{i}", [128, G], BF16) for i in range(2)]


def emit_ln(p, bufs, ln, xtiles, t0, h16, hcol, pst, pst2, pbc, xw=()):
    s_sq = p.sem("ln_sq")
    s_st = p.sem("ln_st")
    s_row = p.sem("ln_row")
    s_bc = p.sem("ln_bc")
    s_ev = p.sem("ln_ev")
    s_sub = p.sem("ln_sub")
    s_ap = p.sem("ln_ap")

    ap0 = s_ap.n
    sq0 = s_sq.n
    st0 = s_st.n
    row0 = s_row.n
    ev0 = s_ev.n
    for c in range(NCH):
        xa = xtiles[c][:, t0:t0 + G]
        sqt = ln.xsq[c % 2]
        w_act = [(s_st, st0 + c - 1)] if c >= 2 else [(s_st, st0)]
        p.act(sqt[:, :], xa, AF.Square, waits=w_act, inc=s_sq)
        wmm = ([(s_row, row0)] + list(xw)) if c == 0 else ()
        p.mm(pst[0:1, :], bufs.oc16[:, :], xa, start=(c == 0), stop=(c == NCH - 1),
             waits=wmm)
        p.mm(pst2[64:65, :], bufs.oc16[:, :], sqt[:, :], start=(c == 0), stop=(c == NCH - 1),
             tile=(0, 64), waits=[(s_sq, sq0 + c + 1)], inc=s_st)
    st_done = s_st.n

    p.act(ln.rA[:, :], pst[0:1, :], AF.Copy, scale=1.0 / D,
          waits=[(s_st, st_done)], inc=s_row)
    p.act(ln.rB[:, :], pst2[64:65, :], AF.Copy, scale=1.0 / D, inc=s_row)
    p.act(ln.rC[:, :], ln.rA[:, :], AF.Square, waits=[(s_row, row0 + 1)], inc=s_row)
    p.emit("vector", lambda eh: eh.scalar_tensor_tensor(
        ln.rB[:, :], ln.rB[:, :], LNEPS, ln.rC[:, :], op0=ALU.add, op1=ALU.subtract),
        waits=[(s_row, s_row.n)], inc=s_row)
    p.act(ln.rB[:, :], ln.rB[:, :], AF.Sqrt, waits=[(s_row, s_row.n)], inc=s_row)
    p.emit("vector", lambda eh: eh.reciprocal(ln.rC[:, :], ln.rB[:, :]),
           waits=[(s_row, s_row.n)], inc=s_row)
    row_done = s_row.n

    p.mm(pbc[:, :], bufs.or32[:, :], ln.rA[:, :], start=True, stop=True,
         waits=[(s_row, row_done), (s_ev, ev0)], inc=s_bc)
    p.act(ln.mb16[:, :], pbc[:, :], AF.Copy, waits=[(s_bc, s_bc.n), (s_ap, ap0)], inc=s_ev)
    p.mm(pbc[:, :], bufs.or32[:, :], ln.rC[:, :], start=True, stop=True,
         waits=[(s_ev, s_ev.n)], inc=s_bc)
    p.act(ln.rb16[:, :], pbc[:, :], AF.Copy, waits=[(s_bc, s_bc.n)], inc=s_ev)
    ev_done = s_ev.n

    ap0 = s_ap.n
    for c in range(NCH):
        xa = xtiles[c][:, t0:t0 + G]
        dst = h16[c][:, hcol:hcol + G]
        ts = ln.tsub[c % 2]
        wv = [(s_ev, ev_done)] if c < 2 else [(s_ap, ap0 + c - 1)]
        p.emit("vector", lambda eh, a=xa, ts=ts: eh.tensor_sub(ts[:, :], a, ln.mb16[:, :]),
               waits=wv, inc=s_sub)
        wp = [(s_sub, s_sub.n)]
        if c == 0:
            wp.append((s_ev, ev_done))
        p.emit("gpsimd", lambda eh, o=dst, ts=ts: eh.tensor_mul(o, ts[:, :], ln.rb16[:, :]),
               waits=wp, inc=s_ap)
    return s_ap.n



class P1State:
    def __init__(self, nc, ctx, tag=""):
        sb = lambda n, s, dt: ctx.enter_context(nc.sbuf_tensor(n + tag, s, dt))
        self.h16 = [sb(f"p1_h{c}", [128, T], BF16) for c in range(NCH)]
        self.logits = [sb(f"p1_lg{c}", [128, T // 2], BF16) for c in range(NCH)]
        self.vaug = [sb(f"p1_va{k}", [128, 2 * (HD + 1)], BF16) for k in range(16)]
        self.wqkv16 = sb("p1_wqkv", [128, 3072], BF16)
        self.wo16 = [sb("p1_wo0", [128, 1024], BF16)]
        self.bq = sb("p1_bq", [128, NPAIR], F32)
        self.bk = sb("p1_bk", [128, NPAIR], F32)
        self.bo = sb("p1_bo", [128, NCH], F32)
        self.bv16 = sb("p1_bv16", [1, 128], BF16)
        self.q = sb("p1_q", [128, T // 2], BF16)
        self.k = sb("p1_k", [128, T], BF16)
        self.nt = [sb(f"p1_nt{i}", [128, G], BF16) for i in range(4)]
        self.mqa = sb("p1_mqa", [128, 1], F32)
        self.mka = sb("p1_mka", [128, 1], F32)
        self.mprod = sb("p1_mprod", [128, 1], F32)
        self.cb = sb("p1_cb", [128, 2], F32)
        self.zinv = sb("p1_zinv", [1, 2 * G], F32)
        self.zb16 = sb("p1_zb16", [128, G], BF16)


def emit_l0_attn_batch(p, bufs, st, ln, hb, xld):
    x = bufs.x
    P = bufs.P
    s_wdma = p.sem("p1_wdma")
    s_bdma = p.sem("p1_bdma")
    s_pe = p.sem("p1_pe")
    s_qev = p.sem("p1_qev")
    s_qkd = p.sem("p1_qkd")
    s_exp = p.sem("p1_exp")
    s_av = p.sem("p1_av")
    s_dve = p.sem("p1_dve")
    s_lgt = p.sem("p1_lgt")
    s_res = p.sem("p1_res")
    s_ap = p.sem("ln_ap")

    if not hasattr(st, "tr"):
        p.dma("sync", st.bq[:, :], ap_full(hb["bq"]), inc=s_bdma)
        p.dma("sync", st.bk[:, :], ap_full(hb["bk"]), inc=s_bdma)
        p.dma("sync", st.bo[:, :], ap_full(hb["bo"]), inc=s_bdma)
        for kt in range(16):
            od = st.vaug[kt].rearrange("p (h u) -> p h u", u=HD + 1)[:, :, HD:HD + 1]
            p.emit("gpsimd", lambda eh, d2=od: eh.memset(d2, 1.0), inc=s_dve)
    bias_ld = s_bdma.n

    p.emit("tensor", lambda eh: eh.nop(), waits=[(xld, xld.n)])
    p.emit("scalar", lambda eh: eh.nop(), waits=[(xld, xld.n)])
    p.emit("vector", lambda eh: eh.nop(), waits=[(xld, xld.n)])
    ln_marks = []
    for g in range(NGB):
        ln_marks.append(emit_ln(p, bufs, ln, x, g * G, st.h16, g * G, P[0], P[1], P[2]))
    ln_done = s_ap.n

    if not hasattr(st, "tr"):
        st.tr = {"qk_pe": 0, "av_end": 0, "exp_end": 0, "wo_pe": [0, 0]}
    tr = st.tr

    for qh in range(2):
        for pr in range(NPAIR):
            wprev = [(s_pe, tr["qk_pe"])]
            p.dma("gpsimd", st.wqkv16[:, :], hb["wqkv"][pr], waits=wprev, inc=s_wdma)
            p.dma("gpsimd", st.bv16[:, :], hb["bv"][:, pr * 128:(pr + 1) * 128],
                  inc=s_wdma)
            w_ld = s_wdma.n
            prev_av_end = tr["av_end"]
            prev_exp_end = tr["exp_end"]

            def qk_emit(wt, bt, dst, groups):
                for gi, g in enumerate(groups):
                    pq = P[gi % 2]
                    for c in range(NCH):
                        wmm = []
                        if c == 0:
                            wmm.append((s_wdma, w_ld))
                            wmm.append((s_qkd, max(s_qkd.n - 1, 0)))
                            wmm.append((s_ap, ln_marks[g]))
                            wmm.append((s_exp, prev_exp_end))
                            wmm.append((s_res, s_res.n))
                        p.mm(pq[:, :], wt[:, c * 128:(c + 1) * 128],
                             st.h16[c][:, g * G:(g + 1) * G],
                             start=(c == 0), stop=(c == NCH - 1), waits=wmm,
                             inc=s_pe if c == NCH - 1 else None)
                    p.emit("vector", lambda eh, d=dst[:, gi * G:(gi + 1) * G], pq=pq,
                           bc=bt[:, pr:pr + 1]: eh.tensor_scalar_add(d, pq[:, :], bc),
                           waits=[(s_pe, s_pe.n), (s_bdma, bias_ld)], inc=s_qkd)
            qk_emit(st.wqkv16[:, 1024:2048], st.bk, st.k, list(range(NGB)))
            qk_emit(st.wqkv16[:, 0:1024], st.bq, st.q, [2 * qh, 2 * qh + 1])
            tr["qk_pe"] = s_pe.n
            qk_done = s_qkd.n

            p.emit("vector", lambda eh: eh.reduce_max(
                st.mqa[:, :], st.q[:, :], axis=AX.X, apply_absolute_value=True),
                waits=[(s_qkd, qk_done)], inc=s_dve)
            p.emit("vector", lambda eh: eh.reduce_max(
                st.mka[:, :], st.k[:, :], axis=AX.X, apply_absolute_value=True), inc=s_dve)
            p.emit("vector", lambda eh: eh.tensor_mul(
                st.mprod[:, :], st.mqa[:, :], st.mka[:, :]),
                waits=[(s_dve, s_dve.n)], inc=s_dve)
            mp_done = s_dve.n
            for kt in range(16):
                pv = P[2 + kt % 2]
                for c in range(NCH):
                    wmm = []
                    if c == 0:
                        wmm.append((s_wdma, w_ld))
                        wmm.append((s_ap, ln_marks[kt // 4]))
                        if kt < 2:
                            wmm.append((s_exp, prev_exp_end))
                        else:
                            wmm.append((s_dve, s_dve.n - 1))
                    p.mm(pv[:, 0:128], st.h16[c][:, kt * 128:(kt + 1) * 128],
                         st.wqkv16[:, 2048 + c * 128:2048 + (c + 1) * 128],
                         start=(c == 0), stop=False, waits=wmm)
                p.mm(pv[:, 0:128], bufs.or16[:, :],
                     st.bv16[:, :], start=False, stop=True, inc=s_pe)
                vdst = st.vaug[kt].rearrange("p (h u) -> p h u", u=HD + 1)[:, :, 0:HD]
                vsrc = pv[:, 0:128].rearrange("p (h u) -> p h u", u=HD)
                p.emit("vector", lambda eh, d2=vdst, s2=vsrc: eh.tensor_copy(d2, s2),
                       waits=[(s_pe, s_pe.n), (s_av, prev_av_end)], inc=s_dve)
            v_done = s_dve.n
            tr["qk_pe"] = s_pe.n

            p.mm(P[6][0:1, 0:2], st.mprod[:, :], bufs.hm[:, :], start=True, stop=True,
                 waits=[(s_dve, mp_done), (s_exp, s_exp.n)], inc=s_pe)
            p.act(st.cb[0:1, 0:2], P[6][0:1, 0:2], AF.Copy, scale=-0.125,
                  waits=[(s_pe, s_pe.n)], inc=s_qev)
            for h in range(2):
                p.mm(P[6][:, 4 + h:5 + h], bufs.or32[:, :], st.cb[0:1, h:h + 1],
                     start=True, stop=True, waits=[(s_qev, s_qev.n)],
                     inc=s_pe if h == 1 else None)
            p.emit("vector", lambda eh: eh.tensor_copy(st.cb[:, 0:2], P[6][:, 4:6]),
                   waits=[(s_pe, s_pe.n)], inc=s_dve)
            cb_done = s_dve.n

            exp_hist = []
            av_hist = []
            for gi in range(2):
                qcols = slice(gi * G, (gi + 1) * G)
                for kc in range(16):
                    j = gi * 16 + kc
                    spar = kc % 2
                    psA, psB = P[0 + spar], P[2 + spar]
                    kc1 = slice(kc * 128, kc * 128 + 64)
                    kc2 = slice(kc * 128 + 64, (kc + 1) * 128)
                    wmm = []
                    if j == 0:
                        wmm.append((s_qkd, qk_done))
                        wmm.append((s_dve, v_done))
                    if j >= 2:
                        wmm.append((s_exp, exp_hist[j - 2]))
                    p.mm(psA[0:64, :], st.k[0:64, kc1], st.q[0:64, qcols],
                         start=True, stop=True, tile=(0, 0), waits=wmm)
                    p.mm(psA[64:128, :], st.k[0:64, kc2], st.q[0:64, qcols],
                         start=True, stop=True, tile=(0, 64))
                    p.mm(psB[0:64, :], st.k[64:128, kc1], st.q[64:128, qcols],
                         start=True, stop=True, tile=(64, 0))
                    p.mm(psB[64:128, :], st.k[64:128, kc2], st.q[64:128, qcols],
                         start=True, stop=True, tile=(64, 64), inc=s_pe)
                    sc_done = s_pe.n
                    for h, pst_ in ((0, psA), (1, psB)):
                        nt = st.nt[2 * h + spar]
                        wact = [(s_pe, sc_done), (s_dve, cb_done)]
                        if j >= 2:
                            wact.append((s_av, av_hist[j - 2]))
                        p.act(nt[:, :], pst_[:, :], AF.Exp, bias=st.cb[:, h:h + 1],
                              scale=0.125, waits=wact, inc=s_exp)
                    exp_hist.append(s_exp.n)
                    first = (kc == 0)
                    va = st.vaug[kc].rearrange("p (h u) -> p h u", u=HD + 1)
                    wav = [(s_exp, exp_hist[-1])]
                    if first:
                        wav.append((s_lgt, s_lgt.n))
                        wav.append((s_dve, s_dve.n))
                    last = (kc == 15)
                    p.mm(P[4][0:64, :], va[:, 0, 0:HD], st.nt[0 + spar][:, :],
                         start=first, stop=last, tile=(0, 0), waits=wav)
                    p.mm(P[7][64:128, :], va[:, 1, 0:HD], st.nt[2 + spar][:, :],
                         start=first, stop=last, tile=(0, 64))
                    p.mm(P[5][0:1, :], bufs.oc16[:, :], st.nt[0 + spar][:, :],
                         start=first, stop=last, tile=(0, 0))
                    p.mm(P[6][0:1, :], bufs.oc16[:, :], st.nt[2 + spar][:, :],
                         start=first, stop=last, tile=(0, 0), inc=s_av)
                    av_hist.append(s_av.n)
                av_done = s_av.n
                tr["av_end"] = av_done
                p.emit("vector", lambda eh: eh.reciprocal(st.zinv[0:1, 0:G], P[5][0:1, :]),
                       waits=[(s_av, av_done)], inc=s_dve)
                p.emit("vector", lambda eh: eh.reciprocal(st.zinv[0:1, G:2 * G], P[6][0:1, :]),
                       inc=s_dve)
                zi_done = s_dve.n
                p.mm(P[6][0:64, :], bufs.or32[:, 0:64], st.zinv[0:1, 0:G],
                     start=True, stop=True, tile=(0, 0),
                     waits=[(s_dve, zi_done), (s_exp, s_exp.n)])
                p.mm(P[6][64:128, :], bufs.or32[:, 0:64], st.zinv[0:1, G:2 * G],
                     start=True, stop=True, tile=(0, 64), inc=s_pe)
                p.emit("vector", lambda eh: eh.tensor_copy(st.zb16[:, :], P[6][:, :]),
                       waits=[(s_pe, s_pe.n)], inc=s_dve)
                tr["exp_end"] = s_exp.n
                lt = st.logits[pr]
                lcols = slice(gi * G, (gi + 1) * G)
                p.emit("vector", lambda eh, lt=lt, lc=lcols: eh.tensor_mul(
                    lt[0:64, lc], P[4][0:64, :], st.zb16[0:64, :]),
                    waits=[(s_dve, s_dve.n)], inc=s_lgt)
                p.emit("vector", lambda eh, lt=lt, lc=lcols: eh.tensor_mul(
                    lt[64:128, lc], P[7][64:128, :], st.zb16[64:128, :]),
                    inc=s_lgt)
        lg_done = s_lgt.n

        s_wo = p.sem("p1_wo")
        for oc in range(NCH):
            wbuf = st.wo16[0]
            p.dma("gpsimd", wbuf[:, :], hb["wo"][oc],
                  waits=[(s_pe, tr["wo_pe"][0])], inc=s_wo)
            wo_ld = s_wo.n
            for gi in range(2):
                po = P[0 + gi % 2]
                for pc in range(NCH):
                    wmm = []
                    if pc == 0:
                        wmm.append((s_lgt, lg_done))
                        wmm.append((s_wo, wo_ld))
                        wmm.append((s_res, max(s_res.n - 1, 0)))
                        if oc == 0 and gi == 0:
                            wmm.append((s_exp, tr["exp_end"]))
                    p.mm(po[:, :], wbuf[:, pc * 128:(pc + 1) * 128],
                         st.logits[pc][:, gi * G:(gi + 1) * G],
                         start=(pc == 0), stop=(pc == NCH - 1), waits=wmm,
                         inc=s_pe if pc == NCH - 1 else None)
                tr["wo_pe"][0] = s_pe.n
                xa = x[oc][:, (2 * qh + gi) * G:(2 * qh + gi + 1) * G]
                p.emit("vector", lambda eh, po=po, xa=xa, oc=oc: eh.scalar_tensor_tensor(
                    xa, po[:, :], st.bo[:, oc:oc + 1], xa, op0=ALU.add, op1=ALU.add),
                    waits=[(s_pe, s_pe.n), (s_bdma, bias_ld)], inc=s_res)
        tr["qk_pe"] = s_pe.n
    return [(s_res, s_res.n), (s_wdma, s_wdma.n), (p.sem("p1_wo"), p.sem("p1_wo").n),
            (s_bdma, s_bdma.n)]



def emit_fences(p, engines, waits):
    for e in engines:
        p.emit(e, lambda eh: eh.nop(), waits=waits)


class P2State:
    def __init__(self, nc, ctx, tag=""):
        sb = lambda n, s, dt: ctx.enter_context(nc.sbuf_tensor(n + tag, s, dt))
        self.h16 = [sb(f"p2_h{c}", [128, G], BF16) for c in range(NCH)]
        self.gw16 = sb("p2_gw16", [128, NCH * NE], BF16)
        self.gb32 = sb("p2_gb32", [1, NE], F32)
        self.gbb = sb("p2_gbb", [128, NE], F32)
        self.csel = sb("p2_csel", [32, NE * 128], BF16)
        self.scores = [sb(f"p2_sc{t}", [128, NE], F32) for t in range(4)]
        self.scb = [sb(f"p2_scb{t}", [128, NE], F32) for t in range(4)]
        self.gs = [sb(f"p2_gs{t}", [128, 8], F32) for t in range(4)]
        self.gtmp = [sb(f"p2_gt{t}", [128, 8], F32) for t in range(4)]
        self.top8 = [sb(f"p2_t8{t}", [128, 8], F32) for t in range(4)]
        self.masked = [sb(f"p2_mk{t}", [128, NE], F32) for t in range(4)]
        self.sel = [sb(f"p2_sl{t}", [128, NE], F32) for t in range(4)]
        self.w8 = [sb(f"p2_w8{t}", [128, NE], F32) for t in range(4)]
        self.sumw = [sb(f"p2_sw{t}", [128, 1], F32) for t in range(4)]
        self.c2 = [sb(f"p2_c2{t}", [128, 1], F32) for t in range(4)]
        self.comb = [sb(f"p2_cb{t}", [128, NE], BF16) for t in range(4)]
        self.combT = sb("p2_combT", [32, G], BF16)
        self.wg16 = [sb(f"p2_wg{i}", [128, 4096], BF16) for i in range(2)]
        self.wu16 = [sb(f"p2_wu{i}", [128, 4096], BF16) for i in range(2)]
        self.wd16 = [sb(f"p2_wd{i}", [128, 4096], BF16) for i in range(2)]
        self.silu = [sb(f"p2_si{i}", [128, G], F32) for i in range(2)]
        self.gu = [[sb(f"p2_gu{i}_{c}", [128, G], BF16) for c in range(4)] for i in range(2)]
        self.accum = [sb(f"p2_ac{c}", [128, G], F32) for c in range(4)]
        self.c_sb = [sb(f"p2_cs{i}", [128, G], F32) for i in range(2)]


def emit_moe_casts(p, hb, scr):
    s = p.sem("p2_cast")
    for e in range(NEX):
        p.dma("gpsimd", scr["wg3s"][e], hb["wg3"][e], inc=s)
        p.dma("gpsimd", scr["wu3s"][e], hb["wu3"][e], inc=s)
        p.dma("gpsimd", scr["wd3s"][e], hb["wd3"][e], inc=s)
        p.emit("gpsimd", lambda eh: eh.nop(), waits=[(s, s.n)])
    return s


def emit_moe_batch(p, bufs, st, ln, hb, scr, prev_done):
    x = bufs.x
    P = bufs.P
    s_cast = p.sem("p2_cast")
    s_wst = p.sem("p2_wst")
    s_gl = p.sem("p2_gl")
    s_sig = p.sem("p2_sig")
    s_rt = p.sem("p2_rt")
    s_pg = p.sem("p2_pg")
    s_pu = p.sem("p2_pu")
    s_si = p.sem("p2_si")
    s_gu = p.sem("p2_gu")
    s_cm = p.sem("p2_cm")
    s_dn = p.sem("p2_dn")
    s_ev = p.sem("p2_ev")
    s_ap = p.sem("ln_ap")

    emit_fences(p, ("tensor", "scalar", "vector", "gpsimd", "sync"), prev_done)

    first_call = not hasattr(st, "tr2")
    if first_call:
        st.tr2 = {"cast": 0}
        p.dma("gpsimd", st.gw16[:, :].rearrange("p (c e) -> p c e", e=NE),
              hb["gw"].rearrange("(c p) e -> p c e", p=128), inc=p.sem("p2_gwd"))
        p.dma("sync", st.gb32[:, :], ap_full(hb["gb"]), inc=p.sem("p2_gbd"))
        p.dma("sync", ap_full(st.csel), ap_full(hb["c_sel"]), inc=p.sem("p2_gbd"))
    s_gwd = p.sem("p2_gwd")
    s_gbd = p.sem("p2_gbd")

    if first_call:
        p.mm(P[3][:, 0:NE], bufs.or32[:, :], st.gb32[:, :], start=True, stop=True,
             waits=[(s_gbd, s_gbd.n)], inc=s_gl)
        p.act(st.gbb[:, :], P[3][:, 0:NE], AF.Copy, waits=[(s_gl, s_gl.n)], inc=s_sig)

    for grp in range(NGB):
        t0 = grp * G
        emit_ln(p, bufs, ln, x, t0, st.h16, 0, P[0], P[1], P[2], xw=[(s_ev, s_ev.n)])
        ln_done = s_ap.n

        glw = [(s_gwd, s_gwd.n), (s_ap, ln_done), (s_sig, s_sig.n), (s_rt, s_rt.n), (s_ev, s_ev.n)]
        for tt in range(4):
            for c in range(NCH):
                p.mm(P[3][:, tt * NE:(tt + 1) * NE],
                     st.h16[c][:, tt * 128:(tt + 1) * 128],
                     st.gw16[:, c * NE:(c + 1) * NE],
                     start=(c == 0), stop=(c == NCH - 1),
                     waits=glw if (tt == 0 and c == 0) else (),
                     inc=s_gl if c == NCH - 1 else None)
        gl_done = s_gl.n
        p.act(st.scores[0][:, :], P[3][:, 0:NE], AF.Sigmoid,
              waits=[(s_gl, gl_done), (s_rt, s_rt.n)], inc=s_sig)
        for tt in range(1, 4):
            p.act(st.scores[tt][:, :], P[3][:, tt * NE:(tt + 1) * NE], AF.Sigmoid, inc=s_sig)
        sig_done = s_sig.n
        for tt in range(4):
            p.emit("vector", lambda eh, t=tt: eh.tensor_add(
                st.scb[t][:, :], st.scores[t][:, :], st.gbb[:, :]),
                waits=[(s_sig, sig_done), (s_rt, s_rt.n)], inc=s_rt)
        pairs = [(0, 1), (0, 2), (0, 3), (1, 2), (1, 3), (2, 3)]
        for tt in range(4):
            v = st.scb[tt][:, :].rearrange("p (g i) -> p g i", i=4)
            for pi, (i, j) in enumerate(pairs):
                dst = st.gs[tt] if pi == 0 else st.gtmp[tt]
                p.emit("vector", lambda eh, d=dst, a=v[:, :, i], b2=v[:, :, j]:
                       eh.tensor_add(d[:, :], a, b2),
                       waits=[(s_rt, s_rt.n)], inc=s_rt)
                if pi > 0:
                    p.emit("vector", lambda eh, t=tt: eh.tensor_max(
                        st.gs[t][:, :], st.gs[t][:, :], st.gtmp[t][:, :]),
                        waits=[(s_rt, s_rt.n)], inc=s_rt)
        for tt in range(4):
            p.emit("vector", lambda eh, t=tt: eh.max(out=st.top8[t][:, :], in_=st.gs[t][:, :]),
                   waits=[(s_rt, s_rt.n)], inc=s_rt)
        for tt in range(4):
            p.emit("vector", lambda eh, t=tt: eh.tensor_scalar(
                st.gtmp[t][:, :], st.gs[t][:, :], st.top8[t][:, 3:4], None, op0=ALU.is_ge),
                waits=[(s_rt, s_rt.n)], inc=s_rt)
        for tt in range(4):
            gb = st.gtmp[tt][:, :, None].broadcast_to([128, 8, 4])
            p.emit("vector", lambda eh, t=tt, g=gb: eh.tensor_mul(
                st.masked[t][:, :].rearrange("p (g i) -> p g i", i=4),
                st.scb[t][:, :].rearrange("p (g i) -> p g i", i=4), g),
                waits=[(s_rt, s_rt.n)], inc=s_rt)
        for tt in range(4):
            p.emit("vector", lambda eh, t=tt: eh.max(out=st.top8[t][:, :], in_=st.masked[t][:, :]),
                   waits=[(s_rt, s_rt.n)], inc=s_rt)
        for tt in range(4):
            p.emit("vector", lambda eh, t=tt: eh.tensor_scalar(
                st.sel[t][:, :], st.masked[t][:, :], st.top8[t][:, 7:8], None, op0=ALU.is_ge),
                waits=[(s_rt, s_rt.n)], inc=s_rt)
        for tt in range(4):
            p.emit("vector", lambda eh, t=tt: eh.scalar_tensor_tensor(
                st.w8[t][:, :], st.scores[t][:, :], 1.0, st.sel[t][:, :],
                op0=ALU.mult, op1=ALU.mult, accum_out=st.sumw[t][:, :]),
                waits=[(s_rt, s_rt.n)], inc=s_rt)
        for tt in range(4):
            p.emit("vector", lambda eh, t=tt: eh.reciprocal(st.c2[t][:, :], st.sumw[t][:, :]),
                   waits=[(s_rt, s_rt.n)], inc=s_rt)
        for tt in range(4):
            p.emit("vector", lambda eh, t=tt: eh.tensor_scalar_mul(
                st.c2[t][:, :], st.c2[t][:, :], 2.5), waits=[(s_rt, s_rt.n)], inc=s_rt)
        for tt in range(4):
            p.emit("vector", lambda eh, t=tt: eh.tensor_scalar_mul(
                st.comb[t][:, :], st.w8[t][:, :], st.c2[t][:, :]),
                waits=[(s_rt, s_rt.n)], inc=s_rt)
        rt_done = s_rt.n
        if DBG and grp == 0 and first_call:
            sdb = p.sem("p2_dbg")
            for c in range(NCH):
                t = p.nc.dram_tensor(f"dbg_h{c}", [128, G], BF16, kind="ExternalOutput")
                p.dma("sync", ap_full(t), st.h16[c][:, :], waits=[(s_ap, ln_done)], inc=sdb)
            for t4 in range(4):
                t = p.nc.dram_tensor(f"dbg_cb{t4}", [128, NE], BF16, kind="ExternalOutput")
                p.dma("sync", ap_full(t), st.comb[t4][:, :], waits=[(s_rt, rt_done)], inc=sdb)
                t = p.nc.dram_tensor(f"dbg_sc{t4}", [128, NE], F32, kind="ExternalOutput")
                p.dma("sync", ap_full(t), st.scores[t4][:, :], waits=[(s_rt, rt_done)], inc=sdb)
        p3b = P[3][:, :].bitcast(BF16)
        for tt in range(4):
            p.transp(p3b[0:32, tt * 128:(tt + 1) * 128], st.comb[tt][:, :], bufs.id16[:, :],
                     waits=[(s_rt, rt_done), (s_sig, s_sig.n)] if tt == 0 else (),
                     inc=s_gl if tt == 3 else None)
        p.emit("vector", lambda eh: eh.tensor_copy(st.combT[:, :], p3b[0:32, 0:G]),
               waits=[(s_gl, s_gl.n)], inc=s_rt)
        combT_done = s_rt.n

        lo_evac_prev = s_ev.n
        n_i = 0
        for e in range(NEX):
            par = e % 2
            routed = e < NE
            if e < 2:
                ecast = 48 * (e + 1)
                p.dma("sync", st.wg16[e % 2][:, :], scr["wg3s"][e],
                      waits=[(s_cast, ecast), (s_dn, st.tr2.get("dn_prev", 0))], inc=s_wst)
                p.dma("sync", st.wu16[e % 2][:, :], scr["wu3s"][e], inc=s_wst)
                p.dma("sync", st.wd16[e % 2][:, :], scr["wd3s"][e], inc=s_wst)
                p.emit("sync", lambda eh: eh.nop(), waits=[(s_wst, s_wst.n)])
                st.tr2[f"wldgu{e}"] = s_wst.n
                st.tr2[f"wldd{e}"] = s_wst.n
            wld_e = st.tr2[f"wldgu{e}"]
            wld_d = st.tr2[f"wldd{e}"]

            if routed:
                pc_ = P[6 + (e % 2)]
                cw = [(s_rt, combT_done), (s_ev, st.tr2.get(f"hiev{e % 2}", 0)),
                      (s_gbd, s_gbd.n)]
                p.mm(pc_[:, :], st.csel[:, e * 128:(e + 1) * 128], st.combT[:, :],
                     start=True, stop=True, waits=cw, inc=s_gl)
                p.act(st.c_sb[par][:, :], pc_[:, :], AF.Copy,
                      waits=[(s_gl, s_gl.n), (s_cm, st.tr2.get(f"cmu{par}", 0))], inc=s_sig)
                st.tr2[f"csb{par}"] = s_sig.n

            for i in range(4):
                n_i += 1
                for c in range(NCH):
                    wmm = []
                    if c == 0:
                        wmm.append((s_wst, wld_e))
                        if i == 0 and e == 0:
                            wmm.append((s_ap, ln_done))
                        wmm.append((s_si, s_si.n))
                    p.mm(P[4][:, :], st.wg16[par][:, c * 512 + i * 128:c * 512 + (i + 1) * 128],
                         st.h16[c][:, :], start=(c == 0), stop=(c == NCH - 1),
                         waits=wmm, inc=s_pg if c == NCH - 1 else None)
                for c in range(NCH):
                    wmm = []
                    if c == 0:
                        wmm.append((s_gu, s_gu.n))
                    p.mm(P[5][:, :], st.wu16[par][:, c * 512 + i * 128:c * 512 + (i + 1) * 128],
                         st.h16[c][:, :], start=(c == 0), stop=(c == NCH - 1),
                         waits=wmm, inc=s_pu if c == NCH - 1 else None)
                sq = st.silu[n_i % 2]
                p.act(sq[:, :], P[4][:, :], AF.Silu,
                      waits=[(s_pg, s_pg.n), (s_gu, max(s_gu.n - 1, 0))], inc=s_si)
                gub = st.gu[par][i]
                p.emit("vector", lambda eh, g=gub, sq=sq: eh.tensor_mul(
                    g[:, :], sq[:, :], P[5][:, :]),
                    waits=[(s_pu, s_pu.n), (s_si, s_si.n),
                           (s_dn, st.tr2.get("dn_prev2", 0))], inc=s_gu)
                st.tr2[f"gu{par}{i}"] = s_gu.n
                if routed:
                    p.emit("gpsimd", lambda eh, g=gub, cc=st.c_sb[par]: eh.tensor_mul(
                        g[:, :], g[:, :], cc[:, :]),
                        waits=[(s_gu, s_gu.n), (s_sig, st.tr2[f"csb{par}"])], inc=s_cm)
                st.tr2[f"cm{par}{i}"] = s_cm.n
            st.tr2[f"cmu{par}"] = s_cm.n
            gu_ready = (s_cm, s_cm.n) if routed else (s_gu, s_gu.n)

            for i in range(4):
                gr_i = (s_cm, st.tr2[f"cm{par}{i}"]) if routed else (s_gu, st.tr2[f"gu{par}{i}"])
                for hc in range(4):
                    wmm = []
                    if hc == 0:
                        wmm.append(gr_i)
                        if i == 0:
                            wmm.append((s_wst, wld_d))
                        if e == 0 and i == 0:
                            wmm.append((s_ev, lo_evac_prev))
                    p.mm(P[hc][:, :], st.wd16[par][:, i * 1024 + hc * 128:i * 1024 + (hc + 1) * 128],
                         st.gu[par][i][:, :], start=(e == 0 and i == 0),
                         stop=(e == NEX - 1 and i == 3), waits=wmm,
                         inc=s_dn if (e == NEX - 1 and hc == 3 and i == 3) else None)
            if e % 2 == 1:
                for hc in range(4, 8):
                    bank = P[6 + (hc % 2)]
                    wmm = []
                    if hc in (4, 5):
                        wmm.append((s_sig, s_sig.n))
                    else:
                        wmm.append((s_ev, s_ev.n))
                    for ee in (e - 1, e):
                        pp = ee % 2
                        gur = (s_cm, st.tr2[f"cmu{pp}"]) if ee < NE else (s_gu, s_gu.n)
                        for i in range(4):
                            w2 = list(wmm) if (ee == e - 1 and i == 0) else []
                            if ee == e - 1 and i == 0:
                                w2.append(gur)
                            p.mm(bank[:, :], st.wd16[pp][:, i * 1024 + hc * 128:i * 1024 + (hc + 1) * 128],
                                 st.gu[pp][i][:, :], start=(ee == e - 1 and i == 0),
                                 stop=(ee == e and i == 3), waits=w2,
                                 inc=s_dn if (ee == e and i == 3) else None)
                    dn_now = s_dn.n
                    acix = hc - 4
                    if e == 1:
                        p.emit("vector", lambda eh, a=st.accum[acix], xc=x[4 + acix], t=t0,
                               b2=bank: eh.tensor_add(a[:, :], xc[:, t:t + G], b2[:, :]),
                               waits=[(s_dn, dn_now)], inc=s_ev)
                    else:
                        p.emit("vector", lambda eh, a=st.accum[acix], b2=bank: eh.tensor_add(
                            a[:, :], a[:, :], b2[:, :]),
                            waits=[(s_dn, dn_now)], inc=s_ev)
                    st.tr2[f"hiev{hc % 2}"] = s_ev.n
                st.tr2["dn_prev2"] = s_dn.n
            en = e + 2
            if en < NEX:
                ecast = 48 * (en + 1)
                p.dma("sync", st.wg16[par][:, :], scr["wg3s"][en],
                      waits=[(s_cast, ecast), (s_pu, s_pu.n)], inc=s_wst)
                p.dma("sync", st.wu16[par][:, :], scr["wu3s"][en], inc=s_wst)
                p.emit("sync", lambda eh: eh.nop(), waits=[(s_wst, s_wst.n)])
                st.tr2[f"wldgu{en}"] = s_wst.n
            if e % 2 == 1:
                for en2 in (e + 1, e + 2):
                    if en2 >= NEX:
                        continue
                    p.dma("sync", st.wd16[en2 % 2][:, :], scr["wd3s"][en2],
                          waits=[(s_cast, 48 * (en2 + 1)), (s_dn, s_dn.n)], inc=s_wst)
                p.emit("sync", lambda eh: eh.nop(), waits=[(s_wst, s_wst.n)])
                if e + 1 < NEX:
                    st.tr2[f"wldd{e + 1}"] = s_wst.n
                if e + 2 < NEX:
                    st.tr2[f"wldd{e + 2}"] = s_wst.n
        st.tr2["dn_prev"] = s_dn.n

        lo_done = s_dn.n
        for c in range(4):
            xc = x[c]
            p.emit("vector", lambda eh, xc=xc, t=t0, b2=P[c]: eh.tensor_add(
                xc[:, t:t + G], xc[:, t:t + G], b2[:, :]),
                waits=[(s_dn, lo_done)] if c == 0 else (), inc=s_ev)
        for c in range(4):
            xc = x[4 + c]
            p.emit("vector", lambda eh, xc=xc, t=t0, a=st.accum[c]: eh.tensor_copy(
                xc[:, t:t + G], a[:, :]),
                waits=[(s_ev, s_ev.n)] if c == 0 else (), inc=s_ev)
        st.tr2["si0"] = s_si.n
    return [(s_ev, s_ev.n), (s_wst, s_wst.n), (s_gwd, s_gwd.n), (s_gbd, s_gbd.n)]



class P3State:
    def __init__(self, nc, ctx, tag=""):
        sb = lambda n, s, dt: ctx.enter_context(nc.sbuf_tensor(n + tag, s, dt))
        self.h16 = [sb(f"p3_h{c}", [128, T], BF16) for c in range(NCH)]
        self.logits = [sb(f"p3_lg{c}", [128, T], BF16) for c in range(NCH)]
        self.vaug = [sb(f"p3_va{k}", [128, 2 * (HD + 1)], BF16) for k in range(16)]
        self.wqkv16 = sb("p3_wqkv", [128, 3072], BF16)
        self.wp16 = sb("p3_wp", [128, 1024], BF16)
        self.q = sb("p3_q", [128, T], BF16)
        self.k = sb("p3_k", [128, T], BF16)
        self.qlm = sb("p3_qlm", [128, HD], BF16)
        self.klm = sb("p3_klm", [128, HD], BF16)
        self.lmf = sb("p3_lmf", [128, HD], F32)
        self.mqa = sb("p3_mqa", [128, 1], F32)
        self.mka = sb("p3_mka", [128, 1], F32)
        self.mprod = sb("p3_mprod", [128, 1], F32)
        self.cb1 = sb("p3_cb1", [128, 2], F32)
        self.cb3 = sb("p3_cb3", [128, 2], F32)
        self.nt3 = [sb(f"p3_nt3{h}", [128, 1024], BF16) for h in range(2)]
        self.nt1 = [sb(f"p3_nt1{h}", [64, G], BF16) for h in range(2)]
        self.k3vn = [sb(f"p3_k3vn{h}", [64, HD], BF16) for h in range(2)]
        self.z3i = [sb(f"p3_z3i{h}", [64, 1], F32) for h in range(2)]
        self.K2 = [sb(f"p3_K2{h}", [64, 64], BF16) for h in range(2)]
        self.K2T = [sb(f"p3_K2T{h}", [64, 64], BF16) for h in range(2)]
        self.dcol = sb("p3_dcol", [64, 2], F32)
        self.V = [sb(f"p3_V{h}", [64, 64], BF16) for h in range(2)]
        self.VT = [sb(f"p3_VT{h}", [64, 64], BF16) for h in range(2)]
        self.Aa = sb("p3_Aa", [64, 64], BF16)
        self.Cc = sb("p3_Cc", [64, 64], BF16)
        self.Ee = sb("p3_Ee", [64, 64], BF16)
        self.KVT = sb("p3_KVT", [64, 64], BF16)
        self.i325 = sb("p3_i325", [64, 64], F32)
        self.id32 = sb("p3_id32", [128, 128], F32)
        self.w2 = sb("p3_w2", [64, 128], BF16)
        self.zrow = sb("p3_zrow", [1, 2 * G], F32)
        self.zb16 = sb("p3_zb16", [128, G], BF16)


def emit_l1_attn_batch(p, bufs, st, ln, hb, prev_done):
    x = bufs.x
    P = bufs.P
    s_wdma = p.sem("p3_wdma")
    s_pe = p.sem("p3_pe")
    s_qev = p.sem("p3_qev")
    s_exp = p.sem("p3_exp")
    s_dve = p.sem("p3_dve")
    s_lgt = p.sem("p3_lgt")
    s_res = p.sem("p3_res")
    s_ap = p.sem("ln_ap")

    emit_fences(p, ("tensor", "scalar", "vector", "gpsimd", "sync"), prev_done)
    p.dma("sync", ap_full(st.id32), ap_full(hb["c_id32"]), inc=p.sem("p3_idd"))
    s_idd = p.sem("p3_idd")
    p.emit("vector", lambda eh: eh.tensor_scalar_mul(
        st.i325[:, :], st.id32[0:64, 0:64], 3.25),
        waits=[(s_idd, s_idd.n)], inc=s_dve)
    for kt in range(16):
        od = st.vaug[kt].rearrange("p (h u) -> p h u", u=HD + 1)[:, :, HD:HD + 1]
        p.emit("gpsimd", lambda eh, d2=od: eh.memset(d2, 1.0), inc=s_dve)

    ln_marks = []
    for g in range(NGB):
        ln_marks.append(emit_ln(p, bufs, ln, x, g * G, st.h16, g * G, P[0], P[1], P[2]))
    ln_done = s_ap.n

    s_qkd = p.sem("p3_qkd")
    tr = {"qk_pe": 0, "exp_end": 0, "k3_pe": 0, "lgt": 0, "k3vn_dve": 0}
    for pr in range(NPAIR):
        p.dma("gpsimd", st.wqkv16[:, :], hb["w1qkv"][pr],
              waits=[(s_pe, tr["qk_pe"])], inc=s_wdma)
        w_ld = s_wdma.n

        prev_exp_end = tr["exp_end"]
        for kt in range(16):
            pv = P[2 + kt % 2]
            for c in range(NCH):
                wmm = []
                if c == 0:
                    wmm.append((s_wdma, w_ld))
                    wmm.append((s_ap, ln_marks[kt // 4]))
                    if kt < 2:
                        wmm.append((s_exp, prev_exp_end))
                        wmm.append((s_dve, tr["k3vn_dve"]))
                    else:
                        wmm.append((s_dve, s_dve.n - 1))
                p.mm(pv[:, 0:128], st.h16[c][:, kt * 128:(kt + 1) * 128],
                     st.wqkv16[:, 2048 + c * 128:2048 + (c + 1) * 128],
                     start=(c == 0), stop=(c == NCH - 1), waits=wmm,
                     inc=s_pe if c == NCH - 1 else None)
            vdst = st.vaug[kt].rearrange("p (h u) -> p h u", u=HD + 1)[:, :, 0:HD]
            vsrc = pv[:, 0:128].rearrange("p (h u) -> p h u", u=HD)
            p.emit("vector", lambda eh, d2=vdst, s2=vsrc: eh.tensor_copy(d2, s2),
                   waits=[(s_pe, s_pe.n), (s_pe, tr["k3_pe"])], inc=s_dve)
        v_done = s_dve.n

        for which, wt, dst, scale in (("k", st.wqkv16[:, 1024:2048], st.k, 1.0),
                                      ("q", st.wqkv16[:, 0:1024], st.q, 0.125)):
            for g in range(NGB):
                pq = P[g % 2]
                for c in range(NCH):
                    wmm = []
                    if c == 0:
                        wmm.append((s_wdma, w_ld))
                        wmm.append((s_qkd, max(s_qkd.n - 1, 0)))
                        wmm.append((s_ap, ln_marks[g]))
                        wmm.append((s_exp, prev_exp_end))
                    p.mm(pq[:, :], wt[:, c * 128:(c + 1) * 128],
                         st.h16[c][:, g * G:(g + 1) * G],
                         start=(c == 0), stop=(c == NCH - 1), waits=wmm,
                         inc=s_pe if c == NCH - 1 else None)
                if scale == 1.0:
                    p.emit("vector", lambda eh, d=dst[:, g * G:(g + 1) * G], pq=pq:
                           eh.tensor_copy(d, pq[:, :]),
                           waits=[(s_pe, s_pe.n)], inc=s_qkd)
                else:
                    p.emit("vector", lambda eh, d=dst[:, g * G:(g + 1) * G], pq=pq, sc=scale:
                           eh.tensor_scalar_mul(d, pq[:, :], sc),
                           waits=[(s_pe, s_pe.n)], inc=s_qkd)
        tr["qk_pe"] = s_pe.n
        qk_done = s_qkd.n

        p.emit("vector", lambda eh: eh.reduce_sum(
            st.lmf[:, :], st.q[:, :].rearrange("p (l w) -> p l w", w=32), axis=AX.X),
            waits=[(s_qkd, qk_done), (s_dve, s_dve.n)], inc=s_dve)
        p.act(st.qlm[:, :], st.lmf[:, :], AF.Copy, scale=1.0 / 32,
              waits=[(s_dve, s_dve.n)], inc=s_qev)
        p.emit("vector", lambda eh: eh.reduce_sum(
            st.lmf[:, :], st.k[:, :].rearrange("p (l w) -> p l w", w=32), axis=AX.X),
            waits=[(s_qev, s_qev.n), (s_dve, s_dve.n)], inc=s_dve)
        p.act(st.klm[:, :], st.lmf[:, :], AF.Copy, scale=1.0 / 32,
              waits=[(s_dve, s_dve.n)], inc=s_qev)
        lm_done = s_qev.n

        def bound(src_a, src_b, dstcb):
            p.emit("vector", lambda eh: eh.reduce_max(
                st.mqa[:, :], src_a, axis=AX.X, apply_absolute_value=True),
                waits=[(s_qev, s_qev.n), (s_dve, s_dve.n)], inc=s_dve)
            p.emit("vector", lambda eh: eh.reduce_max(
                st.mka[:, :], src_b, axis=AX.X, apply_absolute_value=True),
                waits=[(s_dve, s_dve.n)], inc=s_dve)
            p.emit("vector", lambda eh: eh.tensor_mul(
                st.mprod[:, :], st.mqa[:, :], st.mka[:, :]),
                waits=[(s_dve, s_dve.n)], inc=s_dve)
            p.mm(P[4][0:1, 0:2], st.mprod[:, :], bufs.hm[:, :], start=True, stop=True,
                 waits=[(s_dve, s_dve.n), (s_qev, s_qev.n), (s_exp, s_exp.n)], inc=s_pe)
            p.act(dstcb[0:1, 0:2], P[4][0:1, 0:2], AF.Copy, scale=-1.0,
                  waits=[(s_pe, s_pe.n)], inc=s_qev)
            for h in range(2):
                p.mm(P[4][:, 4 + h:5 + h], bufs.or32[:, :], dstcb[0:1, h:h + 1],
                     start=True, stop=True, waits=[(s_qev, s_qev.n)],
                     inc=s_pe if h == 1 else None)
            p.emit("vector", lambda eh, d=dstcb: eh.tensor_copy(d[:, 0:2], P[4][:, 4:6]),
                   waits=[(s_pe, s_pe.n)], inc=s_dve)
        bound(st.q[:, :], st.klm[:, :], st.cb1)
        bound(st.qlm[:, :], st.k[:, :], st.cb3)
        cb_done = s_dve.n

        if P3S < 1:
            tr["exp_end"] = s_exp.n
            tr["k3vn_dve"] = s_dve.n
            continue
        for h in range(2):
            hr = slice(64 * h, 64 * h + 64)
            tp = (64 * h, 0)
            for half in range(2):
                pb = P[half]
                for j in range(8):
                    kc = half * 8 + j
                    wmm = []
                    if h == 0 and half == 0 and j == 0:
                        wmm.append((s_qev, lm_done))
                        wmm.append((s_dve, cb_done))
                    if h == 1 and half == 0 and j == 0:
                        wmm.append((s_exp, s_exp.n))
                    p.mm(pb[:, j * 64:(j + 1) * 64], st.k[hr, kc * 128:(kc + 1) * 128],
                         st.qlm[hr, :], start=True, stop=True, tile=tp, waits=wmm,
                         inc=s_pe if j == 7 else None)
                p.act(st.nt3[h][:, half * 512:(half + 1) * 512], pb[:, :], AF.Exp,
                      bias=st.cb3[:, h:h + 1], waits=[(s_pe, s_pe.n), (s_dve, cb_done)],
                      inc=s_exp)
        tr["exp_end"] = s_exp.n
        nt3_done = s_exp.n

        for h in range(2):
            pk = P[2 + h]
            for kc in range(16):
                wmm = []
                if kc == 0:
                    wmm.append((s_exp, nt3_done))
                    wmm.append((s_dve, v_done))
                p.mm(pk[0:64, 0:65],
                     st.nt3[h][:, kc * 64:(kc + 1) * 64],
                     st.vaug[kc][:, h * 65:(h + 1) * 65],
                     start=(kc == 0), stop=(kc == 15), waits=wmm,
                     inc=s_pe if kc == 15 else None)
        k3_done = s_pe.n
        p.emit("vector", lambda eh: eh.reciprocal(st.z3i[0][:, :], P[2][0:64, 64:65]),
               waits=[(s_pe, k3_done)], inc=s_dve)
        p.emit("vector", lambda eh: eh.reciprocal(st.z3i[1][:, :], P[3][0:64, 64:65]),
               inc=s_dve)
        p.emit("vector", lambda eh: eh.tensor_scalar_mul(
            st.k3vn[0][:, :], P[2][0:64, 0:64], st.z3i[0][:, 0:1]),
            waits=[(s_dve, s_dve.n)], inc=s_dve)
        p.emit("vector", lambda eh: eh.tensor_scalar_mul(
            st.k3vn[1][:, :], P[3][0:64, 0:64], st.z3i[1][:, 0:1]),
            inc=s_dve)
        k3vn_done = s_dve.n
        tr["k3_pe"] = k3_done
        tr["k3vn_dve"] = k3vn_done

        if P3S < 2:
            continue
        for h in range(2):
            hr = slice(64 * h, 64 * h + 64)
            tp = (64 * h, 0)
            p.mm(P[4][0:64, 0:64], st.qlm[hr, :], st.klm[hr, :], start=True, stop=True,
                 tile=tp, waits=[(s_dve, s_dve.n), (s_qev, s_qev.n)], inc=s_pe)
            sc2 = s_pe.n
            p.emit("vector", lambda eh: eh.reduce_max(
                st.mqa[0:64, :], P[4][0:64, 0:64], axis=AX.X, negate=True),
                waits=[(s_pe, sc2)], inc=s_dve)
            p.act(st.K2[h][:, :], P[4][0:64, 0:64], AF.Exp, bias=st.mqa[0:64, 0:1],
                  accum=st.mka[0:64, 0:1], waits=[(s_dve, s_dve.n)], inc=s_qev)
            p.emit("vector", lambda eh, h=h: eh.reciprocal(st.mprod[0:64, :], st.mka[0:64, :]),
                   waits=[(s_qev, s_qev.n)], inc=s_dve)
            p.emit("vector", lambda eh: eh.tensor_copy(st.lmf[0:64, 0:1], st.mka[0:64, 0:1]),
                   inc=s_dve)
            p.emit("vector", lambda eh, h=h: eh.tensor_scalar_mul(
                st.K2[h][:, :], st.K2[h][:, :], st.mprod[0:64, 0:1]),
                waits=[(s_dve, s_dve.n)], inc=s_dve)
            k2_done = s_dve.n
            if P3S2 < 1:
                continue
            p.mm(P[4][0:64, 64:128], st.K2[h][:, :], bufs.id16[0:64, 0:64],
                 start=True, stop=True, waits=[(s_dve, k2_done)], inc=s_pe)
            p.emit("vector", lambda eh, h=h: eh.tensor_copy(st.K2T[h][:, :], P[4][0:64, 64:128]),
                   waits=[(s_pe, s_pe.n)], inc=s_dve)
            if P3S2 < 2:
                continue
            p.mm(P[4][0:1, 128:192], bufs.oc16[0:64, :], st.K2[h][:, :], start=True, stop=True,
                 waits=[(s_dve, s_dve.n)], inc=s_pe)
            p.emit("vector", lambda eh: eh.reduce_max(
                st.mqa[0:1, :], P[4][0:1, 128:192], axis=AX.X),
                waits=[(s_pe, s_pe.n)], inc=s_dve)
            p.emit("vector", lambda eh: eh.reciprocal(st.mka[0:1, :], st.mqa[0:1, :]),
                   waits=[(s_dve, s_dve.n)], inc=s_dve)
            p.mm(P[4][0:64, 192:193], bufs.or32[:, 0:64], st.mka[0:1, 0:1],
                 start=True, stop=True, waits=[(s_dve, s_dve.n)], inc=s_pe)
            p.emit("vector", lambda eh, h=h: eh.tensor_copy(st.dcol[:, h:h + 1], P[4][0:64, 192:193]),
                   waits=[(s_pe, s_pe.n)], inc=s_dve)
            p.emit("vector", lambda eh: eh.tensor_copy(st.lmf[0:64, 1:2], st.mka[0:64, 0:1]),
                   inc=s_dve)
            p.emit("vector", lambda eh, h=h: eh.tensor_scalar_mul(
                st.V[h][:, :], st.K2T[h][:, :], st.dcol[:, h:h + 1]),
                waits=[(s_dve, s_dve.n)], inc=s_dve)
            p.emit("vector", lambda eh, h=h: eh.tensor_scalar_mul(
                st.VT[h][:, :], st.K2[h][:, :], st.dcol[:, h:h + 1]),
                inc=s_dve)
            v0_done = s_dve.n
            if P3S2 < 3:
                continue
            KV = P[4][0:64, 256:320]
            KVT = P[4][0:64, 320:384]
            BB = P[4][0:64, 384:448]
            DD = P[4][0:64, 448:512]
            for it in range(P3IT):
                p.mm(KV, st.K2T[h][:, :], st.V[h][:, :], start=True, stop=True,
                     waits=[(s_dve, s_dve.n)], inc=s_pe)
                p.mm(KVT, st.V[h][:, :], st.K2T[h][:, :], start=True, stop=True, inc=s_pe)
                pe_kv = s_pe.n
                p.emit("vector", lambda eh: eh.tensor_copy(st.KVT[:, :], KVT),
                       waits=[(s_pe, pe_kv)], inc=s_dve)
                p.emit("vector", lambda eh: eh.scalar_tensor_tensor(
                    st.Aa[:, :], st.id32[0:64, 0:64], 7.0, KV, op0=ALU.mult, op1=ALU.subtract),
                    waits=[(s_dve, s_dve.n)], inc=s_dve)
                a_done = s_dve.n
                if P3IP < 1:
                    continue
                p.mm(BB, st.KVT[:, :], st.Aa[:, :], start=True, stop=True,
                     waits=[(s_dve, a_done)], inc=s_pe)
                p.emit("vector", lambda eh: eh.scalar_tensor_tensor(
                    st.Cc[:, :], st.id32[0:64, 0:64], 15.0, BB, op0=ALU.mult, op1=ALU.subtract),
                    waits=[(s_pe, s_pe.n)], inc=s_dve)
                c_done = s_dve.n
                if P3IP < 2:
                    continue
                p.mm(DD, st.KVT[:, :], st.Cc[:, :], start=True, stop=True,
                     waits=[(s_dve, c_done)], inc=s_pe)
                p.emit("vector", lambda eh: eh.scalar_tensor_tensor(
                    st.Ee[:, :], DD, -0.25, st.i325[:, :], op0=ALU.mult, op1=ALU.add),
                    waits=[(s_pe, s_pe.n)], inc=s_dve)
                e_done = s_dve.n
                if P3IP < 3:
                    continue
                p.mm(KV, st.VT[h][:, :], st.Ee[:, :], start=True, stop=True,
                     waits=[(s_dve, e_done)], inc=s_pe)
                p.mm(KVT, st.Ee[:, :], st.VT[h][:, :], start=True, stop=True, inc=s_pe)
                pe_v = s_pe.n
                p.emit("vector", lambda eh, h=h: eh.tensor_copy(st.V[h][:, :], KV),
                       waits=[(s_pe, pe_v)], inc=s_dve)
                p.emit("vector", lambda eh, h=h: eh.tensor_copy(st.VT[h][:, :], KVT),
                       inc=s_dve)
            w2rhs = st.k3vn[h][:, :] if P3W2 != 1 else bufs.id16[0:64, 0:64]
            w2lhs = st.VT[h][:, :] if P3W2 != 2 else st.K2[h][:, :]
            p.mm(P[4][0:64, 256:320], w2lhs, w2rhs,
                 start=True, stop=True, waits=[(s_dve, s_dve.n)], inc=s_pe)
            if P3W2 != 3:
                p.emit("vector", lambda eh, h=h: eh.tensor_copy(
                    st.w2[:, 64 * h:64 * h + 64], P[4][0:64, 256:320]),
                    waits=[(s_pe, s_pe.n)], inc=s_dve)
        w2_done = s_dve.n

        if P3S < 3:
            continue
        for g in range(NGB):
            gcols = slice(g * G, (g + 1) * G)
            for h in range(2):
                hr = slice(64 * h, 64 * h + 64)
                tp = (64 * h, 0)
                wmm = [(s_dve, w2_done)]
                if g > 0 or h > 0:
                    wmm.append((s_exp, s_exp.n))
                if g > 0 and h == 0:
                    wmm.append((s_lgt, s_lgt.n))
                    wmm.append((s_dve, s_dve.n))
                p.mm(P[5][0:64, :], st.klm[hr, :], st.q[hr, gcols], start=True, stop=True,
                     tile=tp, waits=wmm, inc=s_pe)
                p.act(st.nt1[h][:, :], P[5][0:64, :], AF.Exp, bias=st.cb1[0:64, h:h + 1],
                      waits=[(s_pe, s_pe.n)], inc=s_exp)
                zdst = P[6][0:1, :] if h == 0 else P[5][0:1, :]
                p.mm(zdst, bufs.oc16[0:64, :], st.nt1[h][:, :],
                     start=True, stop=True, tile=(0, 0),
                     waits=[(s_exp, s_exp.n), (s_qev, s_qev.n)], inc=None)
                p.mm(P[7][64 * h:64 * h + 64, :], st.w2[:, 64 * h:64 * h + 64],
                     st.nt1[h][:, :], start=True, stop=True, tile=(0, 64 * h),
                     inc=s_pe)
            sv_done = s_pe.n
            p.emit("vector", lambda eh: eh.reciprocal(st.zrow[0:1, 0:G], P[6][0:1, 0:G]),
                   waits=[(s_pe, sv_done)], inc=s_dve)
            p.emit("vector", lambda eh: eh.reciprocal(st.zrow[0:1, G:2 * G], P[5][0:1, 0:G]),
                   inc=s_dve)
            zi_done = s_dve.n
            p.mm(P[6][0:64, :], bufs.or32[:, 0:64], st.zrow[0:1, 0:G],
                 start=True, stop=True, tile=(0, 0), waits=[(s_dve, zi_done)])
            p.mm(P[6][64:128, :], bufs.or32[:, 0:64], st.zrow[0:1, G:2 * G],
                 start=True, stop=True, tile=(0, 64), inc=s_pe)
            p.act(st.zb16[:, :], P[6][:, :], AF.Copy, waits=[(s_pe, s_pe.n)], inc=s_qev)
            lt = st.logits[pr]
            p.emit("vector", lambda eh, lt=lt, lc=gcols: eh.tensor_mul(
                lt[0:64, lc], P[7][0:64, :], st.zb16[0:64, :]),
                waits=[(s_qev, s_qev.n)], inc=s_lgt)
            p.emit("vector", lambda eh, lt=lt, lc=gcols: eh.tensor_mul(
                lt[64:128, lc], P[7][64:128, :], st.zb16[64:128, :]),
                inc=s_lgt)
    lg_done = s_lgt.n

    s_wp = p.sem("p3_wp")
    if P3S < 4:
        return [(s_pe, s_pe.n), (s_dve, s_dve.n), (s_wdma, s_wdma.n), (s_idd, s_idd.n)]
    wp_pe = 0
    for oc in range(NCH):
        p.dma("gpsimd", st.wp16[:, :], hb["w1p"][oc],
              waits=[(s_pe, wp_pe)], inc=s_wp)
        wp_ld = s_wp.n
        for g in range(NGB):
            po = P[0 + g % 2]
            for pc in range(NCH):
                wmm = []
                if pc == 0:
                    wmm.append((s_lgt, lg_done))
                    wmm.append((s_wp, wp_ld))
                    wmm.append((s_res, max(s_res.n - 1, 0)))
                    if oc == 0 and g == 0:
                        wmm.append((s_exp, s_exp.n))
                p.mm(po[:, :], st.wp16[:, pc * 128:(pc + 1) * 128],
                     st.logits[pc][:, g * G:(g + 1) * G],
                     start=(pc == 0), stop=(pc == NCH - 1), waits=wmm,
                     inc=s_pe if pc == NCH - 1 else None)
            wp_pe = s_pe.n
            xa = x[oc][:, g * G:(g + 1) * G]
            p.emit("vector", lambda eh, po=po, xa=xa: eh.tensor_add(xa, xa, po[:, :]),
                   waits=[(s_pe, s_pe.n)], inc=s_res)
    return [(s_res, s_res.n), (s_wdma, s_wdma.n), (s_wp, s_wp.n), (s_idd, s_idd.n)]



class P4State:
    def __init__(self, nc, ctx, tag=""):
        sb = lambda n, s, dt: ctx.enter_context(nc.sbuf_tensor(n + tag, s, dt))
        self.h16 = [sb(f"p4_h{c}", [128, G], BF16) for c in range(NCH)]
        self.w1in16 = [sb(f"p4_wi{c}", [128, 2048], BF16) for c in range(NCH)]
        self.b1in16 = sb("p4_b1", [1, 2048], BF16)
        self.w1o16 = sb("p4_wo", [128, 2048], BF16)
        self.dec = sb("p4_dec", [128, 2048], BF16)
        self.z16 = sb("p4_z16", [128, 2048], BF16)
        self.vmask = sb("p4_vm", [128, 2048], BF16)
        self.mk16 = sb("p4_mk", [128, 2048], BF16)
        self.tscr = sb("p4_ts", [128, 512], BF16)
        self.mkT = sb("p4_mkT", [128, 16 * G], BF16)
        self.outst = [sb(f"p4_os{i}", [128, G], F32) for i in range(2)]


def emit_l1_ffn_batch(p, bufs, st, ln, hb, ydram, b, prev_done):
    x = bufs.x
    P = bufs.P
    PB0 = bufs.PB[0]
    PB1b = bufs.PB[1][:, :].bitcast(BF16)
    s_wdma = p.sem("p4_wdma")
    s_pe = p.sem("p4_pe")
    s_po = p.sem("p4_po")
    s_dve = p.sem("p4_dve")
    s_pool = p.sem("p4_pool")
    s_act = p.sem("p4_act")
    s_ev = p.sem("p4_ev")
    s_yst = p.sem("p4_yst")
    s_ap = p.sem("ln_ap")

    emit_fences(p, ("tensor", "scalar", "vector", "gpsimd", "sync"), prev_done)

    for c in range(NCH):
        p.dma("gpsimd", st.w1in16[c][:, :], hb["w1in"][c * 128:(c + 1) * 128, :], inc=s_wdma)
    p.dma("gpsimd", st.b1in16[:, :], ap_full(hb["b1in"]), inc=s_wdma)
    wi_ld = s_wdma.n

    vm3 = st.vmask[:, 0:2040].rearrange("p (a n) -> p a n", n=255)
    dc3 = st.dec[:, 0:2040].rearrange("p (a n) -> p a n", n=255)

    for tg in range(NGB):
        t0 = tg * G
        emit_ln(p, bufs, ln, x, t0, st.h16, 0, P[0], P[1], P[2],
                xw=[(s_dve, s_dve.n), (s_ev, s_ev.n)])
        ln_done = s_ap.n
        for tt in range(4):
            for sl in range(4):
                for c in range(NCH):
                    wmm = []
                    if sl == 0 and c == 0:
                        wmm.append((s_ap, ln_done))
                        wmm.append((s_wdma, wi_ld))
                        wmm.append((s_dve, s_dve.n))
                    p.mm(P[sl][:, :], st.h16[c][:, tt * 128:(tt + 1) * 128],
                         st.w1in16[c][:, sl * G:(sl + 1) * G],
                         start=(c == 0), stop=False, waits=wmm)
                p.mm(P[sl][:, :], bufs.or16[:, :], st.b1in16[:, sl * G:(sl + 1) * G],
                     start=False, stop=True, inc=s_pe if sl == 3 else None)
            mm_done = s_pe.n
            p.emit("vector", lambda eh: eh.tensor_scalar(
                st.dec[:, 0:2040], PB0[:, 0:2040], 0.0, None, op0=ALU.is_gt),
                waits=[(s_pe, mm_done), (s_pool, s_pool.n)], inc=s_dve)
            p.emit("vector", lambda eh: eh.tensor_copy(st.z16[:, :], PB0[:, :]),
                   waits=[(s_dve, s_dve.n), (s_pool, s_pool.n)], inc=s_dve)
            z_done = s_dve.n
            p.emit("gpsimd", lambda eh: eh.memset(st.vmask[:, :], 0.0),
                   waits=[(s_act, s_act.n), (s_pool, s_pool.n)], inc=s_pool)
            p.emit("gpsimd", lambda eh: eh.memset(vm3[:, :, 0:1], 1.0),
                   waits=[(s_pool, s_pool.n)], inc=s_pool)
            for d in range(7):
                a, bnd = 2 ** d - 1, 2 ** (d + 1) - 1
                cnd = 2 ** (d + 2) - 1
                lvl = vm3[:, :, a:bnd]
                dec_d = dc3[:, :, a:bnd]
                nxt = vm3[:, :, bnd:cnd].rearrange("p a (i two) -> p a i two", two=2)
                t0s = st.tscr[:, :].rearrange("p (a n) -> p a n", n=64)[:, :, 0:2 ** d]
                p.emit("gpsimd", lambda eh, t=t0s, l=lvl, dd=dec_d: eh.tensor_mul(t, l, dd),
                       waits=[(s_pool, s_pool.n), (s_dve, z_done)], inc=s_pool)
                p.emit("gpsimd", lambda eh, n=nxt, l=lvl, t=t0s: eh.tensor_sub(
                    n[:, :, :, 0], l, t), waits=[(s_pool, s_pool.n)], inc=s_pool)
                p.emit("gpsimd", lambda eh, n=nxt, t=t0s: eh.tensor_copy(n[:, :, :, 1], t),
                       waits=[(s_pool, s_pool.n)], inc=s_pool)
            p.emit("gpsimd", lambda eh: eh.tensor_mul(st.mk16[:, :], st.z16[:, :], st.vmask[:, :]),
                   waits=[(s_pool, s_pool.n), (s_act, s_act.n), (s_pe, s_pe.n)], inc=s_pool)
            p.act(st.mk16[:, :], st.mk16[:, :], AF.Silu,
                  waits=[(s_pool, s_pool.n)], inc=s_act)
            silu_done = s_act.n
            for wc in range(16):
                wmm = [(s_act, silu_done)]
                if wc == 0:
                    wmm.append((s_dve, s_dve.n))
                    wmm.append((s_po, s_po.n))
                p.transp(PB1b[:, wc * 128:(wc + 1) * 128], st.mk16[:, wc * 128:(wc + 1) * 128],
                         bufs.id16[:, :], waits=wmm, inc=s_pe if wc == 15 else None)
            tr_done = s_pe.n
            dstT = st.mkT[:, :].rearrange("p (w q) -> p w q", q=G)[:, :, tt * 128:(tt + 1) * 128]
            srcT = PB1b[:, 0:2048].rearrange("p (w j) -> p w j", j=128)
            p.emit("vector", lambda eh, d2=dstT, s2=srcT: eh.tensor_copy(d2, s2),
                   waits=[(s_pe, tr_done)], inc=s_dve)
        mkT_done = s_dve.n

        s_wo2 = p.sem("p4_wo2")
        for og in range(NCH):
            p.dma("gpsimd", st.w1o16[:, :], hb["w1out"][og],
                  waits=[(s_po, s_po.n)], inc=s_wo2)
            wo_ld = s_wo2.n
            po = P[6 + og % 2]
            for wc in range(16):
                wmm = []
                if wc == 0:
                    wmm.append((s_dve, mkT_done))
                    wmm.append((s_wo2, wo_ld))
                    wmm.append((s_ev, max(s_ev.n - 1, 0)))
                p.mm(po[:, :], st.w1o16[:, wc * 128:(wc + 1) * 128],
                     st.mkT[:, wc * G:(wc + 1) * G],
                     start=(wc == 0), stop=(wc == 15), waits=wmm,
                     inc=s_po if wc == 15 else None)
            ost = st.outst[og % 2]
            p.emit("vector", lambda eh, po=po, o=ost, xc=x[og], t=t0: eh.tensor_add(
                o[:, :], xc[:, t:t + G], po[:, :]),
                waits=[(s_po, s_po.n), (s_yst, max(s_yst.n - 16, 0))], inc=s_ev)
            p.dma("sync", ydram[b, og * 128:(og + 1) * 128, t0:t0 + G], ost[:, :],
                  waits=[(s_ev, s_ev.n)], inc=s_yst)
            p.emit("sync", lambda eh: eh.nop(), waits=[(s_yst, s_yst.n)])
    return [(s_yst, s_yst.n), (s_ev, s_ev.n), (s_wdma, s_wdma.n),
            (p.sem("p4_wo2"), p.sem("p4_wo2").n)]



def build_program(phases=("p1", "p2", "p3", "p4")):
    nc = bass.Bass("TRN2", target_bir_lowering=False)
    hb = declare_inputs(nc)
    y = nc.dram_tensor("y", [NB, D, T], F32, kind="ExternalOutput")
    scr = {
        "wg3s": nc.dram_tensor("wg3s", [NEX, 128, 4096], BF16),
        "wu3s": nc.dram_tensor("wu3s", [NEX, 128, 4096], BF16),
        "wd3s": nc.dram_tensor("wd3s", [NEX, 128, 4096], BF16),
    }
    ctx = ExitStack()
    p = Prog(nc, ctx)
    bufs = Bufs(nc, ctx, p, hb)
    p.emit("tensor", lambda eh: eh.nop(), waits=((bufs.ld_c, bufs.ld_c.n),))
    if "p2" in phases:
        emit_moe_casts(p, hb, scr)

    with ExitStack() as pctx:
        ln = LNState(nc, pctx)
        batch_done = ()
        for b in range(NB):
            sx = emit_x_load(p, bufs, hb, b, prev=batch_done)
            last = [(sx, sx.n)]
            if "p1" in phases:
                with ExitStack() as c1:
                    st1 = P1State(nc, c1, tag=f"_b{b}")
                    last = emit_l0_attn_batch(p, bufs, st1, ln, hb, sx)
            if "p2" in phases:
                with ExitStack() as c2:
                    st2 = P2State(nc, c2, tag=f"_b{b}")
                    last = emit_moe_batch(p, bufs, st2, ln, hb, scr, last)
            if "p3" in phases:
                with ExitStack() as c3:
                    st3 = P3State(nc, c3, tag=f"_b{b}")
                    last = emit_l1_attn_batch(p, bufs, st3, ln, hb, last)
            if "p4" in phases:
                with ExitStack() as c4:
                    st4 = P4State(nc, c4, tag=f"_b{b}")
                    last = emit_l1_ffn_batch(p, bufs, st4, ln, hb, y, b, last)
            else:
                emit_x_store(p, bufs, y, b, last)
            batch_done = last
            p.flush()
    s = p.sem("st_x")
    p.emit("gpsimd", lambda eh: eh.nop(), waits=tuple(batch_done) + ((s, s.n),))
    p.flush()
    ctx.close()
    return nc


def kernel(**inputs):
    shared = prep_shared_inputs(inputs)
    per_core = [prep_core_inputs(inputs, c, shared) for c in range(NCORE)]
    nc = build_program(phases=("p1", "p2", "p3", "p4"))
    res = run_bass_kernel_spmd(nc, per_core, core_ids=list(range(NCORE)))
    out = np.empty((32, T, D), np.float32)
    for c in range(NCORE):
        yc = np.asarray(res.results[c]["y"], np.float32)
        out[c * NB:(c + 1) * NB] = yc.transpose(0, 2, 1)
    return out
